# version 06
# baseline (speedup 1.0000x reference)
import numpy as np
import ml_dtypes
import concourse.bass as bass
import concourse.mybir as mybir
from concourse.bass_utils import run_bass_kernel_spmd

F32 = mybir.dt.float32
BF16 = mybir.dt.bfloat16
I32 = mybir.dt.int32
F16 = mybir.dt.float16
AF = mybir.ActivationFunctionType
ALU = mybir.AluOpType
AXX = mybir.AxisListType.X

NCORE = 8
DM = 2048
HID = 4096
STATE = 128
HEADS = 64
PD = 64
CHUNK = 256
CONVD = HID + 2 * STATE
DIN = 2 * HID + 2 * STATE + HEADS
BLK = 512
TPC = 1024
EMB = 2048
NH = 16
DH = 128
QR = 64
KR = 128
RD = 64
FI = 5632
GS = 128
GU_SH = 2 * FI // NCORE
DN_SH = DM // NCORE

LN_EPS = 1e-5
MAMBA_EPS = 1e-5
FFN_EPS = 1e-6

ENGINES = ("sync", "gpsimd", "vector", "scalar", "tensor")


class Sem:
    __slots__ = ("name", "h", "n")

    def __init__(self, name, h):
        self.name, self.h, self.n = name, h, 0


class Buf:
    __slots__ = ("name", "ready", "freed")

    def __init__(self, name=""):
        self.name, self.ready, self.freed = name, [], []


class EngCtx:
    def __init__(self, eng):
        self.eng = eng
        self._c = {}

    @property
    def cid(self):
        if "cid" not in self._c:
            self._c["cid"] = self.eng.snap(self.eng.partition_id(),
                                           min_val=0, max_val=NCORE - 1)
        return self._c["cid"]


class B:
    def __init__(self, nc):
        self.nc = nc
        self.streams = {e: [] for e in ENGINES}
        self.waited = {e: {} for e in ENGINES}

    def sem(self, name):
        return Sem(name, self.nc.semaphore(name).__enter__())

    def raw(self, eng, fn, waits=(), inc=None, amt=1, dyn=None, relax=False):
        if relax is not False and inc is not None:
            waits = list(waits) + [(inc, inc.n if relax is True else relax)]
        wmax = {}
        for (s, c) in waits:
            if c > 0 and c > wmax.get(s, 0):
                wmax[s] = c
        wl = list(wmax.items())
        if inc is not None:
            inc.n += amt
        wd = self.waited[eng]
        if eng == "tensor":
            wl = [(s, c) for (s, c) in wl if s.name != "pe"]

        lagwait = None
        if inc is not None and amt == 16 and not relax:
            lagwait = inc.n

        def closure(e, ctx, wl=wl, fn=fn, inc=inc, amt=amt, dyn=dyn,
                    lagwait=lagwait):
            for s, c in wl:
                if wd.get(s.name, -1) < c:
                    e.wait_ge(s.h, c)
                    wd[s.name] = c
            if dyn is not None:
                dyn(e, ctx)
            try:
                instr = fn(e, ctx)
            except Exception as ex:
                import traceback
                cv = {k: v for k, v in
                      (fn.__defaults__ and
                       zip(fn.__code__.co_varnames[2:], fn.__defaults__)
                       or [])}
                raise RuntimeError(
                    f"op failed: vars={cv} err={ex}") from ex
            if inc is not None and instr is not None:
                instr.then_inc(inc.h, amt)
            if lagwait is not None and lagwait > 0 and \
                    wd.get(inc.name, -1) < lagwait:
                e.wait_ge(inc.h, lagwait)
                wd[inc.name] = lagwait

        self.streams[eng].append(closure)
        return inc.n if inc is not None else None

    def op(self, eng, fn, ins=(), outs=(), inc=None, amt=1, waits=(), dyn=None,
           relax=False):
        wl = list(waits)
        for bu in ins:
            wl += bu.ready
        for bu in outs:
            wl += bu.ready + bu.freed
        n = self.raw(eng, fn, waits=wl, inc=inc, amt=amt, dyn=dyn, relax=relax)
        dep = (inc, n)
        for bu in outs:
            bu.ready = [dep]
            bu.freed = []
        for bu in ins:
            bu.freed.append(dep)
        return n

    def finalize(self):
        streams = self.streams
        with self.nc.Block() as block:
            def run(e, name):
                ctx = EngCtx(e)
                for f in streams[name]:
                    f(e, ctx)

            @block.sync
            def _(e):
                run(e, "sync")

            @block.gpsimd
            def _(e):
                run(e, "gpsimd")

            @block.vector
            def _(e):
                run(e, "vector")

            @block.scalar
            def _(e):
                run(e, "scalar")

            @block.tensor
            def _(e):
                run(e, "tensor")


class Arena:
    def __init__(self, nc, name, cols, dtype):
        self.t = nc.alloc_sbuf_tensor(name, [128, cols], dtype)
        self.cols = cols
        self.cur = 0
        self.marks = []

    def alloc(self, n, name=""):
        assert self.cur + n <= self.cols, \
            f"arena overflow: {name} wants {n} at {self.cur}/{self.cols}"
        ap = self.t.ap()[:, self.cur:self.cur + n]
        self.cur += n
        return ap

    def mark(self):
        self.marks.append(self.cur)

    def release(self):
        self.cur = self.marks.pop()


class Prog:
    def __init__(self, dbg=""):
        self.nc = bass.Bass("TRN2", target_bir_lowering=False)
        self.b = B(self.nc)
        self.dbg = dbg
        self.dbg_out = {}

    def dout(self, name, rows, cols, dt=F32):
        ap = self.nc.dram_tensor("dbg_" + name, [rows, cols], dt,
                                 kind="ExternalOutput").ap()
        self.dbg_out[name] = ap
        return ap

    def declare(self):
        nc = self.nc

        def din(name, shape, dt=F32):
            return nc.dram_tensor(name, shape, dt, kind="ExternalInput").ap()

        self.i_x = din("x", [DM, TPC + 6])
        self.i_winT = din("w_inT", [DM, DIN])
        self.i_conv = din("conv_w", [CONVD, 4])
        self.i_dtb = din("dt_bias", [HEADS, 1])
        self.i_alog = din("A_log", [HEADS, 1])
        self.i_dpair = din("D_pairs", [128, 32])
        self.i_normw = din("norm_w", [128, HID // 128])
        self.i_woutT = din("w_outT", [HID, DM])
        self.i_gqr = din("g_q_row", [1, DM])
        self.i_gkvr = din("g_kv_row", [1, DM])
        self.i_wqa = din("w_qa", [DM, QR])
        self.i_wqb = din("w_qb", [QR, EMB])
        self.i_wkva = din("w_kva_used", [DM, KR + NH * DH])
        self.i_wkvb = din("w_kvb", [KR, EMB])
        self.i_wo = din("w_o", [EMB, EMB])
        self.i_gup = din("gup_packed", [GU_SH, DM // 8], I32)
        self.i_gus = din("gup_scalesT", [GU_SH, DM // GS], F16)
        self.i_dnp = din("down_packed", [DN_SH, FI // 8], I32)
        self.i_dns = din("down_scalesT", [DN_SH, FI // GS], F16)
        self.i_jcol = din("jcol", [128, 1])

        self.o_out = nc.dram_tensor("out", [DM, TPC], F32,
                                    kind="ExternalOutput").ap()

        def scr(name, rows, cols, dt=BF16):
            return nc.dram_tensor(name, [rows, cols], dt, kind="Internal").ap()

        self.d_z = scr("z_spill", 4 * HID, CHUNK)
        self.d_yp = scr("yp_spill", 4 * HID, CHUNK)
        self.d_stin = [scr(f"st_in{r}", 128, HEADS * PD, F32)
                       for r in range(2)]
        self.d_stout = [scr(f"st_out{r}", NCORE * 128, HEADS * PD, F32)
                        for r in range(2)]
        self.d_lrin = [scr(f"lr_in{r}", 1, HEADS, F32) for r in range(2)]
        self.d_lrout = [scr(f"lr_out{r}", NCORE, HEADS, F32)
                        for r in range(2)]
        self.Bd_lrin = [Buf("d_lrin0"), Buf("d_lrin1")]
        self.Bd_lrout = [Buf("d_lrout0"), Buf("d_lrout1")]
        self.d_guin = scr("gu_in", DM, GU_SH)
        self.d_guout = scr("gu_out", NCORE * DM, GU_SH)
        self.d_dnin = scr("dn_in", FI, DN_SH)
        self.d_dnout = scr("dn_out", NCORE * FI, DN_SH)
        self.d_kvin = [scr(f"kv_in{r}", 2 * EMB + 1, BLK) for r in range(2)]
        self.d_kvall = scr("kv_all", 2 * NCORE * (2 * EMB + 1), BLK)
        RUNROWS = NCORE * (2 * EMB + 1)
        self.d_kvout = [self.d_kvall[0:RUNROWS, :],
                        self.d_kvall[RUNROWS:2 * RUNROWS, :]]
        self.d_q = scr("q_spill", NH * DH, TPC)
        self.d_astg = scr("attn_stg", 7 * 4096, BLK)
        self.d_xbc1 = scr("xbc1_spill", 34 * 128, CHUNK)
        self.Bd_xbc1 = Buf("d_xbc1")
        self.Bd_z = Buf("d_z")
        self.Bd_yp = Buf("d_yp")
        self.Bd_stin = [Buf("d_stin0"), Buf("d_stin1")]
        self.Bd_stout = [Buf("d_stout0"), Buf("d_stout1")]
        self.Bd_q = Buf("d_q")

    def mem(self):
        nc = self.nc
        self.af = Arena(nc, "af", 23800, F32)
        self.ab = Arena(nc, "ab", 58752, BF16)

        self.xres = self.af.alloc(16 * TPC, "xres")
        self.onesf = self.af.alloc(128, "onesf")
        self.cc = self.af.alloc(8, "cc")
        self.p_dtb = self.af.alloc(1, "dtb")
        self.p_negA = self.af.alloc(1, "negA")
        self.p_dpair = self.af.alloc(32, "dpair")
        self.p_normw = self.af.alloc(32, "normw")
        self.p_convw = self.af.alloc(34 * 4, "convw")
        self.p_jcol = self.af.alloc(1, "jcol")
        self.identf = self.af.alloc(128, "identf")
        self.ln_rows = self.af.alloc(4 * 512, "ln_rows")
        self.B_lnrows = Buf("lnrows")

        self.grows = self.ab.alloc(2 * DM, "grows")
        self.onesb = self.ab.alloc(128, "onesb")
        self.lnr16 = self.ab.alloc(1024, "lnr16")
        self.B_lnr16 = Buf("lnr16")
        self.identb = self.ab.alloc(128, "identb")
        self.trilm = self.ab.alloc(2 * CHUNK, "trilm")
        self.fmask = self.ab.alloc(4 * BLK, "fmask")

        self.ppsb = [nc.alloc_psum_tensor(f"pp{i}", [128, 1024], F32)
                     for i in range(4)]
        self.pp = [t.ap() for t in self.ppsb]
        self.ps = [self.pp[i // 2][:, (i % 2) * 512:(i % 2) * 512 + 512]
                   for i in range(8)]
        self.psbf = [p.bitcast(BF16) for p in self.ps]
        self.psB = [Buf(f"ps{i}") for i in range(8)]

        self.B_x = Buf("xres")
        self.B_par = Buf("par")
        self.B_c = Buf("const")
        self.B_rows = Buf("rows2048")

    def sems(self):
        b = self.b
        self.s_ld = b.sem("ld")
        self.s_wg = b.sem("wg")
        self.s_pe = b.sem("pe")
        self.s_dv = b.sem("dv")
        self.s_ac = b.sem("ac")
        self.s_gp = b.sem("gp")
        self.s_agst = [b.sem("agst0"), b.sem("agst1")]
        self.s_aggu = b.sem("aggu")
        self.s_agdn = b.sem("agdn")
        self.s_agkv = [b.sem("agkv0"), b.sem("agkv1")]
        self.s_aglr = [b.sem("aglr0"), b.sem("aglr1")]
        self.s_od = b.sem("od")
        self.s_odg = b.sem("odg")
        self.s_aq = b.sem("aq")
        self.s_kq = [b.sem("kq0"), b.sem("kq1")]
        self.s_vq = [b.sem("vq0"), b.sem("vq1")]
        self.s_vt = [b.sem("vt0"), b.sem("vt1")]
        self.s_rs = [b.sem("rs0"), b.sem("rs1")]

    def mm(self, out_ap, lhsT_ap, rhs_ap, pbuf, ins, start, stop, first):
        b = self.b
        wl = []
        for bu in ins:
            wl += bu.ready
        wl += pbuf.ready
        if first:
            wl += pbuf.freed
        n = b.raw("tensor", lambda e, c: e.matmul(
            out_ap, lhsT_ap, rhs_ap, start=start, stop=stop),
            waits=wl, inc=self.s_pe)
        dep = (self.s_pe, n)
        pbuf.ready = [dep]
        if first:
            pbuf.freed = []
        for bu in ins:
            bu.freed.append(dep)
        return n

    def tposeb(self, dst_ap, dst_buf, src_ap, src_bufs, bank, r, c):
        pb = self.psB[bank]
        pv = self.psbf[bank][0:c, 0:r]
        b = self.b
        wl = pb.ready + pb.freed + self.B_c.ready
        for bu in src_bufs:
            wl = wl + bu.ready
        n = b.raw("tensor", lambda e, cx: e.transpose(
            pv, src_ap, self.identb[0:r, 0:r]), waits=wl, inc=self.s_pe)
        pb.ready = [(self.s_pe, n)]
        pb.freed = []
        for bu in src_bufs:
            bu.freed.append((self.s_pe, n))
        b.op("vector", lambda e, cx: e.tensor_copy(dst_ap, pv),
             ins=[pb], outs=[dst_buf], inc=self.s_dv)

    def tposef(self, dst_ap, dst_buf, src_ap, src_bufs, bank, r, c):
        pb = self.psB[bank]
        pv = self.ps[bank][0:c, 0:r]
        b = self.b
        wl = pb.ready + pb.freed + self.B_c.ready
        for bu in src_bufs:
            wl = wl + bu.ready
        n = b.raw("tensor", lambda e, cx: e.transpose(
            pv, src_ap, self.identf[0:r, 0:r]), waits=wl, inc=self.s_pe)
        pb.ready = [(self.s_pe, n)]
        pb.freed = []
        for bu in src_bufs:
            bu.freed.append((self.s_pe, n))
        b.op("vector", lambda e, cx: e.tensor_copy(dst_ap, pv),
             ins=[pb], outs=[dst_buf], inc=self.s_dv)

    def emit_ln(self, dst_tiles, src_tiles, n, gain_row, stg, B_stg,
                banks=(4, 5, 6, 7), n_feat=DM):
        b = self.b
        nt = len(src_tiles)
        s0, s1, s2, s3 = [self.psB[i] for i in banks]
        p0, p1, p2, p3 = [self.ps[i] for i in banks]
        xb = stg[:, 0:n]
        sq = stg[:, n:2 * n]
        for ft in range(nt):
            src, bsrc = src_tiles[ft]
            b.op("vector", lambda e, c, src=src, xb=xb: e.tensor_copy(xb, src),
                 ins=[bsrc], outs=[B_stg], inc=self.s_dv)
            b.op("scalar", lambda e, c, src=src, sq=sq: e.activation(
                sq, src, AF.Square), ins=[bsrc], outs=[B_stg], inc=self.s_ac)
            self.mm(p0[0:1, 0:n], self.onesb[:, 0:1], xb, s0,
                    [B_stg, self.B_c], start=(ft == 0), stop=(ft == nt - 1),
                    first=(ft == 0))
            self.mm(p1[0:1, 0:n], self.onesb[:, 0:1], sq, s1,
                    [B_stg, self.B_c], start=(ft == 0), stop=(ft == nt - 1),
                    first=(ft == 0))
        inv = 1.0 / n_feat
        rows = self.ln_rows
        Br = self.B_lnrows
        m_r = rows[0:1, 0:n]
        v_r = rows[0:1, 512:512 + n]
        a_r = rows[0:1, 1024:1024 + n]
        b_r = rows[0:1, 1536:1536 + n]
        b.op("vector", lambda e, c: e.tensor_scalar_mul(m_r, p0[0:1, 0:n], inv),
             ins=[s0], outs=[Br], inc=self.s_dv)
        b.op("vector", lambda e, c: e.tensor_scalar_mul(v_r, p1[0:1, 0:n], inv),
             ins=[s1], outs=[Br], inc=self.s_dv)
        b.op("vector", lambda e, c: e.scalar_tensor_tensor(
            a_r, m_r, -1.0, m_r, ALU.mult, ALU.mult),
            ins=[Br], outs=[Br], inc=self.s_dv)
        b.op("vector", lambda e, c: e.tensor_add(v_r, v_r, a_r),
             ins=[Br], outs=[Br], inc=self.s_dv)
        b.op("scalar", lambda e, c: e.activation(
            v_r, v_r, AF.Sqrt, bias=self.cc[0:1, 0:1]),
            ins=[Br, self.B_c], outs=[Br], inc=self.s_ac)
        b.op("vector", lambda e, c: e.reciprocal(a_r, v_r),
             ins=[Br], outs=[Br], inc=self.s_dv)
        b.op("vector", lambda e, c: e.scalar_tensor_tensor(
            b_r, m_r, -1.0, a_r, ALU.mult, ALU.mult),
            ins=[Br], outs=[Br], inc=self.s_dv)
        hoist = gain_row is None
        if hoist:
            self.mm(p2[:, 0:n], self.onesf[0:1, :], a_r, s2, [Br, self.B_c],
                    start=True, stop=True, first=True)
            self.mm(p3[:, 0:n], self.onesf[0:1, :], b_r, s3, [Br, self.B_c],
                    start=True, stop=True, first=True)
        else:
            a16 = self.lnr16[0:1, 0:n]
            b16 = self.lnr16[0:1, 512:512 + n]
            b.op("vector", lambda e, c: e.tensor_copy(a16, a_r),
                 ins=[Br], outs=[self.B_lnr16], inc=self.s_dv)
            b.op("vector", lambda e, c: e.tensor_copy(b16, b_r),
                 ins=[Br], outs=[self.B_lnr16], inc=self.s_dv)
        for ft in range(nt):
            if not hoist:
                gr, gb = gain_row
                lrow = gr[0:1, ft * 128:(ft + 1) * 128]
                self.mm(p2[:, 0:n], lrow, a16, s2, [self.B_lnr16, gb],
                        start=True, stop=True, first=True)
                self.mm(p3[:, 0:n], lrow, b16, s3, [self.B_lnr16, gb],
                        start=True, stop=True, first=True)
            src, bsrc = src_tiles[ft]
            dst, bdst = dst_tiles[ft]
            b.op("vector", lambda e, c, src=src, xb=xb: e.tensor_mul(
                xb, src, p2[:, 0:n]),
                ins=[bsrc, s2], outs=[B_stg], inc=self.s_dv)
            b.op("vector", lambda e, c, dst=dst, xb=xb: e.tensor_add(
                dst, xb, p3[:, 0:n]),
                ins=[B_stg, s3], outs=[bdst], inc=self.s_dv)


    def phase_seed(self):
        sems = [self.s_ld, self.s_wg, self.s_pe, self.s_dv, self.s_ac,
                self.s_gp, self.s_od, self.s_odg]
        return [(s, s.n) for s in sems if s.n > 0]

    def odma(self, eng, fn, ins=(), outs=()):
        s = self.s_odg if eng == "gpsimd" else self.s_od
        return self.b.op(eng, fn, ins=ins, outs=outs, inc=s, amt=16,
                         waits=[(s, s.n)])

    def rowbcast(self, bcast_bank, src_full, src_buf, h, n, nrows, bf, mv_bank,
                 evac_eng="scalar"):
        ps, psB = self.ps, self.psB
        b = self.b
        ident = self.identb if bf else self.identf
        ones = self.onesb if bf else self.onesf
        rstg = self.M["rowstg16"] if bf else self.M["rowstg"]
        Brstg = self.M["B_rowstg16"] if bf else self.M["B_rowstg"]
        self.mm(ps[mv_bank][0:1, 0:n], ident[0:nrows, h:h + 1], src_full,
                psB[mv_bank], [src_buf, self.B_c], start=True, stop=True,
                first=True)
        if evac_eng == "scalar":
            b.op("scalar", lambda e, c: e.copy(rstg[0:1, 0:n],
                                               ps[mv_bank][0:1, 0:n]),
                 ins=[psB[mv_bank]], outs=[Brstg], inc=self.s_ac)
        else:
            b.op("vector", lambda e, c: e.tensor_copy(rstg[0:1, 0:n],
                                                      ps[mv_bank][0:1, 0:n]),
                 ins=[psB[mv_bank]], outs=[Brstg], inc=self.s_dv)
        self.mm(ps[bcast_bank][:, 0:n], ones[0:1, :], rstg[0:1, 0:n],
                psB[bcast_bank], [Brstg, self.B_c], start=True, stop=True,
                first=True)

    def prelude(self):
        b = self.b
        ld, gp = self.s_ld, self.s_gp

        xr3 = self.xres.rearrange("p (f t) -> p f t", t=TPC)
        xs3 = self.i_x.rearrange("(f p) t -> p f t", p=128)
        b.op("sync", lambda e, c: e.dma_start(
            out=xr3[:, :, 0:BLK], in_=xs3[:, :, 3:3 + BLK]),
            outs=[self.B_x], inc=ld, amt=16)
        b.op("sync", lambda e, c: e.dma_start(
            out=xr3[:, :, BLK:TPC], in_=xs3[:, :, 518:518 + BLK]),
            outs=[self.B_x], inc=ld, amt=16)
        self.xhalo = [self.af.alloc(16 * 3, f"xhalo{r}") for r in range(2)]
        self.B_xh = [Buf("xhalo0"), Buf("xhalo1")]
        for r, src0 in ((0, 0), (1, 515)):
            b.op("sync", lambda e, c, r=r, src0=src0: e.dma_start(
                out=self.xhalo[r].rearrange("p (f t) -> p f t", t=3),
                in_=xs3[:, :, src0:src0 + 3]),
                outs=[self.B_xh[r]], inc=ld, amt=16)

        def pl(dst, src, bufs=None):
            b.op("sync", lambda e, c, d=dst, s=src: e.dma_start(out=d, in_=s),
                 outs=[bufs or self.B_par], inc=ld, amt=16)

        self.odma("sync", lambda e, c: e.dma_start(
            out=self.p_dtb[0:HEADS, :], in_=self.i_dtb), outs=[self.B_par])
        self.odma("sync", lambda e, c: e.dma_start(
            out=self.p_negA[0:HEADS, :], in_=self.i_alog), outs=[self.B_par])
        pl(self.p_dpair, self.i_dpair)
        pl(self.p_jcol, self.i_jcol)
        def plg(dst, src2):
            self.odma("gpsimd", lambda e, c, d=dst, s=src2: e.dma_start(
                out=d, in_=s), outs=[self.B_rows])
        plg(self.grows[0:1, 0:DM], self.i_gqr)
        plg(self.grows[0:1, DM:2 * DM], self.i_gkvr)
        pl(self.p_normw, self.i_normw)
        b.op("sync", lambda e, c: e.dma_start(
            out=self.p_convw.rearrange("p (a b) -> p a b", b=4),
            in_=self.i_conv.rearrange("(a p) b -> p a b", p=128)),
            outs=[self.B_par], inc=ld, amt=16)

        b.op("gpsimd", lambda e, c: e.memset(self.onesf, 1.0),
             outs=[self.B_c], inc=gp)
        b.op("gpsimd", lambda e, c: e.memset(self.onesb, 1.0),
             outs=[self.B_c], inc=gp)
        for t in (self.identb, self.identf):
            b.op("gpsimd", lambda e, c, t=t: e.memset(t, 1.0),
                 outs=[self.B_c], inc=gp)
            b.op("gpsimd", lambda e, c, t=t: e.affine_select(
                out=t, in_=t, compare_op=ALU.is_equal, fill=0.0,
                base=0, pattern=[[-1, 128]], channel_multiplier=1),
                outs=[self.B_c], inc=gp)
        for ti in range(2):
            t = self.trilm[:, ti * CHUNK:(ti + 1) * CHUNK]
            b.op("gpsimd", lambda e, c, t=t: e.memset(t, 1.0),
                 outs=[self.B_c], inc=gp)
            b.op("gpsimd", lambda e, c, t=t, ti=ti: e.affine_select(
                out=t, in_=t, compare_op=ALU.is_ge, fill=0.0,
                base=-ti * 128, pattern=[[1, CHUNK]], channel_multiplier=-1),
                outs=[self.B_c], inc=gp)
        for w in range(4):
            t = self.fmask[:, w * BLK:(w + 1) * BLK]
            b.op("gpsimd", lambda e, c, t=t: e.memset(t, 1.0),
                 outs=[self.B_c], inc=gp)
            b.op("gpsimd", lambda e, c, t=t, w=w: e.affine_select(
                out=t, in_=t, compare_op=ALU.is_ge, fill=0.0,
                base=-128 * w, pattern=[[1, BLK]], channel_multiplier=-1),
                outs=[self.B_c], inc=gp)
        for i, v in enumerate((LN_EPS, MAMBA_EPS, FFN_EPS, 1.0)):
            b.op("gpsimd", lambda e, c, i=i, v=v: e.memset(
                self.cc[:, i:i + 1], v), outs=[self.B_c], inc=gp)

        b.op("scalar", lambda e, c: e.activation(
            self.p_negA[0:HEADS, :], self.p_negA[0:HEADS, :], AF.Exp),
            ins=[self.B_par], outs=[self.B_par], inc=self.s_ac)
        b.op("vector", lambda e, c: e.tensor_scalar_mul(
            self.p_negA[0:HEADS, :], self.p_negA[0:HEADS, :], -1.0),
            ins=[self.B_par], outs=[self.B_par], inc=self.s_dv)

    def mamba(self):
        af, ab, b = self.af, self.ab, self.b
        af.mark()
        ab.mark()
        M = {}
        self.M = M

        def abuf(ar, n, name):
            M[name] = ar.alloc(n, name)
            M["B_" + name] = Buf(name)
            return M[name]

        self.front_col0 = ab.cur
        abuf(ab, 16 * 515, "xlnT")
        M["wbuf"] = [ab.alloc(16 * 128, f"wb{i}") for i in range(2)]
        M["B_wbuf"] = [Buf("wb0"), Buf("wb1")]
        abuf(ab, 34 * 259, "xbc")
        abuf(ab, 1024, "lnstg")
        abuf(ab, 512, "zstg")
        abuf(ab, 512, "xstg1")
        abuf(ab, 512, "zg")
        abuf(ab, 2 * 4096, "xhT")
        abuf(ab, 2 * 64, "dtT")
        abuf(ab, 2 * 64, "decT")
        abuf(ab, 2 * 128, "BT")
        abuf(ab, 2 * 256, "Mt")
        abuf(ab, 64, "cdT")
        abuf(ab, 256, "dec")
        abuf(ab, 256, "dt16")
        abuf(ab, 102, "hs")
        self.front_col1 = ab.cur
        self.front_bufs = [M["B_xlnT"], M["B_wbuf"][0], M["B_wbuf"][1],
                           M["B_xbc"], M["B_lnstg"], M["B_zstg"],
                           M["B_xstg1"], M["B_zg"], M["B_xhT"], M["B_dtT"],
                           M["B_decT"], M["B_BT"], M["B_Mt"], M["B_cdT"],
                           M["B_dec"], M["B_dt16"], M["B_hs"]]
        M["st0"] = [ab.alloc(4096, f"st0_{r}") for r in range(2)]
        M["B_st0"] = [Buf("st0_0"), Buf("st0_1")]
        M["edAc"] = [ab.alloc(256, f"edAc{i}") for i in range(4)]
        M["B_edAc"] = [Buf(f"edAc{i}") for i in range(4)]
        M["cdB"] = [ab.alloc(64, f"cdB{i}") for i in range(4)]
        M["B_cdB"] = [Buf(f"cdB{i}") for i in range(4)]
        M["Cb"] = [ab.alloc(256, f"Cb{i}") for i in range(4)]
        M["B_Cb"] = [Buf(f"Cb{i}") for i in range(4)]
        M["CBm"] = [ab.alloc(2 * 256, f"CBm{i}") for i in range(4)]
        M["B_CBm"] = [Buf(f"CBm{i}") for i in range(4)]
        abuf(ab, 2 * 256, "ypstg")
        abuf(af, 512, "dtraw")
        abuf(af, 512, "convacc")
        abuf(af, 256, "Gtmp")
        abuf(af, 256, "dtf")
        abuf(af, 256, "dAf")
        abuf(af, 256, "dAc")
        abuf(af, 2 * 64, "dAcT")
        abuf(af, 1, "logtot")
        abuf(af, 64, "logrow")
        abuf(af, 3 * 16, "Lr")
        abuf(af, 16, "bvec")
        abuf(af, 16, "bv2")
        abuf(af, 16, "ohtmp")
        abuf(af, 1, "scm1")
        abuf(af, 512, "rowstg")
        abuf(af, 16, "iota16")
        b.op("gpsimd", lambda e, c: e.iota(
            M["iota16"][0:HEADS, :], pattern=[[1, 16]], base=0,
            channel_multiplier=0, allow_small_or_imprecise_dtypes=True),
            outs=[M["B_iota16"]], inc=self.s_gp)

        nfronts = 1 if self.dbg in ("ip0", "ln0", "f0") else 2
        for run in range(nfronts):
            self.mamba_front(run)
        if self.dbg in ("ip0", "ln0", "f0", "ff"):
            af.release()
            ab.release()
            return

        seed = []
        for bu in self.front_bufs:
            seed += bu.ready + bu.freed

        cur = self.front_col0

        def ov(n, name):
            nonlocal cur
            assert cur + n <= self.front_col1, f"overlay overflow {name}"
            M[name] = ab.t.ap()[:, cur:cur + n]
            bb = Buf(name)
            bb.freed = list(seed)
            M["B_" + name] = bb
            cur += n
            return M[name]

        ov(8192, "ych")
        M["strm"] = M["ych"].bitcast(F32)[:, 0:4096]
        M["B_strm"] = M["B_ych"]
        ov(8192, "senter")
        M["senter_f"] = M["senter"].bitcast(F32)
        ov(4096, "prevb")
        ov(512, "ysq")
        ov(256, "Cw")
        ov(512, "zg2")
        ov(16 * 64, "coefB")
        ov(512, "rowstg16")
        ov(16, "coef16")
        ov(64, "coefT")
        M["wobuf"] = [ov(32 * 128, "wo0"), ov(32 * 128, "wo1")]
        M["B_wobuf"] = [M["B_wo0"], M["B_wo1"]]

        for run in range(2):
            self.mamba_yhalf(run)

        af.release()
        ab.release()

    def mamba_front(self, run):
        M, b = self.M, self.b
        ld, wg, dv, ac = self.s_ld, self.s_wg, self.s_dv, self.s_ac
        ps, psB = self.ps, self.psB
        xr3 = self.xres.rearrange("p (f t) -> p f t", t=TPC)
        xlnT3 = M["xlnT"].rearrange("p (f t) -> p f t", t=515)
        B_xln = M["B_xlnT"]

        xh3 = self.xhalo[run].rearrange("p (f t) -> p f t", t=3)
        self.emit_ln(
            [(xlnT3[:, ft, 0:3], B_xln) for ft in range(16)],
            [(xh3[:, ft, :], self.B_xh[run]) for ft in range(16)],
            3, None, M["lnstg"], M["B_lnstg"])
        self.emit_ln(
            [(xlnT3[:, ft, 3:515], B_xln) for ft in range(16)],
            [(xr3[:, ft, run * BLK:(run + 1) * BLK], self.B_x)
             for ft in range(16)],
            BLK, None, M["lnstg"], M["B_lnstg"])

        if self.dbg == "ln0":
            return
        import os
        xbc3 = M["xbc"].rearrange("p (f t) -> p f t", t=259)
        for m in range(int(os.environ.get("MLIM", "67"))):
            wb = M["wbuf"][m % 2]
            Bwb = M["B_wbuf"][m % 2]
            mc = 64 if m == 66 else 128
            b.op("gpsimd", lambda e, c, m=m, wb=wb, mc=mc: e.dma_start(
                out=wb[:, 0:16 * mc].rearrange("p (k q) -> p k q", k=16),
                in_=self.i_winT[:, m * 128:m * 128 + mc].rearrange(
                    "(k p) q -> p k q", p=128)),
                outs=[Bwb], inc=wg, amt=16)
            wb3 = wb[:, 0:16 * mc].rearrange("p (k q) -> p k q", q=mc)
            pmain, phalo = psB[(m % 2) * 2], psB[(m % 2) * 2 + 1]
            vmain, vhalo = ps[(m % 2) * 2], ps[(m % 2) * 2 + 1]
            for k in range(16):
                self.mm(vmain[0:mc, 0:512], wb3[:, k, :], xlnT3[:, k, 3:515],
                        pmain, [Bwb, B_xln], start=(k == 0), stop=(k == 15),
                        first=(k == 0))
            if 32 <= m < 66:
                for k in range(16):
                    self.mm(vhalo[0:mc, 0:3], wb3[:, k, :], xlnT3[:, k, 0:3],
                            phalo, [Bwb, B_xln], start=(k == 0),
                            stop=(k == 15), first=(k == 0))
            if m < 32:
                b.op("vector", lambda e, c, vmain=vmain: e.tensor_copy(
                    M["zstg"], vmain[:, 0:512]),
                    ins=[pmain], outs=[M["B_zstg"]], inc=dv)
                if os.environ.get("NOSPILL"):
                    continue
                b.op("sync", lambda e, c, m=m, run=run: e.dma_start(
                    out=self.d_z.rearrange(
                        "(ci f p) t -> p ci f t", p=128, f=32)
                    [:, 2 * run:2 * run + 2, m, :],
                    in_=M["zstg"].rearrange("p (ci t) -> p ci t", ci=2)),
                    ins=[M["B_zstg"]], outs=[self.Bd_z], inc=ld, amt=16)
            elif m < 66:
                ft = m - 32
                b.op("vector", lambda e, c, ft=ft, vmain=vmain: e.tensor_copy(
                    xbc3[:, ft, 3:259], vmain[:, 0:256]),
                    ins=[pmain], outs=[M["B_xbc"]], inc=dv)
                b.op("vector", lambda e, c, ft=ft, vhalo=vhalo: e.tensor_copy(
                    xbc3[:, ft, 0:3], vhalo[:, 0:3]),
                    ins=[phalo], outs=[M["B_xbc"]], inc=dv)
                b.op("vector", lambda e, c, vmain=vmain: e.tensor_copy(
                    M["xstg1"][:, 0:256], vmain[:, 256:512]),
                    ins=[pmain], outs=[M["B_xstg1"]], inc=dv)
                b.op("sync", lambda e, c, ft=ft: e.dma_start(
                    out=self.d_xbc1[ft * 128:(ft + 1) * 128, :],
                    in_=M["xstg1"][:, 0:256]),
                    ins=[M["B_xstg1"]], outs=[self.Bd_xbc1], inc=ld, amt=16)
            else:
                b.op("vector", lambda e, c, vmain=vmain: e.tensor_copy(
                    M["dtraw"][0:HEADS, :], vmain[0:HEADS, 0:512]),
                    ins=[pmain], outs=[M["B_dtraw"]], inc=dv)

        if self.dbg == "ip0" or self.dbg == "ln0":
            return
        for cci in range(2):
            self.ssd_front(run, cci)

    def emit_conv(self):
        M, b = self.M, self.b
        dv, ac = self.s_dv, self.s_ac
        xbc3 = M["xbc"].rearrange("p (f t) -> p f t", t=259)
        cw3 = self.p_convw.rearrange("p (f a) -> p f a", a=4)
        acc = M["convacc"][:, 0:256]
        Ba = M["B_convacc"]
        for ft in range(34):
            for tap in range(4):
                if tap == 0:
                    b.op("vector", lambda e, c, ft=ft: e.tensor_scalar(
                        acc, xbc3[:, ft, 0:256], cw3[:, ft, 0:1],
                        None, ALU.mult),
                        ins=[M["B_xbc"], self.B_par], outs=[Ba], inc=dv)
                else:
                    b.op("vector", lambda e, c, ft=ft, tap=tap:
                         e.scalar_tensor_tensor(
                             acc, xbc3[:, ft, tap:tap + 256],
                             cw3[:, ft, tap:tap + 1], acc,
                             ALU.mult, ALU.add),
                         ins=[M["B_xbc"], self.B_par, Ba], outs=[Ba], inc=dv)
            b.op("scalar", lambda e, c, ft=ft: e.activation(
                xbc3[:, ft, 3:259], acc, AF.Silu),
                ins=[Ba], outs=[M["B_xbc"]], inc=ac)

    def ssd_front(self, run, cci):
        M, b = self.M, self.b
        ld, dv, ac = self.s_ld, self.s_dv, self.s_ac
        ps, psB = self.ps, self.psB
        xbc3 = M["xbc"].rearrange("p (f t) -> p f t", t=259)
        col0 = 3
        ch = run * 2 + cci

        if cci == 0:
            b.op("vector", lambda e, c: e.tensor_copy(
                M["hs"].rearrange("p (f t) -> p f t", t=3),
                xbc3[:, :, 256:259]),
                ins=[M["B_xbc"]], outs=[M["B_hs"]], inc=dv)
            self.emit_conv()
        else:
            b.op("sync", lambda e, c: e.dma_start(
                out=xbc3[:, :, 3:259],
                in_=self.d_xbc1.rearrange("(f p) t -> p f t", p=128)),
                ins=[self.Bd_xbc1], outs=[M["B_xbc"]], inc=ld, amt=16)
            b.op("vector", lambda e, c: e.tensor_copy(
                xbc3[:, :, 0:3], M["hs"].rearrange("p (f t) -> p f t", t=3)),
                ins=[M["B_hs"]], outs=[M["B_xbc"]], inc=dv)
            self.emit_conv()

        dtr = M["dtraw"][0:HEADS, cci * CHUNK:(cci + 1) * CHUNK]
        dtf = M["dtf"][0:HEADS, :]
        dA = M["dAf"][0:HEADS, :]
        dAc = M["dAc"][0:HEADS, :]
        G = M["Gtmp"]
        Bdtr, Bdtf, BdA, BdAc = (M["B_dtraw"], M["B_dtf"], M["B_dAf"],
                                 M["B_dAc"])
        b.op("scalar", lambda e, c, dtr=dtr: e.activation(
            G[0:HEADS, :], dtr, AF.Abs, bias=self.p_dtb[0:HEADS, :]),
            ins=[Bdtr, self.B_par], outs=[M["B_Gtmp"]], inc=ac)
        b.op("scalar", lambda e, c: e.activation(
            G[0:HEADS, :], G[0:HEADS, :], AF.Exp, scale=-1.0),
            ins=[M["B_Gtmp"]], outs=[M["B_Gtmp"]], inc=ac)
        b.op("scalar", lambda e, c: e.activation(
            G[0:HEADS, :], G[0:HEADS, :], AF.Ln, bias=self.cc[0:HEADS, 3:4]),
            ins=[M["B_Gtmp"], self.B_c], outs=[M["B_Gtmp"]], inc=ac)
        b.op("scalar", lambda e, c, dtr=dtr: e.activation(
            dtf, dtr, AF.Relu, bias=self.p_dtb[0:HEADS, :]),
            ins=[Bdtr, self.B_par], outs=[Bdtf], inc=ac)
        b.op("vector", lambda e, c: e.tensor_add(dtf, dtf, G[0:HEADS, :]),
             ins=[Bdtf, M["B_Gtmp"]], outs=[Bdtf], inc=dv)
        b.op("vector", lambda e, c: e.tensor_scalar(
            dA, dtf, self.p_negA[0:HEADS, :], None, ALU.mult),
            ins=[Bdtf, self.B_par], outs=[BdA], inc=dv)
        b.op("vector", lambda e, c: e.tensor_tensor_scan(
            dAc, dA, dA, 0.0, ALU.add, ALU.bypass),
            ins=[BdA], outs=[BdAc], inc=dv)
        b.op("scalar", lambda e, c: e.activation(
            M["dec"][0:HEADS, :], dAc, AF.Exp, scale=-1.0,
            bias=dAc[:, CHUNK - 1:CHUNK]),
            ins=[BdAc], outs=[M["B_dec"]], inc=ac)
        b.op("scalar", lambda e, c, ch=ch: e.activation(
            M["edAc"][ch][0:HEADS, :], dAc, AF.Exp),
            ins=[BdAc], outs=[M["B_edAc"][ch]], inc=ac)
        b.op("vector", lambda e, c: e.tensor_copy(M["dt16"][0:HEADS, :], dtf),
             ins=[Bdtf], outs=[M["B_dt16"]], inc=dv)
        if cci == 0:
            b.op("vector", lambda e, c: e.tensor_copy(
                M["logtot"][0:HEADS, :], dAc[:, CHUNK - 1:CHUNK]),
                ins=[BdAc], outs=[M["B_logtot"]], inc=dv)
        else:
            b.op("vector", lambda e, c: e.tensor_add(
                M["logtot"][0:HEADS, :], M["logtot"][0:HEADS, :],
                dAc[:, CHUNK - 1:CHUNK]),
                ins=[M["B_logtot"], BdAc], outs=[M["B_logtot"]], inc=dv)

        dtT3 = M["dtT"].rearrange("p (s h) -> p s h", s=2)
        decT3 = M["decT"].rearrange("p (s h) -> p s h", s=2)
        dAcT3 = M["dAcT"].rearrange("p (s h) -> p s h", s=2)
        BT3 = M["BT"].rearrange("p (s q) -> p s q", s=2)
        for ti in range(2):
            self.tposeb(dtT3[:, ti, :], M["B_dtT"],
                        M["dt16"][0:HEADS, ti * 128:(ti + 1) * 128],
                        [M["B_dt16"]], bank=4, r=HEADS, c=128)
            self.tposeb(decT3[:, ti, :], M["B_decT"],
                        M["dec"][0:HEADS, ti * 128:(ti + 1) * 128],
                        [M["B_dec"]], bank=5, r=HEADS, c=128)
            self.tposef(dAcT3[:, ti, :], M["B_dAcT"],
                        dAc[:, ti * 128:(ti + 1) * 128], [BdAc],
                        bank=6, r=HEADS, c=128)
            self.tposeb(BT3[:, ti, :], M["B_BT"],
                        xbc3[:, 32, col0 + ti * 128:col0 + (ti + 1) * 128],
                        [M["B_xbc"]], bank=4, r=128, c=128)
        self.tposeb(M["cdT"][0:1, 0:HEADS], M["B_cdT"],
                    M["edAc"][ch][0:HEADS, CHUNK - 1:CHUNK],
                    [M["B_edAc"][ch]], bank=5, r=HEADS, c=1)
        self.mm(ps[5][:, 0:HEADS], self.onesb[0:1, :], M["cdT"][0:1, 0:HEADS],
                psB[5], [M["B_cdT"], self.B_c], start=True, stop=True,
                first=True)
        b.op("vector", lambda e, c, ch=ch: e.tensor_copy(
            M["cdB"][ch], ps[5][:, 0:HEADS]),
            ins=[psB[5]], outs=[M["B_cdB"][ch]], inc=dv)
        b.op("vector", lambda e, c, ch=ch: e.tensor_copy(
            M["Cb"][ch], xbc3[:, 33, col0:col0 + CHUNK]),
            ins=[M["B_xbc"]], outs=[M["B_Cb"][ch]], inc=dv)

        xhT3 = M["xhT"].rearrange("p (s f) -> p s f", s=2)
        for ft in range(32):
            for w in range(2):
                self.tposeb(xhT3[:, w, ft * 128:(ft + 1) * 128], M["B_xhT"],
                            xbc3[:, ft, col0 + w * 128:col0 + (w + 1) * 128],
                            [M["B_xbc"]], bank=4 + (ft + w) % 2, r=128, c=128)

        for ti in range(2):
            self.mm(ps[6 + ti][:, 0:CHUNK],
                    xbc3[:, 32, col0 + ti * 128:col0 + (ti + 1) * 128],
                    M["Cb"][ch], psB[6 + ti], [M["B_xbc"], M["B_Cb"][ch]],
                    start=True, stop=True, first=True)
            b.op("vector", lambda e, c, ti=ti, ch=ch: e.tensor_mul(
                M["CBm"][ch][:, ti * CHUNK:(ti + 1) * CHUNK],
                self.trilm[:, ti * CHUNK:(ti + 1) * CHUNK],
                ps[6 + ti][:, 0:CHUNK]),
                ins=[psB[6 + ti], self.B_c], outs=[M["B_CBm"][ch]], inc=dv)

        for ti in range(2):
            b.op("vector", lambda e, c, ti=ti: e.tensor_mul(
                xhT3[:, ti, :].rearrange("p (h q) -> p h q", q=PD),
                xhT3[:, ti, :].rearrange("p (h q) -> p h q", q=PD),
                dtT3[:, ti, :].unsqueeze(2).broadcast_to([128, HEADS, PD])),
                ins=[M["B_xhT"], M["B_dtT"]], outs=[M["B_xhT"]], inc=dv)

        for pair in range(32):
            yb = psB[2 + pair % 2]
            yv = ps[2 + pair % 2]
            for half in range(2):
                h = 2 * pair + half
                lb = psB[h % 2]
                lv = ps[h % 2]
                self.rowbcast(h % 2, M["dAc"][0:HEADS, :], M["B_dAc"], h,
                              CHUNK, HEADS, bf=False, mv_bank=6 + h % 2)
                for ti in range(2):
                    b.op("vector", lambda e, c, h=h, ti=ti, lv=lv:
                         e.tensor_scalar(
                             G, lv[:, 0:CHUNK], dAcT3[:, ti, h:h + 1],
                             0.0, ALU.subtract, ALU.min),
                         ins=[lb, M["B_dAcT"]], outs=[M["B_Gtmp"]], inc=dv)
                    b.op("scalar", lambda e, c, ti=ti: e.activation(
                        M["Mt"][:, ti * CHUNK:(ti + 1) * CHUNK], G, AF.Exp),
                        ins=[M["B_Gtmp"]], outs=[M["B_Mt"]], inc=ac)
                    b.op("vector", lambda e, c, ti=ti, ch=ch: e.tensor_mul(
                        M["Mt"][:, ti * CHUNK:(ti + 1) * CHUNK],
                        M["Mt"][:, ti * CHUNK:(ti + 1) * CHUNK],
                        M["CBm"][ch][:, ti * CHUNK:(ti + 1) * CHUNK]),
                        ins=[M["B_Mt"], M["B_CBm"][ch]], outs=[M["B_Mt"]],
                        inc=dv)
                for ti in range(2):
                    self.mm(yv[half * 64:(half + 1) * 64, 0:CHUNK],
                            xhT3[:, ti, h * PD:(h + 1) * PD],
                            M["Mt"][:, ti * CHUNK:(ti + 1) * CHUNK],
                            yb, [M["B_xhT"], M["B_Mt"]],
                            start=(ti == 0), stop=(ti == 1),
                            first=(half == 0 and ti == 0))
            zgv = M["zg"][:, (pair % 2) * CHUNK:(pair % 2 + 1) * CHUNK]
            b.op("sync", lambda e, c, pair=pair, ch=ch, zgv=zgv: e.dma_start(
                out=zgv,
                in_=self.d_z[ch * HID + pair * 128:
                             ch * HID + (pair + 1) * 128, :]),
                ins=[self.Bd_z], outs=[M["B_zg"]], inc=ld, amt=16)
            b.op("scalar", lambda e, c, zgv=zgv: e.activation(
                zgv, zgv, AF.Silu),
                ins=[M["B_zg"]], outs=[M["B_zg"]], inc=ac)
            stg = M["ypstg"][:, (pair % 2) * CHUNK:(pair % 2 + 1) * CHUNK]
            Bstg = M["B_ypstg"]
            b.op("vector", lambda e, c, pair=pair, yv=yv, stg=stg:
                 e.scalar_tensor_tensor(
                     stg, xbc3[:, pair, col0:col0 + CHUNK],
                     self.p_dpair[:, pair:pair + 1], yv[:, 0:CHUNK],
                     ALU.mult, ALU.add),
                 ins=[M["B_xbc"], self.B_par, yb], outs=[Bstg], inc=dv)
            b.op("vector", lambda e, c, stg=stg, zgv=zgv: e.tensor_mul(
                stg, stg, zgv),
                ins=[Bstg, M["B_zg"]], outs=[Bstg], inc=dv)
            b.op("sync", lambda e, c, pair=pair, ch=ch, stg=stg: e.dma_start(
                out=self.d_yp[ch * HID + pair * 128:
                              ch * HID + (pair + 1) * 128, :],
                in_=stg),
                ins=[Bstg], outs=[self.Bd_yp], inc=ld, amt=16)

        for ti in range(2):
            b.op("vector", lambda e, c, ti=ti: e.tensor_mul(
                xhT3[:, ti, :].rearrange("p (h q) -> p h q", q=PD),
                xhT3[:, ti, :].rearrange("p (h q) -> p h q", q=PD),
                decT3[:, ti, :].unsqueeze(2).broadcast_to([128, HEADS, PD])),
                ins=[M["B_xhT"], M["B_decT"]], outs=[M["B_xhT"]], inc=dv)
        st, stb = M["st0"][run], M["B_st0"][run]
        for pair in range(32):
            sb = psB[6 + pair % 2]
            sv = ps[6 + pair % 2]
            for half in range(2):
                h = 2 * pair + half
                for ti in range(2):
                    self.mm(sv[:, half * 64:(half + 1) * 64],
                            BT3[:, ti, :], xhT3[:, ti, h * PD:(h + 1) * PD],
                            sb, [M["B_BT"], M["B_xhT"]],
                            start=(ti == 0), stop=(ti == 1),
                            first=(half == 0 and ti == 0))
            if cci == 0:
                b.op("vector", lambda e, c, pair=pair, st=st, sv=sv:
                     e.tensor_copy(
                         st[:, pair * 128:(pair + 1) * 128], sv[:, 0:128]),
                     ins=[sb], outs=[stb], inc=dv)
            else:
                tf = M["rowstg"][:, 128:256]
                cdv = M["cdB"][ch][:, 2 * pair:2 * pair + 2].unsqueeze(
                    2).broadcast_to([128, 2, PD])
                b.op("vector", lambda e, c, pair=pair, cdv=cdv, tf=tf, ch=ch:
                     e.tensor_mul(
                         tf.rearrange("p (a q) -> p a q", q=PD),
                         M["st0"][run].rearrange("p (h q) -> p h q", q=PD)
                         [:, 2 * pair:2 * pair + 2, :], cdv),
                     ins=[M["B_st0"][run], M["B_cdB"][ch]],
                     outs=[M["B_rowstg"]], inc=dv)
                b.op("vector", lambda e, c, tf=tf, sv=sv: e.tensor_add(
                    tf, tf, sv[:, 0:128]),
                    ins=[M["B_rowstg"], sb], outs=[M["B_rowstg"]], inc=dv)
                b.op("sync", lambda e, c, pair=pair, run=run, tf=tf:
                     e.dma_start(
                         out=self.d_stin[run][:, pair * 128:(pair + 1) * 128],
                         in_=tf),
                     ins=[M["B_rowstg"]], outs=[self.Bd_stin[run]],
                     inc=ld, amt=16)

        if cci == 1:
            self.tposef(M["logrow"][0:1, 0:HEADS], M["B_logrow"],
                        M["logtot"][0:HEADS, :], [M["B_logtot"]],
                        bank=5, r=HEADS, c=1)
            self.odma("sync", lambda e, c, run=run: e.dma_start(
                out=self.d_lrin[run], in_=M["logrow"][0:1, 0:HEADS]),
                ins=[M["B_logrow"]], outs=[self.Bd_lrin[run]])
            b.op("gpsimd", lambda e, c, run=run: e.collective_compute(
                "AllGather", ALU.bypass,
                replica_groups=[list(range(NCORE))],
                ins=[self.d_stin[run].opt()], outs=[self.d_stout[run].opt()]),
                ins=[self.Bd_stin[run]], inc=self.s_agst[run])
            self.Bd_stout[run].ready = [(self.s_agst[run], 1)]
            b.op("gpsimd", lambda e, c, run=run: e.collective_compute(
                "AllGather", ALU.bypass,
                replica_groups=[list(range(NCORE))],
                ins=[self.d_lrin[run].opt()], outs=[self.d_lrout[run].opt()]),
                ins=[self.Bd_lrin[run]], inc=self.s_aglr[run])
            self.Bd_lrout[run].ready = [(self.s_aglr[run], 1)]

    def mamba_prefix_coef(self, run):
        M, b = self.M, self.b
        ld, dv, ac = self.s_ld, self.s_dv, self.s_ac
        ps, psB = self.ps, self.psB
        Lr = M["Lr"]
        BLr = M["B_Lr"]
        stg16 = self.ln_rows[0:16, 0:64]
        Bstg16 = self.B_lnrows
        for half in range(2):
            self.odma("sync", lambda e, c, half=half: e.dma_start(
                out=stg16[half * 8:half * 8 + 8, :],
                in_=self.d_lrout[half]),
                ins=[self.Bd_lrout[half]], outs=[Bstg16])
        L = Lr[0:HEADS, 0:16]
        S = Lr[0:HEADS, 16:32]
        T1 = Lr[0:HEADS, 32:48]
        self.tposef(L, BLr, stg16, [Bstg16], bank=5, r=16, c=HEADS)
        b.op("vector", lambda e, c: e.tensor_tensor_scan(
            S[:, 0:8], L[:, 0:8], L[:, 0:8], 0.0, ALU.add, ALU.bypass),
            ins=[BLr], outs=[BLr], inc=dv)
        b.op("vector", lambda e, c: e.tensor_tensor_scan(
            T1[:, 0:8], L[:, 8:16], L[:, 8:16], 0.0, ALU.add, ALU.bypass),
            ins=[BLr], outs=[BLr], inc=dv)
        b.op("vector", lambda e, c: e.tensor_add(
            T1[:, 8:9], S[:, 7:8], T1[:, 7:8]),
            ins=[BLr], outs=[BLr], inc=dv)
        b.op("vector", lambda e, c: e.tensor_sub(
            S[:, 8:16], L[:, 8:16], T1[:, 0:8]),
            ins=[BLr], outs=[BLr], inc=dv)
        b.op("vector", lambda e, c: e.tensor_scalar(
            S[:, 8:16], S[:, 8:16], T1[:, 8:9], None, ALU.add),
            ins=[BLr], outs=[BLr], inc=dv)
        if run == 0:
            bv, Bbv = M["bvec"][0:HEADS, :], M["B_bvec"]
            b2, Bb2 = M["bv2"][0:HEADS, :], M["B_bv2"]
            it, Bit = M["iota16"][0:HEADS, :], M["B_iota16"]
            b.op("vector", lambda e, c: e.tensor_scalar(
                bv, it, 7.0, None, ALU.is_le),
                ins=[Bit], outs=[Bbv], inc=dv)
            b.op("vector", lambda e, c: e.tensor_scalar(
                b2, it, -2.0, 23.0, ALU.mult, ALU.add),
                ins=[Bit], outs=[Bb2], inc=dv)
            b.op("vector", lambda e, c: e.tensor_scalar(
                bv, bv, -1.0, 1.0, ALU.mult, ALU.add),
                ins=[Bbv], outs=[Bbv], inc=dv)
            b.op("vector", lambda e, c: e.tensor_mul(b2, b2, bv),
                 ins=[Bb2, Bbv], outs=[Bb2], inc=dv)
            b.op("vector", lambda e, c: e.tensor_add(bv, b2, it),
                 ins=[Bb2, Bit], outs=[Bbv], inc=dv)
        tcol = M["bv2"][0:HEADS, 0:1]
        if run == 0:
            b.op("vector", lambda e, c: e.tensor_copy(
                tcol, self.p_jcol[0:HEADS, :]),
                ins=[self.B_par], outs=[M["B_bv2"]], inc=dv)
        else:
            b.op("vector", lambda e, c: e.tensor_scalar(
                tcol, self.p_jcol[0:HEADS, :], -1.0, 15.0, ALU.mult, ALU.add),
                ins=[self.B_par], outs=[M["B_bv2"]], inc=dv)
        mk = Lr[0:HEADS, 32:48]
        b.op("vector", lambda e, c: e.tensor_scalar(
            mk, M["bvec"][0:HEADS, :], tcol, None, ALU.is_lt),
            ins=[M["B_bvec"], M["B_bv2"]], outs=[BLr], inc=dv)
        oh = M["ohtmp"][0:HEADS, :]
        b.op("vector", lambda e, c: e.tensor_scalar(
            oh, M["bvec"][0:HEADS, :], tcol, 1.0, ALU.subtract, ALU.add),
            ins=[M["B_bvec"], M["B_bv2"]], outs=[M["B_ohtmp"]], inc=dv)
        b.op("vector", lambda e, c: e.tensor_scalar(
            oh, oh, 0.0, None, ALU.is_equal),
            ins=[M["B_ohtmp"]], outs=[M["B_ohtmp"]], inc=dv)
        b.op("vector", lambda e, c: e.tensor_mul(oh, oh, S),
             ins=[M["B_ohtmp"], BLr], outs=[M["B_ohtmp"]], inc=dv)
        b.op("vector", lambda e, c: e.reduce_sum(
            M["scm1"][0:HEADS, :], oh, AXX),
            ins=[M["B_ohtmp"]], outs=[M["B_scm1"]], inc=dv)
        cf = M["ohtmp"][0:HEADS, :]
        b.op("vector", lambda e, c: e.tensor_scalar(
            cf, S, M["scm1"][0:HEADS, :], 0.0, ALU.subtract, ALU.max),
            ins=[BLr, M["B_scm1"]], outs=[M["B_ohtmp"]], inc=dv)
        b.op("scalar", lambda e, c: e.activation(cf, cf, AF.Exp, scale=-1.0),
             ins=[M["B_ohtmp"]], outs=[M["B_ohtmp"]], inc=ac)
        b.op("vector", lambda e, c: e.tensor_mul(cf, cf, mk),
             ins=[M["B_ohtmp"], BLr], outs=[M["B_ohtmp"]], inc=dv)
        b.op("vector", lambda e, c: e.tensor_copy(
            M["coef16"][0:HEADS, :], cf),
            ins=[M["B_ohtmp"]], outs=[M["B_coef16"]], inc=dv)
        self.tposeb(M["coefT"][0:16, 0:HEADS], M["B_coefT"],
                    M["coef16"][0:HEADS, :], [M["B_coef16"]],
                    bank=5, r=HEADS, c=16)
        for o in range(16):
            bank = 6 + o % 2
            self.rowbcast(bank, M["coefT"][0:16, 0:HEADS], M["B_coefT"], o,
                          HEADS, 16, bf=True, mv_bank=4)
            b.op("vector", lambda e, c, o=o, bank=bank: e.tensor_copy(
                M["coefB"][:, o * HEADS:(o + 1) * HEADS],
                self.ps[bank][:, 0:HEADS]),
                ins=[psB[bank]], outs=[M["B_coefB"]], inc=dv)

    def mamba_yhalf(self, run):
        M, b = self.M, self.b
        ld, dv = self.s_ld, self.s_dv
        self.mamba_prefix_coef(run)
        se, Bse = M["senter_f"], M["B_senter"]
        sm, Bsm = M["strm"], M["B_strm"]
        for o in range(16):
            src = self.d_stout[0] if o < 8 else self.d_stout[1]
            Bsrc = self.Bd_stout[0] if o < 8 else self.Bd_stout[1]
            slot = o if o < 8 else o - 8
            b.op("sync", lambda e, c, src=src, slot=slot: e.dma_start(
                out=sm, in_=src[slot * 128:slot * 128 + 128, :]),
                ins=[Bsrc], outs=[Bsm], inc=ld, amt=16)
            cb3 = M["coefB"][:, o * HEADS:(o + 1) * HEADS].unsqueeze(
                2).broadcast_to([128, HEADS, PD])
            if o == 0:
                b.op("vector", lambda e, c, cb3=cb3: e.tensor_mul(
                    se.rearrange("p (h q) -> p h q", q=PD),
                    sm.rearrange("p (h q) -> p h q", q=PD), cb3),
                    ins=[Bsm, M["B_coefB"]], outs=[Bse], inc=dv)
            else:
                b.op("vector", lambda e, c, cb3=cb3: e.tensor_mul(
                    sm.rearrange("p (h q) -> p h q", q=PD),
                    sm.rearrange("p (h q) -> p h q", q=PD), cb3),
                    ins=[Bsm, M["B_coefB"]], outs=[Bsm], inc=dv)
                b.op("vector", lambda e, c: e.tensor_add(se, se, sm),
                     ins=[Bse, Bsm], outs=[Bse], inc=dv)
        for cci in range(2):
            ch = run * 2 + cci
            if cci == 1:
                cdB3 = M["cdB"][run * 2].unsqueeze(2).broadcast_to(
                    [128, HEADS, PD])
                b.op("vector", lambda e, c: e.tensor_mul(
                    se.rearrange("p (h q) -> p h q", q=PD),
                    se.rearrange("p (h q) -> p h q", q=PD), cdB3),
                    ins=[Bse, M["B_cdB"][run * 2]], outs=[Bse], inc=dv)
                b.op("vector", lambda e, c: e.tensor_add(
                    se, se, M["st0"][run]),
                    ins=[Bse, M["B_st0"][run]], outs=[Bse], inc=dv)
            b.op("vector", lambda e, c: e.tensor_copy(M["prevb"], se),
                 ins=[Bse], outs=[M["B_prevb"]], inc=dv)
            self.yhalf_chunk(run, cci, ch, M["prevb"], M["B_prevb"])

    def yhalf_chunk(self, run, cci, ch, prevv, Bprev):
        M, b = self.M, self.b
        ld, dv, ac = self.s_ld, self.s_dv, self.s_ac
        ps, psB = self.ps, self.psB
        yg3 = M["ych"].rearrange("p (f t) -> p f t", t=CHUNK)
        Byg = M["B_ych"]
        Cw, BCw = M["Cw"], M["B_Cw"]
        for pair in range(32):
            yb = psB[2 + pair % 2]
            yv = ps[2 + pair % 2]
            for half in range(2):
                h = 2 * pair + half
                lb = psB[half]
                lv = ps[half]
                self.rowbcast(half, M["edAc"][ch][0:HEADS, :],
                              M["B_edAc"][ch], h, CHUNK, HEADS, bf=True,
                              mv_bank=6 + half)
                b.op("vector", lambda e, c, ch=ch, lv=lv: e.tensor_mul(
                    Cw, M["Cb"][ch], lv[:, 0:CHUNK]),
                    ins=[M["B_Cb"][ch], lb], outs=[BCw], inc=dv)
                self.mm(yv[half * 64:(half + 1) * 64, 0:CHUNK],
                        prevv[:, h * PD:(h + 1) * PD], Cw, yb,
                        [Bprev, BCw], start=True, stop=True,
                        first=(half == 0))
            zgv = M["zg2"][:, (pair % 2) * CHUNK:(pair % 2 + 1) * CHUNK]
            b.op("sync", lambda e, c, pair=pair, ch=ch, zgv=zgv: e.dma_start(
                out=zgv,
                in_=self.d_z[ch * HID + pair * 128:
                             ch * HID + (pair + 1) * 128, :]),
                ins=[self.Bd_z], outs=[M["B_zg2"]], inc=ld, amt=16)
            b.op("scalar", lambda e, c, zgv=zgv: e.activation(
                zgv, zgv, AF.Silu),
                ins=[M["B_zg2"]], outs=[M["B_zg2"]], inc=ac)
            stg = M["ypstg"][:, (pair % 2) * CHUNK:(pair % 2 + 1) * CHUNK]
            b.op("sync", lambda e, c, pair=pair, ch=ch, stg=stg: e.dma_start(
                out=stg,
                in_=self.d_yp[ch * HID + pair * 128:
                              ch * HID + (pair + 1) * 128, :]),
                ins=[self.Bd_yp], outs=[M["B_ypstg"]], inc=ld, amt=16)
            b.op("vector", lambda e, c, pair=pair, zgv=zgv, yv=yv:
                 e.tensor_mul(
                     yg3[:, pair, :], zgv, yv[:, 0:CHUNK]),
                 ins=[M["B_zg2"], yb], outs=[Byg], inc=dv)
            b.op("vector", lambda e, c, pair=pair, stg=stg: e.tensor_add(
                yg3[:, pair, :], yg3[:, pair, :], stg),
                ins=[Byg, M["B_ypstg"]], outs=[Byg], inc=dv)
        sq, Bsq = M["ysq"], M["B_ysq"]
        for ft in range(32):
            b.op("scalar", lambda e, c, ft=ft: e.activation(
                sq[:, 0:CHUNK], yg3[:, ft, :], AF.Square),
                ins=[Byg], outs=[Bsq], inc=ac)
            self.mm(ps[4][0:1, 0:CHUNK], self.onesb[:, 0:1], sq[:, 0:CHUNK],
                    psB[4], [Bsq, self.B_c], start=(ft == 0), stop=(ft == 31),
                    first=(ft == 0))
        rr = self.ln_rows[0:1, 0:CHUNK]
        Brr = self.B_lnrows
        b.op("vector", lambda e, c: e.tensor_scalar_mul(
            rr, ps[4][0:1, 0:CHUNK], 1.0 / HID),
            ins=[psB[4]], outs=[Brr], inc=dv)
        b.op("scalar", lambda e, c: e.activation(
            rr, rr, AF.Sqrt, bias=self.cc[0:1, 1:2]),
            ins=[Brr, self.B_c], outs=[Brr], inc=ac)
        b.op("vector", lambda e, c: e.reciprocal(rr, rr),
             ins=[Brr], outs=[Brr], inc=dv)
        self.mm(ps[5][:, 0:CHUNK], self.onesf[0:1, :], rr, psB[5],
                [Brr, self.B_c], start=True, stop=True, first=True)
        for ft in range(32):
            b.op("vector", lambda e, c, ft=ft: e.scalar_tensor_tensor(
                yg3[:, ft, :], yg3[:, ft, :],
                self.p_normw[:, ft:ft + 1], ps[5][:, 0:CHUNK],
                ALU.mult, ALU.mult),
                ins=[Byg, self.B_par, psB[5]], outs=[Byg], inc=dv)
        xr3 = self.xres.rearrange("p (f t) -> p f t", t=TPC)
        tc0 = run * BLK + cci * CHUNK
        for m in range(16):
            wb = M["wobuf"][m % 2]
            Bwb = M["B_wobuf"][m % 2]
            b.op("gpsimd", lambda e, c, m=m, wb=wb: e.dma_start(
                out=wb.rearrange("p (k q) -> p k q", k=32),
                in_=self.i_woutT[:, m * 128:(m + 1) * 128].rearrange(
                    "(k p) q -> p k q", p=128)),
                outs=[Bwb], inc=self.s_wg, amt=16)
            wb3 = wb.rearrange("p (k q) -> p k q", k=32)
            pb = psB[(m % 2)]
            pv = ps[(m % 2)]
            for k in range(32):
                self.mm(pv[:, 0:CHUNK], wb3[:, k, :], yg3[:, k, :],
                        pb, [Bwb, Byg], start=(k == 0), stop=(k == 31),
                        first=(k == 0))
            b.op("vector", lambda e, c, m=m, pv=pv: e.scalar_tensor_tensor(
                xr3[:, m, tc0:tc0 + CHUNK], pv[:, 0:CHUNK], 1.0,
                xr3[:, m, tc0:tc0 + CHUNK], ALU.mult, ALU.add),
                ins=[pb, self.B_x], outs=[self.B_x], inc=dv)


    def emit_dequant(self):
        nc, b = self.nc, self.b
        af, ab = self.af, self.ab
        dv, gp, ld, wg = self.s_dv, self.s_gp, self.s_ld, self.s_wg
        ps, psB = self.ps, self.psB
        af.mark()
        ab.mark()
        D = {}
        seed = self.phase_seed()

        def abuf(n, name, ar=None):
            D[name] = (ar or ab).alloc(n, name)
            bb = Buf(name)
            bb.freed = list(seed)
            D["B_" + name] = bb
            return D[name]

        wt = abuf(1408, "wt")
        wt32 = wt.bitcast(I32)
        nn = abuf(4096, "nn").bitcast(I32)
        s31 = abuf(4096, "s31").bitcast(I32)
        uu = abuf(4096, "uu").bitcast(I32)
        tt = abuf(4096, "tt").bitcast(I32)
        val = abuf(5632, "val")
        tstg = abuf(5632, "tstg")
        self.scf16 = nc.alloc_sbuf_tensor("scf16", [128, 64], F16)
        D["B_sc"] = Buf("sc")
        MININT = -2147483648

        def deq_chunk(Wc, wsrc, scsrc, vdst, Bwt, Bsc, Bval):
            for k in range(8):
                b.op("vector", lambda e, c, k=k: e.tensor_scalar(
                    nn[:, k * Wc:(k + 1) * Wc], wsrc, 4 * k, 7,
                    ALU.logical_shift_right, ALU.bitwise_and),
                    ins=[Bwt], outs=[D["B_nn"]], inc=dv)
                b.op("vector", lambda e, c, k=k: e.tensor_scalar(
                    s31[:, k * Wc:(k + 1) * Wc], wsrc, 28 - 4 * k, MININT,
                    ALU.logical_shift_left, ALU.bitwise_and),
                    ins=[Bwt], outs=[D["B_s31"]], inc=dv)
            n8 = 8 * Wc
            b.op("vector", lambda e, c: e.tensor_scalar(
                uu[:, 0:n8], nn[:, 0:n8], 2, 250, ALU.mult, ALU.add),
                ins=[D["B_nn"]], outs=[D["B_uu"]], inc=dv)
            b.op("vector", lambda e, c: e.tensor_scalar(
                tt[:, 0:n8], nn[:, 0:n8], 252, None, ALU.add),
                ins=[D["B_nn"]], outs=[D["B_tt"]], inc=dv)
            b.op("vector", lambda e, c: e.tensor_tensor(
                uu[:, 0:n8], uu[:, 0:n8], tt[:, 0:n8], ALU.min),
                ins=[D["B_uu"], D["B_tt"]], outs=[D["B_uu"]], inc=dv)
            b.op("vector", lambda e, c: e.tensor_scalar(
                tt[:, 0:n8], nn[:, 0:n8], 505, None, ALU.mult),
                ins=[D["B_nn"]], outs=[D["B_tt"]], inc=dv)
            b.op("vector", lambda e, c: e.tensor_tensor(
                uu[:, 0:n8], uu[:, 0:n8], tt[:, 0:n8], ALU.min),
                ins=[D["B_uu"], D["B_tt"]], outs=[D["B_uu"]], inc=dv)
            b.op("vector", lambda e, c: e.tensor_single_scalar(
                uu[:, 0:n8], uu[:, 0:n8], 22, ALU.logical_shift_left),
                ins=[D["B_uu"]], outs=[D["B_uu"]], inc=dv)
            b.op("vector", lambda e, c: e.tensor_tensor(
                uu[:, 0:n8], uu[:, 0:n8], s31[:, 0:n8], ALU.bitwise_or),
                ins=[D["B_uu"], D["B_s31"]], outs=[D["B_uu"]], inc=dv)
            ngrp = (Wc + 15) // 16
            scb = scsrc.unsqueeze(1).unsqueeze(3).broadcast_to(
                [128, 8, ngrp, 16])[:, :, :, :].rearrange(
                "p a g j -> p (a g j)") if False else None
            scv = scsrc.unsqueeze(1).unsqueeze(3)
            scv = scv.broadcast_to([128, 8, ngrp, 16])
            uv = uu[:, 0:n8].bitcast(F32).rearrange(
                "p (k g j) -> p k g j", k=8, j=16)
            vv = vdst.rearrange("p (g j k) -> p k g j", k=8, j=16)
            b.op("vector", lambda e, c, uv=uv, scv=scv, vv=vv: e.tensor_tensor(
                vv, uv, scv, ALU.mult),
                ins=[D["B_uu"], Bsc], outs=[Bval], inc=dv)

        for it in range(11):
            b.op("sync", lambda e, c, it=it: e.dma_start(
                out=wt32[:, 0:256],
                in_=self.i_gup[it * 128:(it + 1) * 128, :]),
                ins=[], outs=[D["B_wt"]], inc=ld, amt=16)
            b.op("sync", lambda e, c, it=it: e.dma_start(
                out=self.scf16.ap()[:, 0:16],
                in_=self.i_gus[it * 128:(it + 1) * 128, :]),
                ins=[], outs=[D["B_sc"]], inc=ld, amt=16)
            deq_chunk(256, wt32[:, 0:256], self.scf16.ap()[:, 0:16],
                      val[:, 0:2048], D["B_wt"], D["B_sc"], D["B_val"])
            for k in range(16):
                self.tposeb(tstg[:, k * 128:(k + 1) * 128], D["B_tstg"],
                            val[:, k * 128:(k + 1) * 128], [D["B_val"]],
                            bank=4 + k % 2, r=128, c=128)
            b.op("sync", lambda e, c, it=it: e.dma_start(
                out=self.d_guin.rearrange("(k p) q -> p k q", p=128)
                [:, :, it * 128:(it + 1) * 128],
                in_=tstg[:, 0:2048].rearrange("p (k q) -> p k q", q=128)),
                ins=[D["B_tstg"]], outs=[Buf("guin")], inc=ld, amt=16)
        n_guin = self.s_ld.n
        for ht in range(2):
            b.op("sync", lambda e, c, ht=ht: e.dma_start(
                out=wt32[:, 0:704],
                in_=self.i_dnp[ht * 128:(ht + 1) * 128, :]),
                ins=[], outs=[D["B_wt"]], inc=ld, amt=16)
            b.op("sync", lambda e, c, ht=ht: e.dma_start(
                out=self.scf16.ap()[:, 0:44],
                in_=self.i_dns[ht * 128:(ht + 1) * 128, :]),
                ins=[], outs=[D["B_sc"]], inc=ld, amt=16)
            for cc in range(3):
                Wc = (256, 256, 192)[cc]
                deq_chunk(Wc, wt32[:, cc * 256:cc * 256 + Wc],
                          self.scf16.ap()[:, cc * 16:cc * 16 + (Wc + 15) // 16],
                          val[:, cc * 2048:cc * 2048 + 8 * Wc],
                          D["B_wt"], D["B_sc"], D["B_val"])
            for k in range(44):
                self.tposeb(tstg[:, k * 128:(k + 1) * 128], D["B_tstg"],
                            val[:, k * 128:(k + 1) * 128], [D["B_val"]],
                            bank=4 + k % 2, r=128, c=128)
            b.op("sync", lambda e, c, ht=ht: e.dma_start(
                out=self.d_dnin.rearrange("(k p) q -> p k q", p=128)
                [:, :, ht * 128:(ht + 1) * 128],
                in_=tstg[:, 0:5632].rearrange("p (k q) -> p k q", q=128)),
                ins=[D["B_tstg"]], outs=[Buf("dnin")], inc=ld, amt=16)
        n_dnin = self.s_ld.n
        self.Bd_guout = Buf("d_guout")
        self.Bd_dnout = Buf("d_dnout")
        b.raw("gpsimd", lambda e, c: e.collective_compute(
            "AllGather", ALU.bypass, replica_groups=[list(range(NCORE))],
            ins=[self.d_guin.opt()], outs=[self.d_guout.opt()]),
            waits=[(self.s_ld, n_guin)], inc=self.s_aggu)
        self.Bd_guout.ready = [(self.s_aggu, 1)]
        b.raw("gpsimd", lambda e, c: e.collective_compute(
            "AllGather", ALU.bypass, replica_groups=[list(range(NCORE))],
            ins=[self.d_dnin.opt()], outs=[self.d_dnout.opt()]),
            waits=[(self.s_ld, n_dnin)], inc=self.s_agdn)
        self.Bd_dnout.ready = [(self.s_agdn, 1)]
        af.release()
        ab.release()

    def emit_ffn(self, use):
        nc, b = self.nc, self.b
        af, ab = self.af, self.ab
        dv, ac, ld, wg = self.s_dv, self.s_ac, self.s_ld, self.s_wg
        ps, psB = self.ps, self.psB
        af.mark()
        ab.mark()
        F = {}
        seed = self.phase_seed()

        def abuf(n, name):
            F[name] = ab.alloc(n, name)
            bb = Buf(name)
            bb.freed = list(seed)
            F["B_" + name] = bb
            return F[name]

        xlnT = abuf(16 * BLK, "xlnTf")
        abuf(1024, "lnstgf")
        hbuf = abuf(44 * BLK, "hbuf")
        def mkbuf(name):
            bb = Buf(name)
            bb.freed = list(seed)
            return bb

        F["wgu"] = [ab.alloc(16 * 128, f"wgu{i}") for i in range(4)]
        F["B_wgu"] = [mkbuf(f"wgu{i}") for i in range(4)]
        F["wdn"] = [ab.alloc(22 * 128, f"wdn{i}") for i in range(2)]
        F["B_wdn"] = [mkbuf(f"wdn{i}") for i in range(2)]
        hstg = abuf(BLK, "hstg")
        ysb = F["xlnTf"]
        B_ysb = F["B_xlnTf"]
        ysq = abuf(BLK, "ysqf")
        xr3 = self.xres.rearrange("p (f t) -> p f t", t=TPC)
        xlnT3 = xlnT.rearrange("p (f t) -> p f t", t=BLK)
        hb3 = hbuf.rearrange("p (f t) -> p f t", t=BLK)
        ysb3 = ysb.rearrange("p (f t) -> p f t", t=BLK)

        for run in range(2):
            tc0 = run * BLK
            self.emit_ln(
                [(xlnT3[:, ft, :], F["B_xlnTf"]) for ft in range(16)],
                [(xr3[:, ft, tc0:tc0 + BLK], self.B_x) for ft in range(16)],
                BLK, None, F["lnstgf"], F["B_lnstgf"])
            for gi in range(44):
                bufs = []
                for half, grow in ((0, gi * 128), (1, FI + gi * 128)):
                    wbi = (gi % 2) * 2 + half
                    wb = F["wgu"][wbi]
                    Bwb = F["B_wgu"][wbi]
                    r = grow // GU_SH
                    lc = grow - r * GU_SH
                    b.op("gpsimd", lambda e, c, wb=wb, r=r, lc=lc: e.dma_start(
                        out=wb.rearrange("p (k q) -> p k q", k=16),
                        in_=self.d_guout[r * DM:(r + 1) * DM, lc:lc + 128]
                        .rearrange("(k p) q -> p k q", p=128)),
                        ins=[self.Bd_guout], outs=[Bwb], inc=wg, amt=16)
                    bufs.append((wb, Bwb))
                pg, pu = psB[gi % 2], psB[2 + gi % 2]
                vg, vu = ps[gi % 2], ps[2 + gi % 2]
                for half, (wb, Bwb) in enumerate(bufs):
                    pv = (vg, vu)[half]
                    pb = (pg, pu)[half]
                    wb3 = wb.rearrange("p (k q) -> p k q", k=16)
                    for k in range(16):
                        self.mm(pv[:, 0:BLK], wb3[:, k, :], xlnT3[:, k, :],
                                pb, [Bwb, F["B_xlnTf"]], start=(k == 0),
                                stop=(k == 15), first=(k == 0))
                b.op("scalar", lambda e, c, vg=vg: e.activation(
                    hstg, vg[:, 0:BLK], AF.Silu),
                    ins=[pg], outs=[F["B_hstg"]], inc=ac)
                b.op("vector", lambda e, c, gi=gi, vu=vu: e.tensor_mul(
                    hb3[:, gi, :], hstg, vu[:, 0:BLK]),
                    ins=[F["B_hstg"], pu], outs=[F["B_hbuf"]], inc=dv)
            for m2 in range(16):
                r = m2 // 2
                lc = (m2 % 2) * 128
                for kh in range(2):
                    wb = F["wdn"][kh]
                    Bwb = F["B_wdn"][kh]
                    b.op("gpsimd", lambda e, c, wb=wb, r=r, lc=lc, kh=kh:
                         e.dma_start(
                             out=wb.rearrange("p (k q) -> p k q", q=128),
                             in_=self.d_dnout[r * FI + kh * 22 * 128:
                                              r * FI + (kh + 1) * 22 * 128,
                                              lc:lc + 128].rearrange(
                                 "(k p) q -> p k q", p=128)),
                         ins=[self.Bd_dnout], outs=[Bwb], inc=wg, amt=16)
                pd = psB[6 + m2 % 2]
                vd = ps[6 + m2 % 2]
                for k2 in range(44):
                    wb = F["wdn"][k2 // 22]
                    Bwb = F["B_wdn"][k2 // 22]
                    wb3 = wb.rearrange("p (k q) -> p k q", q=128)
                    self.mm(vd[:, 0:BLK], wb3[:, k2 % 22, :], hb3[:, k2, :],
                            pd, [Bwb, F["B_hbuf"]], start=(k2 == 0),
                            stop=(k2 == 43), first=(k2 == 0))
                b.op("vector", lambda e, c, m2=m2, vd=vd: e.tensor_copy(
                    ysb3[:, m2, :], vd[:, 0:BLK]),
                    ins=[pd], outs=[B_ysb], inc=dv)
            for ft in range(16):
                b.op("scalar", lambda e, c, ft=ft: e.activation(
                    ysq, ysb3[:, ft, :], AF.Square),
                    ins=[B_ysb], outs=[F["B_ysqf"]], inc=ac)
                self.mm(ps[4][0:1, 0:BLK], self.onesb[:, 0:1], ysq,
                        psB[4], [F["B_ysqf"], self.B_c], start=(ft == 0),
                        stop=(ft == 15), first=(ft == 0))
            rr = self.ln_rows[0:1, 0:BLK]
            Brr = self.B_lnrows
            b.op("vector", lambda e, c: e.tensor_scalar_mul(
                rr, ps[4][0:1, 0:BLK], 1.0 / DM),
                ins=[psB[4]], outs=[Brr], inc=dv)
            b.op("scalar", lambda e, c: e.activation(
                rr, rr, AF.Sqrt, bias=self.cc[0:1, 2:3]),
                ins=[Brr, self.B_c], outs=[Brr], inc=ac)
            b.op("vector", lambda e, c: e.reciprocal(rr, rr),
                 ins=[Brr], outs=[Brr], inc=dv)
            self.mm(ps[5][:, 0:BLK], self.onesf[0:1, :], rr, psB[5],
                    [Brr, self.B_c], start=True, stop=True, first=True)
            for ft in range(16):
                b.op("vector", lambda e, c, ft=ft: e.tensor_mul(
                    ysq, ysb3[:, ft, :], ps[5][:, 0:BLK]),
                    ins=[B_ysb, psB[5]], outs=[F["B_ysqf"]], inc=dv)
                b.op("vector", lambda e, c, ft=ft, tc0=tc0: e.tensor_add(
                    xr3[:, ft, tc0:tc0 + BLK], xr3[:, ft, tc0:tc0 + BLK],
                    ysq),
                    ins=[F["B_ysqf"], self.B_x], outs=[self.B_x], inc=dv)
        af.release()
        ab.release()


    def emit_attn(self):
        nc, b = self.nc, self.b
        af, ab = self.af, self.ab
        dv, ac, ld, wg, gp = (self.s_dv, self.s_ac, self.s_ld, self.s_wg,
                              self.s_gp)
        ps, psB, pp = self.ps, self.psB, self.pp
        af.mark()
        ab.mark()
        A = {}
        seed = self.phase_seed()

        def abuf(n, name, ar=None):
            A[name] = (ar or ab).alloc(n, name)
            bb = Buf(name)
            bb.freed = list(seed)
            A["B_" + name] = bb
            return A[name]

        xln = abuf(16 * TPC, "xln")
        abuf(1024, "lnstga")
        qa = abuf(1024, "qa")
        qsb = abuf(1024, "qsb")
        kc = abuf(1024, "kc")

        wqa = abuf(16 * 64, "wqa")
        wqb = abuf(16 * 128, "wqb")
        A["wkva"] = [ab.alloc(16 * 128, f"wkva{i}") for i in range(2)]
        A["B_wkva"] = []
        for i in range(2):
            bb = Buf(f"wkva{i}")
            bb.freed = list(seed)
            A["B_wkva"].append(bb)
        wkvb = abuf(16 * 128, "wkvb")
        cos64 = abuf(TPC, "cos64")
        sin64 = abuf(TPC, "sin64")
        kst = abuf(TPC, "kst")
        vst = abuf(TPC, "vst")
        abuf(TPC, "vfm")
        abuf(BLK, "pacc")
        abuf(TPC, "paccS")
        abuf(TPC, "lrow")
        abuf(TPC, "oacc", af)
        qh = abuf(TPC, "qh")
        A["pb"] = [ab.alloc(BLK, f"pb{i}") for i in range(4)]
        A["B_pb"] = []
        for i in range(4):
            bb = Buf(f"pb{i}")
            bb.freed = list(seed)
            A["B_pb"].append(bb)
        rlb = abuf(TPC, "rlb")
        osb = abuf(TPC, "osb")
        kcsq = rlb
        A["kcsq"] = kcsq
        A["B_kcsq"] = A["B_rlb"]
        zrowf = abuf(64, "zrowf", af)
        zrow = zrowf.bitcast(BF16)
        A["zrow"] = zrow
        A["B_zrow"] = A["B_zrowf"]
        theta = abuf(TPC, "theta", af)
        th2 = abuf(TPC, "th2", af)
        qi32 = abuf(TPC, "qi32", af)
        invf = abuf(2, "invf", af)
        sgn = abuf(1, "sgn", af)
        poss = self.ln_rows[0:1, 1024:2048]
        A["poss"] = poss
        A["B_poss"] = self.B_lnrows
        bnd = abuf(8, "bnd", af)
        negC = abuf(1, "negC", af)
        xr3 = self.xres.rearrange("p (f t) -> p f t", t=TPC)
        xln3 = xln.rearrange("p (f t) -> p f t", t=TPC)

        b.op("gpsimd", lambda e, c: e.memset(zrow, 0.0),
             outs=[A["B_zrow"]], inc=gp)

        b.op("gpsimd", lambda e, c: e.iota(
            invf[0:HEADS, 0:1], pattern=[[1, 1]], base=0, channel_multiplier=1,
            allow_small_or_imprecise_dtypes=True),
            outs=[A["B_invf"]], inc=gp)
        b.op("vector", lambda e, c: e.tensor_scalar(
            sgn[0:HEADS, :], invf[0:HEADS, 0:1], 31.5, None, ALU.is_gt),
            ins=[A["B_invf"]], outs=[A["B_sgn"]], inc=dv)
        b.op("vector", lambda e, c: e.scalar_tensor_tensor(
            invf[0:HEADS, 1:2], sgn[0:HEADS, :], -32.0, invf[0:HEADS, 0:1],
            ALU.mult, ALU.add),
            ins=[A["B_sgn"], A["B_invf"]], outs=[A["B_invf"]], inc=dv)
        import math as _m
        b.op("scalar", lambda e, c: e.activation(
            invf[0:HEADS, 0:1], invf[0:HEADS, 1:2], AF.Exp,
            scale=-_m.log(10000.0) / 32.0),
            ins=[A["B_invf"]], outs=[A["B_invf"]], inc=ac)
        b.op("gpsimd", lambda e, c: e.iota(
            poss[0:1, 0:TPC], pattern=[[0, 2], [1, BLK]], base=0,
            channel_multiplier=0, allow_small_or_imprecise_dtypes=True),
            outs=[A["B_poss"]], inc=gp)
        b.op("vector", lambda e, c: e.tensor_scalar_mul(
            bnd[0:1, 0:1], self.p_jcol[0:1, :], 512.0),
            ins=[self.B_par], outs=[A["B_bnd"]], inc=dv)
        b.op("vector", lambda e, c: e.tensor_scalar(
            bnd[0:1, 1:2], bnd[0:1, 0:1], -1.0, 7680.0, ALU.mult, ALU.add),
            ins=[A["B_bnd"]], outs=[A["B_bnd"]], inc=dv)
        b.op("vector", lambda e, c: e.tensor_scalar(
            poss[0:1, 0:BLK], poss[0:1, 0:BLK], bnd[0:1, 0:1], None, ALU.add),
            ins=[A["B_poss"], A["B_bnd"]], outs=[A["B_poss"]], inc=dv)
        b.op("vector", lambda e, c: e.tensor_scalar(
            poss[0:1, BLK:TPC], poss[0:1, BLK:TPC], bnd[0:1, 1:2], None,
            ALU.add),
            ins=[A["B_poss"], A["B_bnd"]], outs=[A["B_poss"]], inc=dv)
        self.tposef(poss[0:1, 0:0 + HEADS] if False else
                    self.ln_rows[0:1, 0:HEADS], self.B_lnrows,
                    invf[0:HEADS, 0:1], [A["B_invf"]], bank=5, r=HEADS, c=1)
        for half in range(2):
            self.mm(ps[half][0:HEADS, 0:BLK], self.ln_rows[0:1, 0:HEADS],
                    poss[0:1, half * BLK:(half + 1) * BLK], psB[half],
                    [self.B_lnrows, A["B_poss"]], start=True, stop=True,
                    first=True)
            b.op("vector", lambda e, c, half=half: e.tensor_copy(
                theta[0:HEADS, half * BLK:(half + 1) * BLK],
                ps[half][0:HEADS, 0:BLK]),
                ins=[psB[half]], outs=[A["B_theta"]], inc=dv)

        def trig(dst, dbuf, offs):
            TWO_PI = 2.0 * _m.pi
            b.op("vector", lambda e, c: e.tensor_scalar(
                th2[0:HEADS, :], theta[0:HEADS, :], 1.0 / TWO_PI,
                0.5 + offs / TWO_PI, ALU.mult, ALU.add),
                ins=[A["B_theta"]], outs=[A["B_th2"]], inc=dv)
            b.op("vector", lambda e, c: e.tensor_scalar(
                qi32[0:HEADS, :].bitcast(I32) if False else
                qi32[0:HEADS, :], th2[0:HEADS, :], 0.0, None, ALU.add),
                ins=[A["B_th2"]], outs=[A["B_qi32"]], inc=dv)
            b.op("vector", lambda e, c: e.tensor_copy(
                th2[0:HEADS, :].bitcast(I32), qi32[0:HEADS, :]),
                ins=[A["B_qi32"]], outs=[A["B_th2"]], inc=dv)
            b.op("vector", lambda e, c: e.scalar_tensor_tensor(
                th2[0:HEADS, :], th2[0:HEADS, :].bitcast(I32), -TWO_PI,
                theta[0:HEADS, :], ALU.mult, ALU.add),
                ins=[A["B_th2"], A["B_theta"]], outs=[A["B_th2"]], inc=dv)
            if offs != 0.0:
                b.op("vector", lambda e, c: e.tensor_scalar_add(
                    th2[0:HEADS, :], th2[0:HEADS, :], offs),
                    ins=[A["B_th2"]], outs=[A["B_th2"]], inc=dv)
            b.op("vector", lambda e, c: e.tensor_scalar(
                th2[0:HEADS, :], th2[0:HEADS, :], _m.pi, -_m.pi,
                ALU.min, ALU.max),
                ins=[A["B_th2"]], outs=[A["B_th2"]], inc=dv)
            b.op("scalar", lambda e, c, dst=dst: e.activation(
                dst[0:HEADS, :], th2[0:HEADS, :], AF.Sin),
                ins=[A["B_th2"]], outs=[dbuf], inc=ac)

        trig(sin64, A["B_sin64"], 0.0)
        trig(cos64, A["B_cos64"], _m.pi / 2.0)

        self.emit_ln(
            [(xln3[:, ft, 0:BLK], A["B_xln"]) for ft in range(16)],
            [(xr3[:, ft, 0:BLK], self.B_x) for ft in range(16)],
            BLK, (self.grows[0:1, 0:DM], self.B_rows), A["lnstga"],
            A["B_lnstga"])
        self.emit_ln(
            [(xln3[:, ft, BLK:TPC], A["B_xln"]) for ft in range(16)],
            [(xr3[:, ft, BLK:TPC], self.B_x) for ft in range(16)],
            BLK, (self.grows[0:1, 0:DM], self.B_rows), A["lnstga"],
            A["B_lnstga"])
        b.op("gpsimd", lambda e, c: e.dma_start(
            out=wqa.rearrange("p (k q) -> p k q", k=16),
            in_=self.i_wqa.rearrange("(k p) q -> p k q", p=128)),
            outs=[A["B_wqa"]], inc=wg, amt=16)
        wqa3 = wqa.rearrange("p (k q) -> p k q", k=16)
        for half in range(2):
            pb_ = psB[half]
            pv = ps[half]
            for k in range(16):
                self.mm(pv[0:QR, 0:BLK], wqa3[:, k, :],
                        xln3[:, k, half * BLK:(half + 1) * BLK], pb_,
                        [A["B_wqa"], A["B_xln"]], start=(k == 0),
                        stop=(k == 15), first=(k == 0))
            b.op("vector", lambda e, c, half=half, pv=pv: e.tensor_copy(
                qa[0:QR, half * BLK:(half + 1) * BLK], pv[0:QR, 0:BLK]),
                ins=[pb_], outs=[A["B_qa"]], inc=dv)
        b.op("gpsimd", lambda e, c: e.dma_start(
            out=wqb[0:QR, :].rearrange("p (k q) -> p k q", k=16),
            in_=self.i_wqb.rearrange("a (k q) -> a k q", q=128)),
            outs=[A["B_wqb"]], inc=wg, amt=16)
        wqb3 = wqb[0:QR, :].rearrange("p (k q) -> p k q", k=16)
        maxq = bnd[0:1, 2:3]

        sgnc = bnd[0:HEADS, 4:5]
        b.op("vector", lambda e, c: e.tensor_scalar(
            sgnc, sgn[0:HEADS, :], 2.0, -1.0, ALU.mult, ALU.add),
            ins=[A["B_sgn"]], outs=[A["B_bnd"]], inc=dv)
        b.op("vector", lambda e, c: e.tensor_scalar(
            sin64[0:HEADS, :], sin64[0:HEADS, :], sgnc, None, ALU.mult),
            ins=[A["B_sin64"], A["B_bnd"]], outs=[A["B_sin64"]], inc=dv)

        rotb = th2
        Brotb = A["B_th2"]

        def rope_evac(dst, dbuf, srcv, srcb):
            for half in range(2):
                sl = slice(half * BLK, (half + 1) * BLK)
                sv, sb_ = srcv[half], srcb[half]
                shmask = (list(range(8, 16)) + list(range(0, 8))
                          + list(range(16, 32)))
                b.op("vector", lambda e, c, sv=sv, shmask=shmask:
                     e.stream_shuffle(
                         rotb[0:HEADS, 0:BLK], sv[0:HEADS, 0:BLK], shmask),
                     ins=[sb_], outs=[Brotb], inc=dv)
                b.op("vector", lambda e, c, sl=sl: e.tensor_mul(
                    rotb[0:HEADS, 0:BLK], rotb[0:HEADS, 0:BLK],
                    sin64[0:HEADS, sl]),
                    ins=[Brotb, A["B_sin64"]], outs=[Brotb], inc=dv)
                b.op("vector", lambda e, c, sv=sv, sl=sl, dst=dst:
                     e.tensor_mul(dst[0:HEADS, sl], sv[0:HEADS, 0:BLK],
                                  cos64[0:HEADS, sl]),
                     ins=[sb_, A["B_cos64"]], outs=[dbuf], inc=dv)
                b.op("vector", lambda e, c, sl=sl, dst=dst: e.tensor_add(
                    dst[0:HEADS, sl], dst[0:HEADS, sl], rotb[0:HEADS, 0:BLK]),
                    ins=[Brotb, dbuf], outs=[dbuf], inc=dv)
                b.op("vector", lambda e, c, sv=sv, sl=sl, dst=dst:
                     e.tensor_copy(dst[HEADS:128, sl], sv[HEADS:128, 0:BLK]),
                     ins=[sb_], outs=[dbuf], inc=dv)

        for h in range(NH):
            pbs = [psB[2], psB[3]]
            pvs = [ps[2], ps[3]]
            for half in range(2):
                self.mm(pvs[half][:, 0:BLK], wqb3[:, h, :],
                        qa[0:QR, half * BLK:(half + 1) * BLK], pbs[half],
                        [A["B_wqb"], A["B_qa"]], start=True, stop=True,
                        first=True)
            rope_evac(qsb, A["B_qsb"], pvs, pbs)
            b.op("scalar", lambda e, c: e.activation(
                kcsq, qsb, AF.Square), ins=[A["B_qsb"]],
                outs=[A["B_kcsq"]], inc=ac)
            for half in range(2):
                self.mm(ps[4 + half][0:1, 0:BLK], self.onesb[:, 0:1],
                        kcsq[:, half * BLK:(half + 1) * BLK], psB[4 + half],
                        [A["B_kcsq"], self.B_c], start=True, stop=True,
                        first=True)
                b.op("vector", lambda e, c, half=half: e.reduce_max(
                    bnd[0:1, 3:4], ps[4 + half][0:1, 0:BLK], AXX),
                    ins=[psB[4 + half]], outs=[A["B_bnd"]], inc=dv)
                if h == 0 and half == 0:
                    b.op("vector", lambda e, c: e.tensor_copy(
                        maxq, bnd[0:1, 3:4]),
                        ins=[A["B_bnd"]], outs=[A["B_bnd"]], inc=dv)
                else:
                    b.op("vector", lambda e, c: e.tensor_tensor(
                        maxq, maxq, bnd[0:1, 3:4], ALU.max),
                        ins=[A["B_bnd"]], outs=[A["B_bnd"]], inc=dv)
            b.op("sync", lambda e, c, h=h: e.dma_start(
                out=self.d_q[h * DH:(h + 1) * DH, :], in_=qsb),
                ins=[A["B_qsb"]], outs=[self.Bd_q], inc=ld, amt=16)

        self.emit_ln(
            [(xln3[:, ft, 0:BLK], A["B_xln"]) for ft in range(16)],
            [(xr3[:, ft, 0:BLK], self.B_x) for ft in range(16)],
            BLK, (self.grows[0:1, DM:2 * DM], self.B_rows), A["lnstga"],
            A["B_lnstga"])
        self.emit_ln(
            [(xln3[:, ft, BLK:TPC], A["B_xln"]) for ft in range(16)],
            [(xr3[:, ft, BLK:TPC], self.B_x) for ft in range(16)],
            BLK, (self.grows[0:1, DM:2 * DM], self.B_rows), A["lnstga"],
            A["B_lnstga"])
        vsb = qsb
        B_vsb = A["B_qsb"]
        vflat = [self.d_kvin[r][EMB:2 * EMB, :].flatten()
                 .rearrange("(t d) -> t d", d=EMB) for r in range(2)]
        Bd_kvin = [Buf("kvin0"), Buf("kvin1")]
        b.op("gpsimd", lambda e, c: e.dma_start(
            out=wkvb.rearrange("p (k q) -> p k q", k=16),
            in_=self.i_wkvb.rearrange("a (k q) -> a k q", q=128)),
            outs=[A["B_wkvb"]], inc=wg, amt=16)
        for h2 in range(2):
            b.op("scalar", lambda e, c, h2=h2: e.activation(
                kcsq[:, 0:TPC], wkvb[0:KR, h2 * TPC:(h2 + 1) * TPC],
                AF.Square),
                ins=[A["B_wkvb"]], outs=[A["B_kcsq"]], inc=ac)
            for q4 in range(2):
                self.mm(ps[5][0:1, 0:BLK], self.onesb[0:KR, 0:1],
                        kcsq[0:KR, q4 * 512:(q4 + 1) * 512], psB[5],
                        [A["B_kcsq"], self.B_c], start=(h2 == 0 and q4 == 0),
                        stop=(h2 == 1 and q4 == 1),
                        first=(h2 == 0 and q4 == 0))
        b.op("vector", lambda e, c: e.reduce_sum(
            bnd[0:1, 5:6], ps[5][0:1, 0:BLK], AXX),
            ins=[psB[5]], outs=[A["B_bnd"]], inc=dv)
        for m in range(17):
            wb = A["wkva"][m % 2]
            Bwb = A["B_wkva"][m % 2]
            b.op("gpsimd", lambda e, c, m=m, wb=wb: e.dma_start(
                out=wb.rearrange("p (k q) -> p k q", k=16),
                in_=self.i_wkva[:, m * 128:(m + 1) * 128].rearrange(
                    "(k p) q -> p k q", p=128)),
                outs=[Bwb], inc=wg, amt=16)
            vstg = A["pb"][0]
            B_vstg = A["B_pb"][0]
            wb3 = wb.rearrange("p (k q) -> p k q", k=16)
            for half in range(2):
                pb_ = psB[half]
                pv = ps[half]
                for k in range(16):
                    self.mm(pv[:, 0:BLK], wb3[:, k, :],
                            xln3[:, k, half * BLK:(half + 1) * BLK], pb_,
                            [Bwb, A["B_xln"]], start=(k == 0),
                            stop=(k == 15), first=(k == 0))
                if m == 0:
                    b.op("vector", lambda e, c, half=half, pv=pv:
                         e.tensor_copy(
                             kc[:, half * BLK:(half + 1) * BLK],
                             pv[:, 0:BLK]),
                         ins=[pb_], outs=[A["B_kc"]], inc=dv)
                else:
                    hh = m - 1
                    b.op("vector", lambda e, c, pv=pv:
                         e.tensor_copy(vsb[:, 0:BLK], pv[:, 0:BLK]),
                         ins=[pb_], outs=[B_vsb], inc=dv)
                    b.op("sync", lambda e, c, half=half, hh=hh:
                         e.dma_start(
                             out=self.d_kvin[half][EMB + hh * DH:
                                                   EMB + (hh + 1) * DH, :],
                             in_=vsb[:, 0:BLK]),
                         ins=[B_vsb], outs=[Bd_kvin[half]], inc=ld,
                         amt=16)
        b.op("scalar", lambda e, c: e.activation(
            kcsq, kc, AF.Square), ins=[A["B_kc"]], outs=[A["B_kcsq"]], inc=ac)
        for half in range(2):
            self.mm(ps[4][0:1, 0:BLK], self.onesb[:, 0:1],
                    kcsq[:, half * BLK:(half + 1) * BLK], psB[4],
                    [A["B_kcsq"], self.B_c], start=True, stop=True, first=True)
            b.op("vector", lambda e, c: e.reduce_max(
                bnd[0:1, 3:4], ps[4][0:1, 0:BLK], AXX),
                ins=[psB[4]], outs=[A["B_bnd"]], inc=dv)
            b.op("vector", lambda e, c: e.tensor_tensor(
                bnd[0:1, 3:4], bnd[0:1, 3:4], bnd[0:1, 5:6], ALU.mult),
                ins=[A["B_bnd"]], outs=[A["B_bnd"]], inc=dv)
            b.op("vector", lambda e, c: e.tensor_scalar_mul(
                bnd[0:1, 3:4], bnd[0:1, 3:4], 1.02),
                ins=[A["B_bnd"]], outs=[A["B_bnd"]], inc=dv)
            b.op("vector", lambda e, c: e.memset(
                self.ln_rows[0:1, 0:BLK], 0.0),
                ins=[], outs=[self.B_lnrows], inc=dv)
            b.op("vector", lambda e, c: e.tensor_copy(
                self.ln_rows[0:1, 0:1], bnd[0:1, 3:4]),
                ins=[A["B_bnd"]], outs=[self.B_lnrows], inc=dv)
            self.odma("gpsimd", lambda e, c, half=half: e.dma_start(
                out=self.d_kvin[half][4096:4097, :],
                in_=self.ln_rows[0:1, 0:BLK]),
                ins=[self.B_lnrows], outs=[Bd_kvin[half]])
        wkvb3 = wkvb.rearrange("p (k q) -> p k q", k=16)
        for h in range(NH):
            pbs = [psB[2], psB[3]]
            pvs = [ps[2], ps[3]]
            for half in range(2):
                self.mm(pvs[half][:, 0:BLK], wkvb3[:, h, :],
                        kc[:, half * BLK:(half + 1) * BLK], pbs[half],
                        [A["B_wkvb"], A["B_kc"]], start=True, stop=True,
                        first=True)
            rope_evac(qsb, A["B_qsb"], pvs, pbs)
            for half in range(2):
                b.op("sync", lambda e, c, h=h, half=half: e.dma_start(
                    out=self.d_kvin[half][h * DH:(h + 1) * DH, :],
                    in_=qsb[:, half * BLK:(half + 1) * BLK]),
                    ins=[A["B_qsb"]], outs=[Bd_kvin[half]], inc=ld, amt=16)
        self.Bd_kvout = [Buf("kvout0"), Buf("kvout1")]
        for r in range(2):
            self.b.op("gpsimd", lambda e, c, r=r: e.collective_compute(
                "AllGather", ALU.bypass, replica_groups=[list(range(NCORE))],
                ins=[self.d_kvin[r].opt()], outs=[self.d_kvout[r].opt()]),
                ins=[Bd_kvin[r]], inc=self.s_agkv[r])
            self.Bd_kvout[r].ready = [(self.s_agkv[r], 1)]
        stg8 = self.ln_rows[0:8, 0:2]
        for r in range(2):
            self.odma("gpsimd", lambda e, c, r=r: e.dma_start(
                out=stg8[:, r:r + 1],
                in_=self.d_kvout[r].rearrange("(s q) t -> s q t", q=4097)
                [:, 4096, 0:1],
                allow_slow_non_contiguous=True),
                ins=[self.Bd_kvout[r]], outs=[self.B_lnrows])
        self.tposef(self.ln_rows[0:1, 8:16], self.B_lnrows, stg8[:, 0:1],
                    [self.B_lnrows], bank=5, r=8, c=1)
        self.tposef(self.ln_rows[0:1, 16:24], self.B_lnrows, stg8[:, 1:2],
                    [self.B_lnrows], bank=5, r=8, c=1)
        b.op("vector", lambda e, c: e.reduce_max(
            bnd[0:1, 6:7], self.ln_rows[0:1, 8:24], AXX),
            ins=[self.B_lnrows], outs=[A["B_bnd"]], inc=dv)
        b.op("vector", lambda e, c: e.tensor_tensor(
            bnd[0:1, 6:7], bnd[0:1, 6:7], maxq, ALU.mult),
            ins=[A["B_bnd"]], outs=[A["B_bnd"]], inc=dv)
        b.op("scalar", lambda e, c: e.activation(
            bnd[0:1, 6:7], bnd[0:1, 6:7], AF.Sqrt),
            ins=[A["B_bnd"]], outs=[A["B_bnd"]], inc=ac)
        b.op("vector", lambda e, c: e.tensor_scalar_mul(
            bnd[0:1, 6:7], bnd[0:1, 6:7], -1.0 / float(DH) ** 0.5),
            ins=[A["B_bnd"]], outs=[A["B_bnd"]], inc=dv)
        b.op("vector", lambda e, c: e.tensor_copy(
            self.ln_rows[0:1, 24:25], bnd[0:1, 6:7]),
            ins=[A["B_bnd"]], outs=[self.B_lnrows], inc=dv)
        self.mm(ps[5][:, 0:1], self.onesf[0:1, :], self.ln_rows[0:1, 24:25],
                psB[5], [self.B_lnrows, self.B_c], start=True, stop=True,
                first=True)
        b.op("vector", lambda e, c: e.tensor_copy(negC[:, 0:1], ps[5][:, 0:1]),
             ins=[psB[5]], outs=[A["B_negC"]], inc=dv)

        self.emit_flash(A, Bd_kvin)
        self.emit_wo(A)
        af.release()
        ab.release()

    def emit_flash(self, A, Bd_kvin):
        import bass_rust
        b = self.b
        dv, ac, ld, wg = self.s_dv, self.s_ac, self.s_ld, self.s_wg
        ps, psB, pp = self.ps, self.psB, self.pp
        kst, vst, qh = A["kst"], A["vst"], A["qh"]
        pacc = A["pacc"]
        SC = 1.0 / float(DH) ** 0.5
        ppS = [pp[0][:, 0:512], pp[0][:, 512:1024],
               pp[1][:, 0:512], pp[1][:, 512:1024]]
        pbS = [psB[0], psB[1], psB[2], psB[3]]
        ppV = pp[2]
        pbV = [Buf("ppV0"), Buf("ppV1")]
        pbV[0].freed = psB[4].ready + psB[4].freed
        pbV[1].freed = psB[5].ready + psB[5].freed
        ppT = pp[3]
        pbT = Buf("ppT")
        pbT.freed = (psB[6].ready + psB[6].freed + psB[7].ready
                     + psB[7].freed)
        zb = A["zrow"]
        B_kst = [Buf("kst0"), Buf("kst1")]
        B_vst = [Buf("vst0"), Buf("vst1")]
        B_vfm = [Buf("vfm0"), Buf("vfm1")]
        for bb in B_kst + B_vst + B_vfm:
            bb.freed = list(A["B_kst"].freed)
        B_pacc = Buf("pacc")
        B_pacc.freed = list(A["B_pacc"].freed)
        Bd_kvall = Buf("kvall")
        Bd_kvall.ready = [(self.s_agkv[0], 1), (self.s_agkv[1], 1)]
        B_astg = Buf("astg")

        for s in range(1, 8):
            eng, qsem = (("sync", self.s_rs[0]) if s % 2 else
                         ("scalar", self.s_rs[1]))

            def restage(e, c, s=s):
                cidr = e.to_reg(e.partition_id())
                t = e.alloc_register(f"rs_t{s}")
                rk = e.alloc_register(f"rs_k{s}")
                e.reg_alu(t, cidr, 17 - 2 * s, ALU.add)
                e.reg_alu(rk, cidr, s, ALU.is_lt)
                e.reg_alu(t, t, rk, ALU.mult)
                e.reg_alu(rk, t, s - 1, ALU.add)
                e.reg_alu(rk, rk, 4097, ALU.mult)
                e.free_register(t)
                rv = bass_rust.RuntimeValue(rk, min_val=0,
                                            max_val=15 * 4097)
                return e.dma_start(
                    out=self.d_astg[(s - 1) * 4096:s * 4096, :],
                    in_=self.d_kvall[bass.ds(rv, 4096), :])
            b.op(eng, restage, ins=[Bd_kvall], outs=[B_astg], inc=qsem,
                 amt=16, relax=True)
        B_astg.ready = [(self.s_rs[0], self.s_rs[0].n),
                        (self.s_rs[1], self.s_rs[1].n)]

        def qsrv(e, c, s):
            if "fl_qs" not in c._c:
                cidr = e.to_reg(e.partition_id())
                t = e.alloc_register("fl_t")
                rvs = {}
                for s2 in range(1, 8):
                    rq = e.alloc_register(f"fl_qs{s2}")
                    e.reg_alu(t, cidr, s2, ALU.is_lt)
                    e.reg_alu(rq, t, 512, ALU.mult)
                    rvs[s2] = bass_rust.RuntimeValue(
                        rq, min_val=0, max_val=512)
                e.free_register(t)
                c._c["fl_qs"] = rvs
            return c._c["fl_qs"][s]

        def qcol(e, c, sl, t, row1=False):
            pfx = t[0:1, :] if row1 else t[:, :]
            if sl == 0:
                return pfx[:, 0:512]
            if sl >= 8:
                return pfx[:, 512:1024]
            return pfx[:, bass.ds(qsrv(e, c, sl), 512)]

        for h in range(NH):
            b.op("sync", lambda e, c, h=h: e.dma_start(
                out=qh, in_=self.d_q[h * DH:(h + 1) * DH, :]),
                ins=[self.Bd_q], outs=[A["B_qh"]], inc=self.s_ld, amt=16)
            b.op("vector", lambda e, c: e.memset(A["paccS"], 0.0),
                 ins=[], outs=[A["B_paccS"]], inc=dv)
            b.op("vector", lambda e, c: e.memset(A["oacc"], 0.0),
                 ins=[], outs=[A["B_oacc"]], inc=dv)

            def stage_dma(sl, h):
                par = sl % 2
                ko = kst[:, par * 512:(par + 1) * 512]
                vf = A["vfm"][:, par * 512:(par + 1) * 512]
                if sl == 0 or sl == 8:
                    r = 0 if sl == 0 else 1
                    ksrc = self.d_kvin[r][h * DH:(h + 1) * DH, :]
                    vsrc = self.d_kvin[r][EMB + h * DH:
                                          EMB + (h + 1) * DH, :]
                    kdep = vdep = Bd_kvin[r]
                elif sl <= 7:
                    base = (sl - 1) * 4096
                    ksrc = self.d_astg[base + h * DH:
                                       base + (h + 1) * DH, :]
                    vsrc = self.d_astg[base + EMB + h * DH:
                                       base + EMB + (h + 1) * DH, :]
                    kdep = vdep = B_astg
                else:
                    seg = sl - 9
                    base = seg * 4097
                    ksrc = self.d_kvall[base + h * DH:
                                        base + (h + 1) * DH, :]
                    vsrc = self.d_kvall[base + EMB + h * DH:
                                        base + EMB + (h + 1) * DH, :]
                    kdep = vdep = Bd_kvall
                b.op("gpsimd", lambda e, c, ko=ko, ksrc=ksrc: e.dma_start(
                    out=ko, in_=ksrc),
                    ins=[kdep], outs=[B_kst[par]], inc=self.s_kq[par],
                    amt=16, relax=True)
                b.op("gpsimd", lambda e, c, vf=vf, vsrc=vsrc: e.dma_start(
                    out=vf, in_=vsrc),
                    ins=[vdep], outs=[B_vfm[par]], inc=self.s_vq[par],
                    amt=16, relax=True)
                n0 = [self.s_vt[0].n, self.s_vt[1].n]
                for w in range(4):
                    teng = "sync" if w < 2 else "scalar"
                    ti = 0 if w < 2 else 1
                    b.op(teng, lambda e, c, vf=vf, par=par, w=w:
                         e.dma_start_transpose(
                             out=vst[:, par * 512 + w * 128:
                                     par * 512 + (w + 1) * 128],
                             in_=vf[:, w * 128:(w + 1) * 128]),
                         ins=[B_vfm[par]], outs=[B_vst[par]],
                         inc=self.s_vt[ti], amt=16, relax=n0[ti])
                B_vst[par].ready = [(self.s_vt[0], self.s_vt[0].n),
                                    (self.s_vt[1], self.s_vt[1].n)]

            stage_dma(0, h)
            stage_dma(1, h)
            for sl in range(17):
                par = sl % 2
                diag = sl in (0, 8)
                for w in range(4):
                    pS = ppS[w]
                    pbSw = pbS[w]
                    pbuf = A["pb"][w]
                    pbufB = A["B_pb"][w]
                    wl = (list(pbSw.ready) + list(pbSw.freed)
                          + B_kst[par].ready + A["B_qh"].ready)
                    n = b.raw("tensor", lambda e, c, pS=pS, w=w, par=par,
                              sl=sl: e.matmul(
                                  pS,
                                  kst[:, par * 512 + w * 128:
                                      par * 512 + (w + 1) * 128],
                                  qcol(e, c, sl, qh), start=True,
                                  stop=True),
                              waits=wl, inc=self.s_pe)
                    pbSw.ready = [(self.s_pe, n)]
                    pbSw.freed = []
                    B_kst[par].freed.append((self.s_pe, n))
                    b.op("scalar", lambda e, c, pS=pS, pbuf=pbuf:
                         e.activation(pbuf, pS, AF.Exp, scale=SC,
                                      bias=A["negC"][:, 0:1]),
                         ins=[pbSw, A["B_negC"]], outs=[pbufB],
                         inc=self.s_ac)
                    if diag:
                        b.op("vector", lambda e, c, pbuf=pbuf, w=w:
                             e.tensor_mul(
                                 pbuf, pbuf,
                                 self.fmask[:, w * BLK:(w + 1) * BLK]),
                             ins=[pbufB, self.B_c], outs=[pbufB],
                             inc=self.s_dv)
                    if w == 0:
                        b.op("vector", lambda e, c, pbuf=pbuf:
                             e.tensor_copy(pacc, pbuf),
                             ins=[pbufB], outs=[B_pacc], inc=dv)
                    else:
                        b.op("vector", lambda e, c, pbuf=pbuf:
                             e.tensor_add(pacc, pacc, pbuf),
                             ins=[pbufB, B_pacc], outs=[B_pacc], inc=dv)
                pv = ppV[:, par * 512:(par + 1) * 512]
                for w in range(4):
                    pbuf = A["pb"][w]
                    pbufB = A["B_pb"][w]
                    wl = (list(pbV[par].ready)
                          + (list(pbV[par].freed) if w == 0 else [])
                          + pbufB.ready + B_vst[par].ready)
                    n = b.raw("tensor", lambda e, c, pbuf=pbuf, w=w,
                              par=par, pv=pv: e.matmul(
                                  pv,
                                  vst[:, par * 512 + w * 128:
                                      par * 512 + (w + 1) * 128],
                                  pbuf, start=(w == 0), stop=(w == 3)),
                              waits=wl, inc=self.s_pe)
                    pbV[par].ready = [(self.s_pe, n)]
                    if w == 0:
                        pbV[par].freed = []
                    pbufB.freed.append((self.s_pe, n))
                    B_vst[par].freed.append((self.s_pe, n))
                b.op("gpsimd", lambda e, c, sl=sl: e.tensor_add(
                    qcol(e, c, sl, A["paccS"]),
                    qcol(e, c, sl, A["paccS"]), pacc),
                    ins=[B_pacc, A["B_paccS"]], outs=[A["B_paccS"]],
                    inc=self.s_gp)
                b.op("vector", lambda e, c, sl=sl, pv=pv:
                     e.scalar_tensor_tensor(
                         qcol(e, c, sl, A["oacc"]), pv, 1.0,
                         qcol(e, c, sl, A["oacc"]), ALU.mult, ALU.add),
                     ins=[pbV[par], A["B_oacc"]], outs=[A["B_oacc"]],
                     inc=dv)
                if sl + 2 <= 16:
                    stage_dma(sl + 2, h)
            for side in range(2):
                wl = (list(pbT.ready) + (list(pbT.freed) if side == 0
                                         else [])
                      + A["B_paccS"].ready + self.B_c.ready)
                n = b.raw("tensor", lambda e, c, side=side:
                          e.matmul(
                              ppT[0:1, side * 512:(side + 1) * 512],
                              self.onesb[:, 0:1],
                              A["paccS"][:, side * 512:(side + 1) * 512],
                              start=True, stop=True), waits=wl,
                          inc=self.s_pe)
                pbT.ready = [(self.s_pe, n)]
                if side == 0:
                    pbT.freed = []
                A["B_paccS"].freed.append((self.s_pe, n))
            rl = self.ln_rows[0:1, 0:TPC]
            b.op("vector", lambda e, c: e.reciprocal(
                rl, ppT[0:1, :]),
                ins=[pbT], outs=[self.B_lnrows], inc=dv)
            b.op("vector", lambda e, c: e.tensor_copy(
                A["lrow"][0:1, :], rl),
                ins=[self.B_lnrows], outs=[A["B_lrow"]], inc=dv)
            for half in range(2):
                wl = (list(pbT.ready) + list(pbT.freed)
                      + A["B_lrow"].ready + self.B_c.ready)
                n = b.raw("tensor", lambda e, c, half=half:
                          e.matmul(
                              ppT[:, half * BLK:(half + 1) * BLK],
                              self.onesb[0:1, :],
                              A["lrow"][0:1, half * BLK:(half + 1) * BLK],
                              start=True, stop=True), waits=wl,
                          inc=self.s_pe)
                pbT.ready = [(self.s_pe, n)]
            b.op("vector", lambda e, c: e.tensor_copy(
                A["rlb"], ppT[:, :]),
                ins=[pbT], outs=[A["B_rlb"]], inc=dv)
            b.op("vector", lambda e, c: e.tensor_mul(
                A["osb"], A["oacc"], A["rlb"]),
                ins=[A["B_oacc"], A["B_rlb"]], outs=[A["B_osb"]], inc=dv)
            b.op("sync", lambda e, c, h=h: e.dma_start(
                out=self.d_q[h * DH:(h + 1) * DH, :], in_=A["osb"]),
                ins=[A["B_osb"]], outs=[self.Bd_q], inc=self.s_ld, amt=16)
        for i, pb_ in ((4, pbV[0]), (5, pbV[1]), (6, pbT), (7, pbT)):
            psB[i].ready = list(pb_.ready)
            psB[i].freed = list(pb_.freed)

    def emit_wo(self, A):
        b = self.b
        dv, wg, ld = self.s_dv, self.s_wg, self.s_ld
        ps, psB = self.ps, self.psB
        xr3 = self.xres.rearrange("p (f t) -> p f t", t=TPC)
        wob = [A["wkva"][i] for i in range(2)]
        Bwob = [A["B_wkva"][i] for i in range(2)]
        obuf = A["xln"][:, 0:8192]
        obuf2 = A["xln"][:, 8192:16384]
        B_obuf = Buf("obuf")
        B_obuf.freed = A["B_xln"].freed + A["B_xln"].ready
        B_obuf2 = Buf("obuf2")
        B_obuf2.freed = A["B_xln"].freed + A["B_xln"].ready
        for k in range(16):
            b.op("sync", lambda e, c, k=k: e.dma_start(
                out=obuf[:, k * 512:(k + 1) * 512],
                in_=self.d_q[k * DH:(k + 1) * DH, 0:512]),
                ins=[self.Bd_q], outs=[B_obuf], inc=ld, amt=16)
            b.op("sync", lambda e, c, k=k: e.dma_start(
                out=obuf2[:, k * 512:(k + 1) * 512],
                in_=self.d_q[k * DH:(k + 1) * DH, 512:1024]),
                ins=[self.Bd_q], outs=[B_obuf2], inc=ld, amt=16)
        for m in range(16):
            wb, Bwb = wob[m % 2], Bwob[m % 2]
            b.op("gpsimd", lambda e, c, m=m, wb=wb: e.dma_start(
                out=wb.rearrange("p (k q) -> p k q", k=16),
                in_=self.i_wo[:, m * 128:(m + 1) * 128].rearrange(
                    "(k p) q -> p k q", p=128)),
                outs=[Bwb], inc=wg, amt=16)
            wb3 = wb.rearrange("p (k q) -> p k q", k=16)
            for half in range(2):
                pb_ = psB[4 + half]
                pv = ps[4 + half]
                src = obuf if half == 0 else obuf2
                sbuf = B_obuf if half == 0 else B_obuf2
                for k in range(16):
                    self.mm(pv[:, 0:BLK], wb3[:, k, :],
                            src[:, k * 512:(k + 1) * 512], pb_,
                            [Bwb, sbuf], start=(k == 0), stop=(k == 15),
                            first=(k == 0))
                tc0 = half * BLK
                b.op("vector", lambda e, c, m=m, tc0=tc0, pv=pv:
                     e.scalar_tensor_tensor(
                         xr3[:, m, tc0:tc0 + BLK], pv[:, 0:BLK], 1.0,
                         xr3[:, m, tc0:tc0 + BLK], ALU.mult, ALU.add),
                     ins=[pb_, self.B_x], outs=[self.B_x], inc=dv)

    def emit_store(self):
        b = self.b
        b.op("sync", lambda e, c: e.dma_start(
            out=self.o_out.rearrange("(f p) t -> p f t", p=128),
            in_=self.xres.rearrange("p (f t) -> p f t", t=TPC)),
            ins=[self.B_x], outs=[Buf("out")], inc=self.s_ld, amt=16)

    def finish(self):
        b = self.b
        sems = [self.s_ld, self.s_wg, self.s_pe, self.s_dv, self.s_ac,
                self.s_gp, self.s_od, self.s_odg, self.s_aq] \
            + self.s_kq + self.s_vq + self.s_vt + self.s_rs + self.s_agst \
            + [self.s_aggu, self.s_agdn] + self.s_agkv + self.s_aglr
        waits = [(s, s.n) for s in sems if s.n > 0]
        b.raw("gpsimd", lambda e, c: None, waits=waits)

    def build(self):
        self.declare()
        self.mem()
        self.sems()
        self.prelude()
        if self.dbg != "pre":
            self.mamba()
        if self.dbg in ("f2", "a3", ""):
            self.emit_dequant()
            self.emit_ffn(0)
        if self.dbg in ("a3", ""):
            self.emit_attn()
        if self.dbg == "":
            self.emit_ffn(1)
            self.emit_store()
        if self.dbg in ("pre", "ip0", "ln0", "f0", "ff", "m1", "f2", "a3"):
            o = self.dout("xres", DM, TPC)
            self.b.op("sync", lambda e, c: e.dma_start(
                out=o.rearrange("(f p) t -> p f t", p=128),
                in_=self.xres.rearrange("p (f t) -> p f t", t=TPC)),
                ins=[self.B_x], outs=[Buf("dbgo")], inc=self.s_ld, amt=16)
            self.finish()
            self.b.finalize()
            return
        self.finish()
        self.b.finalize()


def _prep_core_inputs(inputs, j):
    f32 = np.float32
    x = np.asarray(inputs["L0_attn_a_x"], f32)[0]
    xT = np.ascontiguousarray(x.T)
    bA, bB = j, 15 - j
    xc = np.zeros((DM, TPC + 6), f32)
    for slot, bb in ((0, bA), (1, bB)):
        c0 = slot * 515
        t0 = bb * BLK
        if t0 >= 3:
            xc[:, c0:c0 + 3] = xT[:, t0 - 3:t0]
        xc[:, c0 + 3:c0 + 3 + BLK] = xT[:, t0:t0 + BLK]

    D = np.asarray(inputs["L0_attn_a_D"], f32)
    dp = np.zeros((128, 32), f32)
    for t in range(32):
        dp[0:64, t] = D[2 * t]
        dp[64:128, t] = D[2 * t + 1]

    wkva = np.asarray(inputs["L0_attn_b_w_kva"], f32)
    used = [wkva[:, 0:KR]]
    for h in range(NH):
        c0 = KR + h * 2 * DH
        used.append(wkva[:, c0:c0 + DH])
    wkva_used = np.ascontiguousarray(np.concatenate(used, axis=1))

    gup = np.asarray(inputs["L0_ffn_gate_up_packed"], np.int32)
    gus = np.asarray(inputs["L0_ffn_gate_up_scales"], np.float16)
    dnp = np.asarray(inputs["L0_ffn_down_packed"], np.int32)
    dns = np.asarray(inputs["L0_ffn_down_scales"], np.float16)

    d = {
        "x": xc,
        "w_inT": np.ascontiguousarray(
            np.asarray(inputs["L0_attn_a_in_proj_w"], f32).T),
        "conv_w": np.asarray(inputs["L0_attn_a_conv_w"], f32),
        "dt_bias": np.asarray(inputs["L0_attn_a_dt_bias"], f32).reshape(-1, 1),
        "A_log": np.asarray(inputs["L0_attn_a_A_log"], f32).reshape(-1, 1),
        "D_pairs": dp,
        "norm_w": np.ascontiguousarray(
            np.asarray(inputs["L0_attn_a_norm_w"], f32).reshape(32, 128).T),
        "w_outT": np.ascontiguousarray(
            np.asarray(inputs["L0_attn_a_out_proj_w"], f32).T),
        "g_q_row": np.asarray(inputs["L0_attn_b_g_q"], f32).reshape(1, -1),
        "g_kv_row": np.asarray(inputs["L0_attn_b_g_kv"], f32).reshape(1, -1),
        "w_qa": np.asarray(inputs["L0_attn_b_w_qa"], f32),
        "w_qb": np.asarray(inputs["L0_attn_b_w_qb"], f32),
        "w_kva_used": wkva_used,
        "w_kvb": np.asarray(inputs["L0_attn_b_w_kvb"], f32),
        "w_o": np.asarray(inputs["L0_attn_b_w_o"], f32),
        "gup_packed": np.ascontiguousarray(gup[j * GU_SH:(j + 1) * GU_SH, :]),
        "gup_scalesT": np.ascontiguousarray(
            gus[:, j * GU_SH:(j + 1) * GU_SH].T),
        "down_packed": np.ascontiguousarray(dnp[j * DN_SH:(j + 1) * DN_SH, :]),
        "down_scalesT": np.ascontiguousarray(
            dns[:, j * DN_SH:(j + 1) * DN_SH].T),
        "jcol": np.full((128, 1), float(j), f32),
    }
    return d


_PROG_CACHE = {}


def _get_prog(dbg=""):
    if dbg not in _PROG_CACHE:
        p = Prog(dbg)
        p.build()
        _PROG_CACHE[dbg] = p
    return _PROG_CACHE[dbg]


def run_cores(inputs, dbg=""):
    p = _get_prog(dbg)
    in_maps = [_prep_core_inputs(inputs, j) for j in range(NCORE)]
    res = run_bass_kernel_spmd(p.nc, in_maps, core_ids=list(range(NCORE)))
    return res.results


def kernel(**inputs):
    res = run_cores(inputs)
    full = np.zeros((SEQ_TOTAL, DM), np.float32)
    for j in range(NCORE):
        o = np.asarray(res[j]["out"], np.float32)
        full[j * BLK:(j + 1) * BLK] = o[:, 0:BLK].T
        full[(15 - j) * BLK:(16 - j) * BLK] = o[:, BLK:TPC].T
    return full.reshape(1, SEQ_TOTAL, DM)


SEQ_TOTAL = 8192


# version 07
# speedup vs baseline: 1.0582x; 1.0582x over previous
import numpy as np
import ml_dtypes
import concourse.bass as bass
import concourse.mybir as mybir
from concourse.bass_utils import run_bass_kernel_spmd

F32 = mybir.dt.float32
BF16 = mybir.dt.bfloat16
I32 = mybir.dt.int32
F16 = mybir.dt.float16
AF = mybir.ActivationFunctionType
ALU = mybir.AluOpType
AXX = mybir.AxisListType.X

NCORE = 8
DM = 2048
HID = 4096
STATE = 128
HEADS = 64
PD = 64
CHUNK = 256
CONVD = HID + 2 * STATE
DIN = 2 * HID + 2 * STATE + HEADS
BLK = 512
TPC = 1024
EMB = 2048
NH = 16
DH = 128
QR = 64
KR = 128
RD = 64
FI = 5632
GS = 128
GU_SH = 2 * FI // NCORE
DN_SH = DM // NCORE

LN_EPS = 1e-5
MAMBA_EPS = 1e-5
FFN_EPS = 1e-6

ENGINES = ("sync", "gpsimd", "vector", "scalar", "tensor")


class Sem:
    __slots__ = ("name", "h", "n")

    def __init__(self, name, h):
        self.name, self.h, self.n = name, h, 0


class Buf:
    __slots__ = ("name", "ready", "freed")

    def __init__(self, name=""):
        self.name, self.ready, self.freed = name, [], []


class EngCtx:
    def __init__(self, eng):
        self.eng = eng
        self._c = {}

    @property
    def cid(self):
        if "cid" not in self._c:
            self._c["cid"] = self.eng.snap(self.eng.partition_id(),
                                           min_val=0, max_val=NCORE - 1)
        return self._c["cid"]


class B:
    def __init__(self, nc):
        self.nc = nc
        self.streams = {e: [] for e in ENGINES}
        self.waited = {e: {} for e in ENGINES}

    def sem(self, name):
        return Sem(name, self.nc.semaphore(name).__enter__())

    def raw(self, eng, fn, waits=(), inc=None, amt=1, dyn=None, relax=False):
        if relax is not False and inc is not None:
            waits = list(waits) + [(inc, inc.n if relax is True else relax)]
        wmax = {}
        for (s, c) in waits:
            if c > 0 and c > wmax.get(s, 0):
                wmax[s] = c
        wl = list(wmax.items())
        if inc is not None:
            inc.n += amt
        wd = self.waited[eng]
        if eng == "tensor":
            wl = [(s, c) for (s, c) in wl if s.name != "pe"]

        lagwait = None
        if inc is not None and amt == 16 and not relax:
            lagwait = inc.n

        def closure(e, ctx, wl=wl, fn=fn, inc=inc, amt=amt, dyn=dyn,
                    lagwait=lagwait):
            for s, c in wl:
                if wd.get(s.name, -1) < c:
                    e.wait_ge(s.h, c)
                    wd[s.name] = c
            if dyn is not None:
                dyn(e, ctx)
            try:
                instr = fn(e, ctx)
            except Exception as ex:
                import traceback
                cv = {k: v for k, v in
                      (fn.__defaults__ and
                       zip(fn.__code__.co_varnames[2:], fn.__defaults__)
                       or [])}
                raise RuntimeError(
                    f"op failed: vars={cv} err={ex}") from ex
            if inc is not None and instr is not None:
                instr.then_inc(inc.h, amt)
            if lagwait is not None and lagwait > 0 and \
                    wd.get(inc.name, -1) < lagwait:
                e.wait_ge(inc.h, lagwait)
                wd[inc.name] = lagwait

        self.streams[eng].append(closure)
        return inc.n if inc is not None else None

    def op(self, eng, fn, ins=(), outs=(), inc=None, amt=1, waits=(), dyn=None,
           relax=False):
        wl = list(waits)
        for bu in ins:
            wl += bu.ready
        for bu in outs:
            wl += bu.ready + bu.freed
        n = self.raw(eng, fn, waits=wl, inc=inc, amt=amt, dyn=dyn, relax=relax)
        dep = (inc, n)
        for bu in outs:
            bu.ready = [dep]
            bu.freed = []
        for bu in ins:
            bu.freed.append(dep)
        return n

    def finalize(self):
        streams = self.streams
        with self.nc.Block() as block:
            def run(e, name):
                ctx = EngCtx(e)
                for f in streams[name]:
                    f(e, ctx)

            @block.sync
            def _(e):
                run(e, "sync")

            @block.gpsimd
            def _(e):
                run(e, "gpsimd")

            @block.vector
            def _(e):
                run(e, "vector")

            @block.scalar
            def _(e):
                run(e, "scalar")

            @block.tensor
            def _(e):
                run(e, "tensor")


class Arena:
    def __init__(self, nc, name, cols, dtype):
        self.t = nc.alloc_sbuf_tensor(name, [128, cols], dtype)
        self.cols = cols
        self.cur = 0
        self.marks = []

    def alloc(self, n, name=""):
        assert self.cur + n <= self.cols, \
            f"arena overflow: {name} wants {n} at {self.cur}/{self.cols}"
        ap = self.t.ap()[:, self.cur:self.cur + n]
        self.cur += n
        return ap

    def mark(self):
        self.marks.append(self.cur)

    def release(self):
        self.cur = self.marks.pop()


class Prog:
    def __init__(self, dbg=""):
        self.nc = bass.Bass("TRN2", target_bir_lowering=False)
        self.b = B(self.nc)
        self.dbg = dbg
        self.dbg_out = {}

    def dout(self, name, rows, cols, dt=F32):
        ap = self.nc.dram_tensor("dbg_" + name, [rows, cols], dt,
                                 kind="ExternalOutput").ap()
        self.dbg_out[name] = ap
        return ap

    def declare(self):
        nc = self.nc

        def din(name, shape, dt=F32):
            return nc.dram_tensor(name, shape, dt, kind="ExternalInput").ap()

        self.i_x = din("x", [DM, TPC + 6])
        self.i_winT = din("w_inT", [DM, DIN])
        self.i_conv = din("conv_w", [CONVD, 4])
        self.i_dtb = din("dt_bias", [HEADS, 1])
        self.i_alog = din("A_log", [HEADS, 1])
        self.i_dpair = din("D_pairs", [128, 32])
        self.i_normw = din("norm_w", [128, HID // 128])
        self.i_woutT = din("w_outT", [HID, DM])
        self.i_gqr = din("g_q_row", [1, DM])
        self.i_gkvr = din("g_kv_row", [1, DM])
        self.i_wqa = din("w_qa", [DM, QR])
        self.i_wqb = din("w_qb", [QR, EMB])
        self.i_wkva = din("w_kva_used", [DM, KR + NH * DH])
        self.i_wkvb = din("w_kvb", [KR, EMB])
        self.i_wo = din("w_o", [EMB, EMB])
        self.i_gup = din("gup_packed", [GU_SH, DM // 8], I32)
        self.i_gus = din("gup_scalesT", [GU_SH, DM // GS], F16)
        self.i_dnp = din("down_packed", [DN_SH, FI // 8], I32)
        self.i_dns = din("down_scalesT", [DN_SH, FI // GS], F16)
        self.i_jcol = din("jcol", [128, 1])

        self.o_out = nc.dram_tensor("out", [DM, TPC], F32,
                                    kind="ExternalOutput").ap()

        def scr(name, rows, cols, dt=BF16):
            return nc.dram_tensor(name, [rows, cols], dt, kind="Internal").ap()

        self.d_z = scr("z_spill", 4 * HID, CHUNK)
        self.d_yp = scr("yp_spill", 4 * HID, CHUNK)
        self.d_stin = [scr(f"st_in{r}", 128, HEADS * PD, F32)
                       for r in range(2)]
        self.d_stout = [scr(f"st_out{r}", NCORE * 128, HEADS * PD, F32)
                        for r in range(2)]
        self.d_lrin = [scr(f"lr_in{r}", 1, HEADS, F32) for r in range(2)]
        self.d_lrout = [scr(f"lr_out{r}", NCORE, HEADS, F32)
                        for r in range(2)]
        self.Bd_lrin = [Buf("d_lrin0"), Buf("d_lrin1")]
        self.Bd_lrout = [Buf("d_lrout0"), Buf("d_lrout1")]
        self.d_guin = scr("gu_in", DM, GU_SH)
        self.d_guout = scr("gu_out", NCORE * DM, GU_SH)
        self.d_dnin = scr("dn_in", FI, DN_SH)
        self.d_dnout = scr("dn_out", NCORE * FI, DN_SH)
        self.d_kvin = [scr(f"kv_in{r}", 2 * EMB + 1, BLK) for r in range(2)]
        self.d_kvall = scr("kv_all", 2 * NCORE * (2 * EMB + 1), BLK)
        RUNROWS = NCORE * (2 * EMB + 1)
        self.d_kvout = [self.d_kvall[0:RUNROWS, :],
                        self.d_kvall[RUNROWS:2 * RUNROWS, :]]
        self.d_q = scr("q_spill", NH * DH, TPC)
        self.d_astg = scr("attn_stg", 7 * 4096, BLK)
        self.d_xbc1 = scr("xbc1_spill", 34 * 128, CHUNK)
        self.Bd_xbc1 = Buf("d_xbc1")
        self.Bd_z = Buf("d_z")
        self.Bd_yp = Buf("d_yp")
        self.Bd_stin = [Buf("d_stin0"), Buf("d_stin1")]
        self.Bd_stout = [Buf("d_stout0"), Buf("d_stout1")]
        self.Bd_q = Buf("d_q")

    def mem(self):
        nc = self.nc
        self.af = Arena(nc, "af", 23800, F32)
        self.ab = Arena(nc, "ab", 58752, BF16)

        self.xres = self.af.alloc(16 * TPC, "xres")
        self.onesf = self.af.alloc(128, "onesf")
        self.cc = self.af.alloc(8, "cc")
        self.p_dtb = self.af.alloc(1, "dtb")
        self.p_negA = self.af.alloc(1, "negA")
        self.p_dpair = self.af.alloc(32, "dpair")
        self.p_normw = self.af.alloc(32, "normw")
        self.p_convw = self.af.alloc(34 * 4, "convw")
        self.p_jcol = self.af.alloc(1, "jcol")
        self.identf = self.af.alloc(128, "identf")
        self.ln_rows = self.af.alloc(4 * 512, "ln_rows")
        self.B_lnrows = Buf("lnrows")

        self.grows = self.ab.alloc(2 * DM, "grows")
        self.onesb = self.ab.alloc(128, "onesb")
        self.lnr16 = self.ab.alloc(1024, "lnr16")
        self.B_lnr16 = Buf("lnr16")
        self.identb = self.ab.alloc(128, "identb")
        self.trilm = self.ab.alloc(2 * CHUNK, "trilm")
        self.fmask = self.ab.alloc(4 * BLK, "fmask")

        self.ppsb = [nc.alloc_psum_tensor(f"pp{i}", [128, 1024], F32)
                     for i in range(4)]
        self.pp = [t.ap() for t in self.ppsb]
        self.ps = [self.pp[i // 2][:, (i % 2) * 512:(i % 2) * 512 + 512]
                   for i in range(8)]
        self.psbf = [p.bitcast(BF16) for p in self.ps]
        self.psB = [Buf(f"ps{i}") for i in range(8)]

        self.B_x = Buf("xres")
        self.B_par = Buf("par")
        self.B_c = Buf("const")
        self.B_rows = Buf("rows2048")

    def sems(self):
        b = self.b
        self.s_ld = b.sem("ld")
        self.s_wg = b.sem("wg")
        self.s_pe = b.sem("pe")
        self.s_dv = b.sem("dv")
        self.s_ac = b.sem("ac")
        self.s_gp = b.sem("gp")
        self.s_agst = [b.sem("agst0"), b.sem("agst1")]
        self.s_aggu = b.sem("aggu")
        self.s_agdn = b.sem("agdn")
        self.s_agkv = [b.sem("agkv0"), b.sem("agkv1")]
        self.s_aglr = [b.sem("aglr0"), b.sem("aglr1")]
        self.s_od = b.sem("od")
        self.s_odg = b.sem("odg")
        self.s_aq = b.sem("aq")
        self.s_kq = [b.sem("kq0"), b.sem("kq1")]
        self.s_vq = [b.sem("vq0"), b.sem("vq1")]
        self.s_vt = [b.sem("vt0"), b.sem("vt1")]
        self.s_rs = [b.sem("rs0"), b.sem("rs1")]

    def mm(self, out_ap, lhsT_ap, rhs_ap, pbuf, ins, start, stop, first):
        b = self.b
        wl = []
        for bu in ins:
            wl += bu.ready
        wl += pbuf.ready
        if first:
            wl += pbuf.freed
        n = b.raw("tensor", lambda e, c: e.matmul(
            out_ap, lhsT_ap, rhs_ap, start=start, stop=stop),
            waits=wl, inc=self.s_pe)
        dep = (self.s_pe, n)
        pbuf.ready = [dep]
        if first:
            pbuf.freed = []
        for bu in ins:
            bu.freed.append(dep)
        return n

    def tposeb(self, dst_ap, dst_buf, src_ap, src_bufs, bank, r, c):
        pb = self.psB[bank]
        pv = self.psbf[bank][0:c, 0:r]
        b = self.b
        wl = pb.ready + pb.freed + self.B_c.ready
        for bu in src_bufs:
            wl = wl + bu.ready
        n = b.raw("tensor", lambda e, cx: e.transpose(
            pv, src_ap, self.identb[0:r, 0:r]), waits=wl, inc=self.s_pe)
        pb.ready = [(self.s_pe, n)]
        pb.freed = []
        for bu in src_bufs:
            bu.freed.append((self.s_pe, n))
        b.op("vector", lambda e, cx: e.tensor_copy(dst_ap, pv),
             ins=[pb], outs=[dst_buf], inc=self.s_dv)

    def tposef(self, dst_ap, dst_buf, src_ap, src_bufs, bank, r, c):
        pb = self.psB[bank]
        pv = self.ps[bank][0:c, 0:r]
        b = self.b
        wl = pb.ready + pb.freed + self.B_c.ready
        for bu in src_bufs:
            wl = wl + bu.ready
        n = b.raw("tensor", lambda e, cx: e.transpose(
            pv, src_ap, self.identf[0:r, 0:r]), waits=wl, inc=self.s_pe)
        pb.ready = [(self.s_pe, n)]
        pb.freed = []
        for bu in src_bufs:
            bu.freed.append((self.s_pe, n))
        b.op("vector", lambda e, cx: e.tensor_copy(dst_ap, pv),
             ins=[pb], outs=[dst_buf], inc=self.s_dv)

    def emit_ln(self, dst_tiles, src_tiles, n, gain_row, stg, B_stg,
                banks=(4, 5, 6, 7), n_feat=DM):
        b = self.b
        nt = len(src_tiles)
        s0, s1, s2, s3 = [self.psB[i] for i in banks]
        p0, p1, p2, p3 = [self.ps[i] for i in banks]
        xb = stg[:, 0:n]
        sq = stg[:, n:2 * n]
        for ft in range(nt):
            src, bsrc = src_tiles[ft]
            b.op("vector", lambda e, c, src=src, xb=xb: e.tensor_copy(xb, src),
                 ins=[bsrc], outs=[B_stg], inc=self.s_dv)
            b.op("scalar", lambda e, c, src=src, sq=sq: e.activation(
                sq, src, AF.Square), ins=[bsrc], outs=[B_stg], inc=self.s_ac)
            self.mm(p0[0:1, 0:n], self.onesb[:, 0:1], xb, s0,
                    [B_stg, self.B_c], start=(ft == 0), stop=(ft == nt - 1),
                    first=(ft == 0))
            self.mm(p1[0:1, 0:n], self.onesb[:, 0:1], sq, s1,
                    [B_stg, self.B_c], start=(ft == 0), stop=(ft == nt - 1),
                    first=(ft == 0))
        inv = 1.0 / n_feat
        rows = self.ln_rows
        Br = self.B_lnrows
        m_r = rows[0:1, 0:n]
        v_r = rows[0:1, 512:512 + n]
        a_r = rows[0:1, 1024:1024 + n]
        b_r = rows[0:1, 1536:1536 + n]
        b.op("vector", lambda e, c: e.tensor_scalar_mul(m_r, p0[0:1, 0:n], inv),
             ins=[s0], outs=[Br], inc=self.s_dv)
        b.op("vector", lambda e, c: e.tensor_scalar_mul(v_r, p1[0:1, 0:n], inv),
             ins=[s1], outs=[Br], inc=self.s_dv)
        b.op("vector", lambda e, c: e.scalar_tensor_tensor(
            a_r, m_r, -1.0, m_r, ALU.mult, ALU.mult),
            ins=[Br], outs=[Br], inc=self.s_dv)
        b.op("vector", lambda e, c: e.tensor_add(v_r, v_r, a_r),
             ins=[Br], outs=[Br], inc=self.s_dv)
        b.op("scalar", lambda e, c: e.activation(
            v_r, v_r, AF.Sqrt, bias=self.cc[0:1, 0:1]),
            ins=[Br, self.B_c], outs=[Br], inc=self.s_ac)
        b.op("vector", lambda e, c: e.reciprocal(a_r, v_r),
             ins=[Br], outs=[Br], inc=self.s_dv)
        b.op("vector", lambda e, c: e.scalar_tensor_tensor(
            b_r, m_r, -1.0, a_r, ALU.mult, ALU.mult),
            ins=[Br], outs=[Br], inc=self.s_dv)
        hoist = gain_row is None
        if hoist:
            self.mm(p2[:, 0:n], self.onesf[0:1, :], a_r, s2, [Br, self.B_c],
                    start=True, stop=True, first=True)
            self.mm(p3[:, 0:n], self.onesf[0:1, :], b_r, s3, [Br, self.B_c],
                    start=True, stop=True, first=True)
        else:
            a16 = self.lnr16[0:1, 0:n]
            b16 = self.lnr16[0:1, 512:512 + n]
            b.op("vector", lambda e, c: e.tensor_copy(a16, a_r),
                 ins=[Br], outs=[self.B_lnr16], inc=self.s_dv)
            b.op("vector", lambda e, c: e.tensor_copy(b16, b_r),
                 ins=[Br], outs=[self.B_lnr16], inc=self.s_dv)
        for ft in range(nt):
            if not hoist:
                gr, gb = gain_row
                lrow = gr[0:1, ft * 128:(ft + 1) * 128]
                self.mm(p2[:, 0:n], lrow, a16, s2, [self.B_lnr16, gb],
                        start=True, stop=True, first=True)
                self.mm(p3[:, 0:n], lrow, b16, s3, [self.B_lnr16, gb],
                        start=True, stop=True, first=True)
            src, bsrc = src_tiles[ft]
            dst, bdst = dst_tiles[ft]
            b.op("vector", lambda e, c, src=src, xb=xb: e.tensor_mul(
                xb, src, p2[:, 0:n]),
                ins=[bsrc, s2], outs=[B_stg], inc=self.s_dv)
            b.op("vector", lambda e, c, dst=dst, xb=xb: e.tensor_add(
                dst, xb, p3[:, 0:n]),
                ins=[B_stg, s3], outs=[bdst], inc=self.s_dv)


    def phase_seed(self):
        sems = [self.s_ld, self.s_wg, self.s_pe, self.s_dv, self.s_ac,
                self.s_gp, self.s_od, self.s_odg]
        return [(s, s.n) for s in sems if s.n > 0]

    def odma(self, eng, fn, ins=(), outs=()):
        s = self.s_odg if eng == "gpsimd" else self.s_od
        return self.b.op(eng, fn, ins=ins, outs=outs, inc=s, amt=16,
                         waits=[(s, s.n)])

    def rowbcast(self, bcast_bank, src_full, src_buf, h, n, nrows, bf, mv_bank,
                 evac_eng="scalar"):
        ps, psB = self.ps, self.psB
        b = self.b
        ident = self.identb if bf else self.identf
        ones = self.onesb if bf else self.onesf
        rstg = self.M["rowstg16"] if bf else self.M["rowstg"]
        Brstg = self.M["B_rowstg16"] if bf else self.M["B_rowstg"]
        self.mm(ps[mv_bank][0:1, 0:n], ident[0:nrows, h:h + 1], src_full,
                psB[mv_bank], [src_buf, self.B_c], start=True, stop=True,
                first=True)
        if evac_eng == "scalar":
            b.op("scalar", lambda e, c: e.copy(rstg[0:1, 0:n],
                                               ps[mv_bank][0:1, 0:n]),
                 ins=[psB[mv_bank]], outs=[Brstg], inc=self.s_ac)
        else:
            b.op("vector", lambda e, c: e.tensor_copy(rstg[0:1, 0:n],
                                                      ps[mv_bank][0:1, 0:n]),
                 ins=[psB[mv_bank]], outs=[Brstg], inc=self.s_dv)
        self.mm(ps[bcast_bank][:, 0:n], ones[0:1, :], rstg[0:1, 0:n],
                psB[bcast_bank], [Brstg, self.B_c], start=True, stop=True,
                first=True)

    def prelude(self):
        b = self.b
        ld, gp = self.s_ld, self.s_gp

        xr3 = self.xres.rearrange("p (f t) -> p f t", t=TPC)
        xs3 = self.i_x.rearrange("(f p) t -> p f t", p=128)
        b.op("sync", lambda e, c: e.dma_start(
            out=xr3[:, :, 0:BLK], in_=xs3[:, :, 3:3 + BLK]),
            outs=[self.B_x], inc=ld, amt=16)
        b.op("sync", lambda e, c: e.dma_start(
            out=xr3[:, :, BLK:TPC], in_=xs3[:, :, 518:518 + BLK]),
            outs=[self.B_x], inc=ld, amt=16)
        self.xhalo = [self.af.alloc(16 * 3, f"xhalo{r}") for r in range(2)]
        self.B_xh = [Buf("xhalo0"), Buf("xhalo1")]
        for r, src0 in ((0, 0), (1, 515)):
            b.op("sync", lambda e, c, r=r, src0=src0: e.dma_start(
                out=self.xhalo[r].rearrange("p (f t) -> p f t", t=3),
                in_=xs3[:, :, src0:src0 + 3]),
                outs=[self.B_xh[r]], inc=ld, amt=16)

        def pl(dst, src, bufs=None):
            b.op("sync", lambda e, c, d=dst, s=src: e.dma_start(out=d, in_=s),
                 outs=[bufs or self.B_par], inc=ld, amt=16)

        self.odma("sync", lambda e, c: e.dma_start(
            out=self.p_dtb[0:HEADS, :], in_=self.i_dtb), outs=[self.B_par])
        self.odma("sync", lambda e, c: e.dma_start(
            out=self.p_negA[0:HEADS, :], in_=self.i_alog), outs=[self.B_par])
        pl(self.p_dpair, self.i_dpair)
        pl(self.p_jcol, self.i_jcol)
        def plg(dst, src2):
            self.odma("gpsimd", lambda e, c, d=dst, s=src2: e.dma_start(
                out=d, in_=s), outs=[self.B_rows])
        plg(self.grows[0:1, 0:DM], self.i_gqr)
        plg(self.grows[0:1, DM:2 * DM], self.i_gkvr)
        pl(self.p_normw, self.i_normw)
        b.op("sync", lambda e, c: e.dma_start(
            out=self.p_convw.rearrange("p (a b) -> p a b", b=4),
            in_=self.i_conv.rearrange("(a p) b -> p a b", p=128)),
            outs=[self.B_par], inc=ld, amt=16)

        b.op("gpsimd", lambda e, c: e.memset(self.onesf, 1.0),
             outs=[self.B_c], inc=gp)
        b.op("gpsimd", lambda e, c: e.memset(self.onesb, 1.0),
             outs=[self.B_c], inc=gp)
        for t in (self.identb, self.identf):
            b.op("gpsimd", lambda e, c, t=t: e.memset(t, 1.0),
                 outs=[self.B_c], inc=gp)
            b.op("gpsimd", lambda e, c, t=t: e.affine_select(
                out=t, in_=t, compare_op=ALU.is_equal, fill=0.0,
                base=0, pattern=[[-1, 128]], channel_multiplier=1),
                outs=[self.B_c], inc=gp)
        for ti in range(2):
            t = self.trilm[:, ti * CHUNK:(ti + 1) * CHUNK]
            b.op("gpsimd", lambda e, c, t=t: e.memset(t, 1.0),
                 outs=[self.B_c], inc=gp)
            b.op("gpsimd", lambda e, c, t=t, ti=ti: e.affine_select(
                out=t, in_=t, compare_op=ALU.is_ge, fill=0.0,
                base=-ti * 128, pattern=[[1, CHUNK]], channel_multiplier=-1),
                outs=[self.B_c], inc=gp)
        for w in range(4):
            t = self.fmask[:, w * BLK:(w + 1) * BLK]
            b.op("gpsimd", lambda e, c, t=t: e.memset(t, 1.0),
                 outs=[self.B_c], inc=gp)
            b.op("gpsimd", lambda e, c, t=t, w=w: e.affine_select(
                out=t, in_=t, compare_op=ALU.is_ge, fill=0.0,
                base=-128 * w, pattern=[[1, BLK]], channel_multiplier=-1),
                outs=[self.B_c], inc=gp)
        for i, v in enumerate((LN_EPS, MAMBA_EPS, FFN_EPS, 1.0)):
            b.op("gpsimd", lambda e, c, i=i, v=v: e.memset(
                self.cc[:, i:i + 1], v), outs=[self.B_c], inc=gp)

        b.op("scalar", lambda e, c: e.activation(
            self.p_negA[0:HEADS, :], self.p_negA[0:HEADS, :], AF.Exp),
            ins=[self.B_par], outs=[self.B_par], inc=self.s_ac)
        b.op("vector", lambda e, c: e.tensor_scalar_mul(
            self.p_negA[0:HEADS, :], self.p_negA[0:HEADS, :], -1.0),
            ins=[self.B_par], outs=[self.B_par], inc=self.s_dv)

    def mamba(self):
        af, ab, b = self.af, self.ab, self.b
        af.mark()
        ab.mark()
        M = {}
        self.M = M

        def abuf(ar, n, name):
            M[name] = ar.alloc(n, name)
            M["B_" + name] = Buf(name)
            return M[name]

        self.front_col0 = ab.cur
        abuf(ab, 16 * 515, "xlnT")
        M["wbuf"] = [ab.alloc(16 * 128, f"wb{i}") for i in range(2)]
        M["B_wbuf"] = [Buf("wb0"), Buf("wb1")]
        abuf(ab, 34 * 259, "xbc")
        abuf(ab, 1024, "lnstg")
        abuf(ab, 512, "zstg")
        abuf(ab, 512, "xstg1")
        abuf(ab, 512, "zg")
        abuf(ab, 2 * 4096, "xhT")
        abuf(ab, 2 * 64, "dtT")
        abuf(ab, 2 * 64, "decT")
        abuf(ab, 2 * 128, "BT")
        abuf(ab, 2 * 256, "Mt")
        abuf(ab, 64, "cdT")
        abuf(ab, 256, "dec")
        abuf(ab, 256, "dt16")
        abuf(ab, 102, "hs")
        self.front_col1 = ab.cur
        self.front_bufs = [M["B_xlnT"], M["B_wbuf"][0], M["B_wbuf"][1],
                           M["B_xbc"], M["B_lnstg"], M["B_zstg"],
                           M["B_xstg1"], M["B_zg"], M["B_xhT"], M["B_dtT"],
                           M["B_decT"], M["B_BT"], M["B_Mt"], M["B_cdT"],
                           M["B_dec"], M["B_dt16"], M["B_hs"]]
        M["st0"] = [ab.alloc(4096, f"st0_{r}") for r in range(2)]
        M["B_st0"] = [Buf("st0_0"), Buf("st0_1")]
        M["edAc"] = [ab.alloc(256, f"edAc{i}") for i in range(4)]
        M["B_edAc"] = [Buf(f"edAc{i}") for i in range(4)]
        M["cdB"] = [ab.alloc(64, f"cdB{i}") for i in range(4)]
        M["B_cdB"] = [Buf(f"cdB{i}") for i in range(4)]
        M["Cb"] = [ab.alloc(256, f"Cb{i}") for i in range(4)]
        M["B_Cb"] = [Buf(f"Cb{i}") for i in range(4)]
        M["CBm"] = [ab.alloc(2 * 256, f"CBm{i}") for i in range(4)]
        M["B_CBm"] = [Buf(f"CBm{i}") for i in range(4)]
        abuf(ab, 2 * 256, "ypstg")
        abuf(af, 512, "dtraw")
        abuf(af, 512, "convacc")
        abuf(af, 256, "Gtmp")
        abuf(af, 256, "dtf")
        abuf(af, 256, "dAf")
        abuf(af, 256, "dAc")
        abuf(af, 2 * 64, "dAcT")
        abuf(af, 1, "logtot")
        abuf(af, 64, "logrow")
        abuf(af, 3 * 16, "Lr")
        abuf(af, 16, "bvec")
        abuf(af, 16, "bv2")
        abuf(af, 16, "ohtmp")
        abuf(af, 1, "scm1")
        abuf(af, 512, "rowstg")
        abuf(af, 16, "iota16")
        b.op("gpsimd", lambda e, c: e.iota(
            M["iota16"][0:HEADS, :], pattern=[[1, 16]], base=0,
            channel_multiplier=0, allow_small_or_imprecise_dtypes=True),
            outs=[M["B_iota16"]], inc=self.s_gp)

        nfronts = 1 if self.dbg in ("ip0", "ln0", "f0") else 2
        for run in range(nfronts):
            self.mamba_front(run)
        if self.dbg in ("ip0", "ln0", "f0", "ff"):
            af.release()
            ab.release()
            return

        seed = []
        for bu in self.front_bufs:
            seed += bu.ready + bu.freed

        cur = self.front_col0

        def ov(n, name):
            nonlocal cur
            assert cur + n <= self.front_col1, f"overlay overflow {name}"
            M[name] = ab.t.ap()[:, cur:cur + n]
            bb = Buf(name)
            bb.freed = list(seed)
            M["B_" + name] = bb
            cur += n
            return M[name]

        ov(8192, "ych")
        M["strm"] = M["ych"].bitcast(F32)[:, 0:4096]
        M["B_strm"] = M["B_ych"]
        ov(8192, "senter")
        M["senter_f"] = M["senter"].bitcast(F32)
        ov(4096, "prevb")
        ov(512, "ysq")
        ov(256, "Cw")
        ov(512, "zg2")
        ov(16 * 64, "coefB")
        ov(512, "rowstg16")
        ov(16, "coef16")
        ov(64, "coefT")
        M["wobuf"] = [ov(32 * 128, "wo0"), ov(32 * 128, "wo1")]
        M["B_wobuf"] = [M["B_wo0"], M["B_wo1"]]

        for run in range(2):
            self.mamba_yhalf(run)

        af.release()
        ab.release()

    def mamba_front(self, run):
        M, b = self.M, self.b
        ld, wg, dv, ac = self.s_ld, self.s_wg, self.s_dv, self.s_ac
        ps, psB = self.ps, self.psB
        xr3 = self.xres.rearrange("p (f t) -> p f t", t=TPC)
        xlnT3 = M["xlnT"].rearrange("p (f t) -> p f t", t=515)
        B_xln = M["B_xlnT"]

        xh3 = self.xhalo[run].rearrange("p (f t) -> p f t", t=3)
        self.emit_ln(
            [(xlnT3[:, ft, 0:3], B_xln) for ft in range(16)],
            [(xh3[:, ft, :], self.B_xh[run]) for ft in range(16)],
            3, None, M["lnstg"], M["B_lnstg"])
        self.emit_ln(
            [(xlnT3[:, ft, 3:515], B_xln) for ft in range(16)],
            [(xr3[:, ft, run * BLK:(run + 1) * BLK], self.B_x)
             for ft in range(16)],
            BLK, None, M["lnstg"], M["B_lnstg"])

        if self.dbg == "ln0":
            return
        import os
        xbc3 = M["xbc"].rearrange("p (f t) -> p f t", t=259)
        for m in range(int(os.environ.get("MLIM", "67"))):
            wb = M["wbuf"][m % 2]
            Bwb = M["B_wbuf"][m % 2]
            mc = 64 if m == 66 else 128
            b.op("gpsimd", lambda e, c, m=m, wb=wb, mc=mc: e.dma_start(
                out=wb[:, 0:16 * mc].rearrange("p (k q) -> p k q", k=16),
                in_=self.i_winT[:, m * 128:m * 128 + mc].rearrange(
                    "(k p) q -> p k q", p=128)),
                outs=[Bwb], inc=wg, amt=16)
            wb3 = wb[:, 0:16 * mc].rearrange("p (k q) -> p k q", q=mc)
            pmain, phalo = psB[(m % 2) * 2], psB[(m % 2) * 2 + 1]
            vmain, vhalo = ps[(m % 2) * 2], ps[(m % 2) * 2 + 1]
            for k in range(16):
                self.mm(vmain[0:mc, 0:512], wb3[:, k, :], xlnT3[:, k, 3:515],
                        pmain, [Bwb, B_xln], start=(k == 0), stop=(k == 15),
                        first=(k == 0))
            if 32 <= m < 66:
                for k in range(16):
                    self.mm(vhalo[0:mc, 0:3], wb3[:, k, :], xlnT3[:, k, 0:3],
                            phalo, [Bwb, B_xln], start=(k == 0),
                            stop=(k == 15), first=(k == 0))
            if m < 32:
                b.op("vector", lambda e, c, vmain=vmain: e.tensor_copy(
                    M["zstg"], vmain[:, 0:512]),
                    ins=[pmain], outs=[M["B_zstg"]], inc=dv)
                if os.environ.get("NOSPILL"):
                    continue
                b.op("sync", lambda e, c, m=m, run=run: e.dma_start(
                    out=self.d_z.rearrange(
                        "(ci f p) t -> p ci f t", p=128, f=32)
                    [:, 2 * run:2 * run + 2, m, :],
                    in_=M["zstg"].rearrange("p (ci t) -> p ci t", ci=2)),
                    ins=[M["B_zstg"]], outs=[self.Bd_z], inc=ld, amt=16)
            elif m < 66:
                ft = m - 32
                b.op("vector", lambda e, c, ft=ft, vmain=vmain: e.tensor_copy(
                    xbc3[:, ft, 3:259], vmain[:, 0:256]),
                    ins=[pmain], outs=[M["B_xbc"]], inc=dv)
                b.op("vector", lambda e, c, ft=ft, vhalo=vhalo: e.tensor_copy(
                    xbc3[:, ft, 0:3], vhalo[:, 0:3]),
                    ins=[phalo], outs=[M["B_xbc"]], inc=dv)
                b.op("vector", lambda e, c, vmain=vmain: e.tensor_copy(
                    M["xstg1"][:, 0:256], vmain[:, 256:512]),
                    ins=[pmain], outs=[M["B_xstg1"]], inc=dv)
                b.op("sync", lambda e, c, ft=ft: e.dma_start(
                    out=self.d_xbc1[ft * 128:(ft + 1) * 128, :],
                    in_=M["xstg1"][:, 0:256]),
                    ins=[M["B_xstg1"]], outs=[self.Bd_xbc1], inc=ld, amt=16)
            else:
                b.op("vector", lambda e, c, vmain=vmain: e.tensor_copy(
                    M["dtraw"][0:HEADS, :], vmain[0:HEADS, 0:512]),
                    ins=[pmain], outs=[M["B_dtraw"]], inc=dv)

        if self.dbg == "ip0" or self.dbg == "ln0":
            return
        for cci in range(2):
            self.ssd_front(run, cci)

    def emit_conv(self):
        M, b = self.M, self.b
        dv, ac = self.s_dv, self.s_ac
        xbc3 = M["xbc"].rearrange("p (f t) -> p f t", t=259)
        cw3 = self.p_convw.rearrange("p (f a) -> p f a", a=4)
        acc = M["convacc"][:, 0:256]
        Ba = M["B_convacc"]
        for ft in range(34):
            for tap in range(4):
                if tap == 0:
                    b.op("vector", lambda e, c, ft=ft: e.tensor_scalar(
                        acc, xbc3[:, ft, 0:256], cw3[:, ft, 0:1],
                        None, ALU.mult),
                        ins=[M["B_xbc"], self.B_par], outs=[Ba], inc=dv)
                else:
                    b.op("vector", lambda e, c, ft=ft, tap=tap:
                         e.scalar_tensor_tensor(
                             acc, xbc3[:, ft, tap:tap + 256],
                             cw3[:, ft, tap:tap + 1], acc,
                             ALU.mult, ALU.add),
                         ins=[M["B_xbc"], self.B_par, Ba], outs=[Ba], inc=dv)
            b.op("scalar", lambda e, c, ft=ft: e.activation(
                xbc3[:, ft, 3:259], acc, AF.Silu),
                ins=[Ba], outs=[M["B_xbc"]], inc=ac)

    def ssd_front(self, run, cci):
        M, b = self.M, self.b
        ld, dv, ac = self.s_ld, self.s_dv, self.s_ac
        ps, psB = self.ps, self.psB
        xbc3 = M["xbc"].rearrange("p (f t) -> p f t", t=259)
        col0 = 3
        ch = run * 2 + cci

        if cci == 0:
            b.op("vector", lambda e, c: e.tensor_copy(
                M["hs"].rearrange("p (f t) -> p f t", t=3),
                xbc3[:, :, 256:259]),
                ins=[M["B_xbc"]], outs=[M["B_hs"]], inc=dv)
            self.emit_conv()
        else:
            b.op("sync", lambda e, c: e.dma_start(
                out=xbc3[:, :, 3:259],
                in_=self.d_xbc1.rearrange("(f p) t -> p f t", p=128)),
                ins=[self.Bd_xbc1], outs=[M["B_xbc"]], inc=ld, amt=16)
            b.op("vector", lambda e, c: e.tensor_copy(
                xbc3[:, :, 0:3], M["hs"].rearrange("p (f t) -> p f t", t=3)),
                ins=[M["B_hs"]], outs=[M["B_xbc"]], inc=dv)
            self.emit_conv()

        dtr = M["dtraw"][0:HEADS, cci * CHUNK:(cci + 1) * CHUNK]
        dtf = M["dtf"][0:HEADS, :]
        dA = M["dAf"][0:HEADS, :]
        dAc = M["dAc"][0:HEADS, :]
        G = M["Gtmp"]
        Bdtr, Bdtf, BdA, BdAc = (M["B_dtraw"], M["B_dtf"], M["B_dAf"],
                                 M["B_dAc"])
        b.op("scalar", lambda e, c, dtr=dtr: e.activation(
            G[0:HEADS, :], dtr, AF.Abs, bias=self.p_dtb[0:HEADS, :]),
            ins=[Bdtr, self.B_par], outs=[M["B_Gtmp"]], inc=ac)
        b.op("scalar", lambda e, c: e.activation(
            G[0:HEADS, :], G[0:HEADS, :], AF.Exp, scale=-1.0),
            ins=[M["B_Gtmp"]], outs=[M["B_Gtmp"]], inc=ac)
        b.op("scalar", lambda e, c: e.activation(
            G[0:HEADS, :], G[0:HEADS, :], AF.Ln, bias=self.cc[0:HEADS, 3:4]),
            ins=[M["B_Gtmp"], self.B_c], outs=[M["B_Gtmp"]], inc=ac)
        b.op("scalar", lambda e, c, dtr=dtr: e.activation(
            dtf, dtr, AF.Relu, bias=self.p_dtb[0:HEADS, :]),
            ins=[Bdtr, self.B_par], outs=[Bdtf], inc=ac)
        b.op("vector", lambda e, c: e.tensor_add(dtf, dtf, G[0:HEADS, :]),
             ins=[Bdtf, M["B_Gtmp"]], outs=[Bdtf], inc=dv)
        b.op("vector", lambda e, c: e.tensor_scalar(
            dA, dtf, self.p_negA[0:HEADS, :], None, ALU.mult),
            ins=[Bdtf, self.B_par], outs=[BdA], inc=dv)
        b.op("vector", lambda e, c: e.tensor_tensor_scan(
            dAc, dA, dA, 0.0, ALU.add, ALU.bypass),
            ins=[BdA], outs=[BdAc], inc=dv)
        b.op("scalar", lambda e, c: e.activation(
            M["dec"][0:HEADS, :], dAc, AF.Exp, scale=-1.0,
            bias=dAc[:, CHUNK - 1:CHUNK]),
            ins=[BdAc], outs=[M["B_dec"]], inc=ac)
        b.op("scalar", lambda e, c, ch=ch: e.activation(
            M["edAc"][ch][0:HEADS, :], dAc, AF.Exp),
            ins=[BdAc], outs=[M["B_edAc"][ch]], inc=ac)
        b.op("vector", lambda e, c: e.tensor_copy(M["dt16"][0:HEADS, :], dtf),
             ins=[Bdtf], outs=[M["B_dt16"]], inc=dv)
        if cci == 0:
            b.op("vector", lambda e, c: e.tensor_copy(
                M["logtot"][0:HEADS, :], dAc[:, CHUNK - 1:CHUNK]),
                ins=[BdAc], outs=[M["B_logtot"]], inc=dv)
        else:
            b.op("vector", lambda e, c: e.tensor_add(
                M["logtot"][0:HEADS, :], M["logtot"][0:HEADS, :],
                dAc[:, CHUNK - 1:CHUNK]),
                ins=[M["B_logtot"], BdAc], outs=[M["B_logtot"]], inc=dv)

        dtT3 = M["dtT"].rearrange("p (s h) -> p s h", s=2)
        decT3 = M["decT"].rearrange("p (s h) -> p s h", s=2)
        dAcT3 = M["dAcT"].rearrange("p (s h) -> p s h", s=2)
        BT3 = M["BT"].rearrange("p (s q) -> p s q", s=2)
        for ti in range(2):
            self.tposeb(dtT3[:, ti, :], M["B_dtT"],
                        M["dt16"][0:HEADS, ti * 128:(ti + 1) * 128],
                        [M["B_dt16"]], bank=4, r=HEADS, c=128)
            self.tposeb(decT3[:, ti, :], M["B_decT"],
                        M["dec"][0:HEADS, ti * 128:(ti + 1) * 128],
                        [M["B_dec"]], bank=5, r=HEADS, c=128)
            self.tposef(dAcT3[:, ti, :], M["B_dAcT"],
                        dAc[:, ti * 128:(ti + 1) * 128], [BdAc],
                        bank=6, r=HEADS, c=128)
            self.tposeb(BT3[:, ti, :], M["B_BT"],
                        xbc3[:, 32, col0 + ti * 128:col0 + (ti + 1) * 128],
                        [M["B_xbc"]], bank=4, r=128, c=128)
        self.tposeb(M["cdT"][0:1, 0:HEADS], M["B_cdT"],
                    M["edAc"][ch][0:HEADS, CHUNK - 1:CHUNK],
                    [M["B_edAc"][ch]], bank=5, r=HEADS, c=1)
        self.mm(ps[5][:, 0:HEADS], self.onesb[0:1, :], M["cdT"][0:1, 0:HEADS],
                psB[5], [M["B_cdT"], self.B_c], start=True, stop=True,
                first=True)
        b.op("vector", lambda e, c, ch=ch: e.tensor_copy(
            M["cdB"][ch], ps[5][:, 0:HEADS]),
            ins=[psB[5]], outs=[M["B_cdB"][ch]], inc=dv)
        b.op("vector", lambda e, c, ch=ch: e.tensor_copy(
            M["Cb"][ch], xbc3[:, 33, col0:col0 + CHUNK]),
            ins=[M["B_xbc"]], outs=[M["B_Cb"][ch]], inc=dv)

        xhT3 = M["xhT"].rearrange("p (s f) -> p s f", s=2)
        for ft in range(32):
            for w in range(2):
                self.tposeb(xhT3[:, w, ft * 128:(ft + 1) * 128], M["B_xhT"],
                            xbc3[:, ft, col0 + w * 128:col0 + (w + 1) * 128],
                            [M["B_xbc"]], bank=4 + (ft + w) % 2, r=128, c=128)

        for ti in range(2):
            self.mm(ps[6 + ti][:, 0:CHUNK],
                    xbc3[:, 32, col0 + ti * 128:col0 + (ti + 1) * 128],
                    M["Cb"][ch], psB[6 + ti], [M["B_xbc"], M["B_Cb"][ch]],
                    start=True, stop=True, first=True)
            b.op("vector", lambda e, c, ti=ti, ch=ch: e.tensor_mul(
                M["CBm"][ch][:, ti * CHUNK:(ti + 1) * CHUNK],
                self.trilm[:, ti * CHUNK:(ti + 1) * CHUNK],
                ps[6 + ti][:, 0:CHUNK]),
                ins=[psB[6 + ti], self.B_c], outs=[M["B_CBm"][ch]], inc=dv)

        for ti in range(2):
            b.op("vector", lambda e, c, ti=ti: e.tensor_mul(
                xhT3[:, ti, :].rearrange("p (h q) -> p h q", q=PD),
                xhT3[:, ti, :].rearrange("p (h q) -> p h q", q=PD),
                dtT3[:, ti, :].unsqueeze(2).broadcast_to([128, HEADS, PD])),
                ins=[M["B_xhT"], M["B_dtT"]], outs=[M["B_xhT"]], inc=dv)

        for pair in range(32):
            yb = psB[2 + pair % 2]
            yv = ps[2 + pair % 2]
            for half in range(2):
                h = 2 * pair + half
                lb = psB[h % 2]
                lv = ps[h % 2]
                self.rowbcast(h % 2, M["dAc"][0:HEADS, :], M["B_dAc"], h,
                              CHUNK, HEADS, bf=False, mv_bank=6 + h % 2)
                for ti in range(2):
                    b.op("vector", lambda e, c, h=h, ti=ti, lv=lv:
                         e.tensor_scalar(
                             G, lv[:, 0:CHUNK], dAcT3[:, ti, h:h + 1],
                             0.0, ALU.subtract, ALU.min),
                         ins=[lb, M["B_dAcT"]], outs=[M["B_Gtmp"]], inc=dv)
                    b.op("scalar", lambda e, c, ti=ti: e.activation(
                        M["Mt"][:, ti * CHUNK:(ti + 1) * CHUNK], G, AF.Exp),
                        ins=[M["B_Gtmp"]], outs=[M["B_Mt"]], inc=ac)
                    b.op("vector", lambda e, c, ti=ti, ch=ch: e.tensor_mul(
                        M["Mt"][:, ti * CHUNK:(ti + 1) * CHUNK],
                        M["Mt"][:, ti * CHUNK:(ti + 1) * CHUNK],
                        M["CBm"][ch][:, ti * CHUNK:(ti + 1) * CHUNK]),
                        ins=[M["B_Mt"], M["B_CBm"][ch]], outs=[M["B_Mt"]],
                        inc=dv)
                for ti in range(2):
                    self.mm(yv[half * 64:(half + 1) * 64, 0:CHUNK],
                            xhT3[:, ti, h * PD:(h + 1) * PD],
                            M["Mt"][:, ti * CHUNK:(ti + 1) * CHUNK],
                            yb, [M["B_xhT"], M["B_Mt"]],
                            start=(ti == 0), stop=(ti == 1),
                            first=(half == 0 and ti == 0))
            zgv = M["zg"][:, (pair % 2) * CHUNK:(pair % 2 + 1) * CHUNK]
            b.op("sync", lambda e, c, pair=pair, ch=ch, zgv=zgv: e.dma_start(
                out=zgv,
                in_=self.d_z[ch * HID + pair * 128:
                             ch * HID + (pair + 1) * 128, :]),
                ins=[self.Bd_z], outs=[M["B_zg"]], inc=ld, amt=16)
            b.op("scalar", lambda e, c, zgv=zgv: e.activation(
                zgv, zgv, AF.Silu),
                ins=[M["B_zg"]], outs=[M["B_zg"]], inc=ac)
            stg = M["ypstg"][:, (pair % 2) * CHUNK:(pair % 2 + 1) * CHUNK]
            Bstg = M["B_ypstg"]
            b.op("vector", lambda e, c, pair=pair, yv=yv, stg=stg:
                 e.scalar_tensor_tensor(
                     stg, xbc3[:, pair, col0:col0 + CHUNK],
                     self.p_dpair[:, pair:pair + 1], yv[:, 0:CHUNK],
                     ALU.mult, ALU.add),
                 ins=[M["B_xbc"], self.B_par, yb], outs=[Bstg], inc=dv)
            b.op("vector", lambda e, c, stg=stg, zgv=zgv: e.tensor_mul(
                stg, stg, zgv),
                ins=[Bstg, M["B_zg"]], outs=[Bstg], inc=dv)
            b.op("sync", lambda e, c, pair=pair, ch=ch, stg=stg: e.dma_start(
                out=self.d_yp[ch * HID + pair * 128:
                              ch * HID + (pair + 1) * 128, :],
                in_=stg),
                ins=[Bstg], outs=[self.Bd_yp], inc=ld, amt=16)

        for ti in range(2):
            b.op("vector", lambda e, c, ti=ti: e.tensor_mul(
                xhT3[:, ti, :].rearrange("p (h q) -> p h q", q=PD),
                xhT3[:, ti, :].rearrange("p (h q) -> p h q", q=PD),
                decT3[:, ti, :].unsqueeze(2).broadcast_to([128, HEADS, PD])),
                ins=[M["B_xhT"], M["B_decT"]], outs=[M["B_xhT"]], inc=dv)
        st, stb = M["st0"][run], M["B_st0"][run]
        for pair in range(32):
            sb = psB[6 + pair % 2]
            sv = ps[6 + pair % 2]
            for half in range(2):
                h = 2 * pair + half
                for ti in range(2):
                    self.mm(sv[:, half * 64:(half + 1) * 64],
                            BT3[:, ti, :], xhT3[:, ti, h * PD:(h + 1) * PD],
                            sb, [M["B_BT"], M["B_xhT"]],
                            start=(ti == 0), stop=(ti == 1),
                            first=(half == 0 and ti == 0))
            if cci == 0:
                b.op("vector", lambda e, c, pair=pair, st=st, sv=sv:
                     e.tensor_copy(
                         st[:, pair * 128:(pair + 1) * 128], sv[:, 0:128]),
                     ins=[sb], outs=[stb], inc=dv)
            else:
                tf = M["rowstg"][:, 128:256]
                cdv = M["cdB"][ch][:, 2 * pair:2 * pair + 2].unsqueeze(
                    2).broadcast_to([128, 2, PD])
                b.op("vector", lambda e, c, pair=pair, cdv=cdv, tf=tf, ch=ch:
                     e.tensor_mul(
                         tf.rearrange("p (a q) -> p a q", q=PD),
                         M["st0"][run].rearrange("p (h q) -> p h q", q=PD)
                         [:, 2 * pair:2 * pair + 2, :], cdv),
                     ins=[M["B_st0"][run], M["B_cdB"][ch]],
                     outs=[M["B_rowstg"]], inc=dv)
                b.op("vector", lambda e, c, tf=tf, sv=sv: e.tensor_add(
                    tf, tf, sv[:, 0:128]),
                    ins=[M["B_rowstg"], sb], outs=[M["B_rowstg"]], inc=dv)
                b.op("sync", lambda e, c, pair=pair, run=run, tf=tf:
                     e.dma_start(
                         out=self.d_stin[run][:, pair * 128:(pair + 1) * 128],
                         in_=tf),
                     ins=[M["B_rowstg"]], outs=[self.Bd_stin[run]],
                     inc=ld, amt=16)

        if cci == 1:
            self.tposef(M["logrow"][0:1, 0:HEADS], M["B_logrow"],
                        M["logtot"][0:HEADS, :], [M["B_logtot"]],
                        bank=5, r=HEADS, c=1)
            self.odma("sync", lambda e, c, run=run: e.dma_start(
                out=self.d_lrin[run], in_=M["logrow"][0:1, 0:HEADS]),
                ins=[M["B_logrow"]], outs=[self.Bd_lrin[run]])
            b.op("gpsimd", lambda e, c, run=run: e.collective_compute(
                "AllGather", ALU.bypass,
                replica_groups=[list(range(NCORE))],
                ins=[self.d_stin[run].opt()], outs=[self.d_stout[run].opt()]),
                ins=[self.Bd_stin[run]], inc=self.s_agst[run])
            self.Bd_stout[run].ready = [(self.s_agst[run], 1)]
            b.op("gpsimd", lambda e, c, run=run: e.collective_compute(
                "AllGather", ALU.bypass,
                replica_groups=[list(range(NCORE))],
                ins=[self.d_lrin[run].opt()], outs=[self.d_lrout[run].opt()]),
                ins=[self.Bd_lrin[run]], inc=self.s_aglr[run])
            self.Bd_lrout[run].ready = [(self.s_aglr[run], 1)]

    def mamba_prefix_coef(self, run):
        M, b = self.M, self.b
        ld, dv, ac = self.s_ld, self.s_dv, self.s_ac
        ps, psB = self.ps, self.psB
        Lr = M["Lr"]
        BLr = M["B_Lr"]
        stg16 = self.ln_rows[0:16, 0:64]
        Bstg16 = self.B_lnrows
        for half in range(2):
            self.odma("sync", lambda e, c, half=half: e.dma_start(
                out=stg16[half * 8:half * 8 + 8, :],
                in_=self.d_lrout[half]),
                ins=[self.Bd_lrout[half]], outs=[Bstg16])
        L = Lr[0:HEADS, 0:16]
        S = Lr[0:HEADS, 16:32]
        T1 = Lr[0:HEADS, 32:48]
        self.tposef(L, BLr, stg16, [Bstg16], bank=5, r=16, c=HEADS)
        b.op("vector", lambda e, c: e.tensor_tensor_scan(
            S[:, 0:8], L[:, 0:8], L[:, 0:8], 0.0, ALU.add, ALU.bypass),
            ins=[BLr], outs=[BLr], inc=dv)
        b.op("vector", lambda e, c: e.tensor_tensor_scan(
            T1[:, 0:8], L[:, 8:16], L[:, 8:16], 0.0, ALU.add, ALU.bypass),
            ins=[BLr], outs=[BLr], inc=dv)
        b.op("vector", lambda e, c: e.tensor_add(
            T1[:, 8:9], S[:, 7:8], T1[:, 7:8]),
            ins=[BLr], outs=[BLr], inc=dv)
        b.op("vector", lambda e, c: e.tensor_sub(
            S[:, 8:16], L[:, 8:16], T1[:, 0:8]),
            ins=[BLr], outs=[BLr], inc=dv)
        b.op("vector", lambda e, c: e.tensor_scalar(
            S[:, 8:16], S[:, 8:16], T1[:, 8:9], None, ALU.add),
            ins=[BLr], outs=[BLr], inc=dv)
        if run == 0:
            bv, Bbv = M["bvec"][0:HEADS, :], M["B_bvec"]
            b2, Bb2 = M["bv2"][0:HEADS, :], M["B_bv2"]
            it, Bit = M["iota16"][0:HEADS, :], M["B_iota16"]
            b.op("vector", lambda e, c: e.tensor_scalar(
                bv, it, 7.0, None, ALU.is_le),
                ins=[Bit], outs=[Bbv], inc=dv)
            b.op("vector", lambda e, c: e.tensor_scalar(
                b2, it, -2.0, 23.0, ALU.mult, ALU.add),
                ins=[Bit], outs=[Bb2], inc=dv)
            b.op("vector", lambda e, c: e.tensor_scalar(
                bv, bv, -1.0, 1.0, ALU.mult, ALU.add),
                ins=[Bbv], outs=[Bbv], inc=dv)
            b.op("vector", lambda e, c: e.tensor_mul(b2, b2, bv),
                 ins=[Bb2, Bbv], outs=[Bb2], inc=dv)
            b.op("vector", lambda e, c: e.tensor_add(bv, b2, it),
                 ins=[Bb2, Bit], outs=[Bbv], inc=dv)
        tcol = M["bv2"][0:HEADS, 0:1]
        if run == 0:
            b.op("vector", lambda e, c: e.tensor_copy(
                tcol, self.p_jcol[0:HEADS, :]),
                ins=[self.B_par], outs=[M["B_bv2"]], inc=dv)
        else:
            b.op("vector", lambda e, c: e.tensor_scalar(
                tcol, self.p_jcol[0:HEADS, :], -1.0, 15.0, ALU.mult, ALU.add),
                ins=[self.B_par], outs=[M["B_bv2"]], inc=dv)
        mk = Lr[0:HEADS, 32:48]
        b.op("vector", lambda e, c: e.tensor_scalar(
            mk, M["bvec"][0:HEADS, :], tcol, None, ALU.is_lt),
            ins=[M["B_bvec"], M["B_bv2"]], outs=[BLr], inc=dv)
        oh = M["ohtmp"][0:HEADS, :]
        b.op("vector", lambda e, c: e.tensor_scalar(
            oh, M["bvec"][0:HEADS, :], tcol, 1.0, ALU.subtract, ALU.add),
            ins=[M["B_bvec"], M["B_bv2"]], outs=[M["B_ohtmp"]], inc=dv)
        b.op("vector", lambda e, c: e.tensor_scalar(
            oh, oh, 0.0, None, ALU.is_equal),
            ins=[M["B_ohtmp"]], outs=[M["B_ohtmp"]], inc=dv)
        b.op("vector", lambda e, c: e.tensor_mul(oh, oh, S),
             ins=[M["B_ohtmp"], BLr], outs=[M["B_ohtmp"]], inc=dv)
        b.op("vector", lambda e, c: e.reduce_sum(
            M["scm1"][0:HEADS, :], oh, AXX),
            ins=[M["B_ohtmp"]], outs=[M["B_scm1"]], inc=dv)
        cf = M["ohtmp"][0:HEADS, :]
        b.op("vector", lambda e, c: e.tensor_scalar(
            cf, S, M["scm1"][0:HEADS, :], 0.0, ALU.subtract, ALU.max),
            ins=[BLr, M["B_scm1"]], outs=[M["B_ohtmp"]], inc=dv)
        b.op("scalar", lambda e, c: e.activation(cf, cf, AF.Exp, scale=-1.0),
             ins=[M["B_ohtmp"]], outs=[M["B_ohtmp"]], inc=ac)
        b.op("vector", lambda e, c: e.tensor_mul(cf, cf, mk),
             ins=[M["B_ohtmp"], BLr], outs=[M["B_ohtmp"]], inc=dv)
        b.op("vector", lambda e, c: e.tensor_copy(
            M["coef16"][0:HEADS, :], cf),
            ins=[M["B_ohtmp"]], outs=[M["B_coef16"]], inc=dv)
        self.tposeb(M["coefT"][0:16, 0:HEADS], M["B_coefT"],
                    M["coef16"][0:HEADS, :], [M["B_coef16"]],
                    bank=5, r=HEADS, c=16)
        for o in range(16):
            bank = 6 + o % 2
            self.rowbcast(bank, M["coefT"][0:16, 0:HEADS], M["B_coefT"], o,
                          HEADS, 16, bf=True, mv_bank=4)
            b.op("vector", lambda e, c, o=o, bank=bank: e.tensor_copy(
                M["coefB"][:, o * HEADS:(o + 1) * HEADS],
                self.ps[bank][:, 0:HEADS]),
                ins=[psB[bank]], outs=[M["B_coefB"]], inc=dv)

    def mamba_yhalf(self, run):
        M, b = self.M, self.b
        ld, dv = self.s_ld, self.s_dv
        self.mamba_prefix_coef(run)
        se, Bse = M["senter_f"], M["B_senter"]
        sm, Bsm = M["strm"], M["B_strm"]
        for o in range(16):
            src = self.d_stout[0] if o < 8 else self.d_stout[1]
            Bsrc = self.Bd_stout[0] if o < 8 else self.Bd_stout[1]
            slot = o if o < 8 else o - 8
            b.op("sync", lambda e, c, src=src, slot=slot: e.dma_start(
                out=sm, in_=src[slot * 128:slot * 128 + 128, :]),
                ins=[Bsrc], outs=[Bsm], inc=ld, amt=16)
            cb3 = M["coefB"][:, o * HEADS:(o + 1) * HEADS].unsqueeze(
                2).broadcast_to([128, HEADS, PD])
            if o == 0:
                b.op("vector", lambda e, c, cb3=cb3: e.tensor_mul(
                    se.rearrange("p (h q) -> p h q", q=PD),
                    sm.rearrange("p (h q) -> p h q", q=PD), cb3),
                    ins=[Bsm, M["B_coefB"]], outs=[Bse], inc=dv)
            else:
                b.op("vector", lambda e, c, cb3=cb3: e.tensor_mul(
                    sm.rearrange("p (h q) -> p h q", q=PD),
                    sm.rearrange("p (h q) -> p h q", q=PD), cb3),
                    ins=[Bsm, M["B_coefB"]], outs=[Bsm], inc=dv)
                b.op("vector", lambda e, c: e.tensor_add(se, se, sm),
                     ins=[Bse, Bsm], outs=[Bse], inc=dv)
        for cci in range(2):
            ch = run * 2 + cci
            if cci == 1:
                cdB3 = M["cdB"][run * 2].unsqueeze(2).broadcast_to(
                    [128, HEADS, PD])
                b.op("vector", lambda e, c: e.tensor_mul(
                    se.rearrange("p (h q) -> p h q", q=PD),
                    se.rearrange("p (h q) -> p h q", q=PD), cdB3),
                    ins=[Bse, M["B_cdB"][run * 2]], outs=[Bse], inc=dv)
                b.op("vector", lambda e, c: e.tensor_add(
                    se, se, M["st0"][run]),
                    ins=[Bse, M["B_st0"][run]], outs=[Bse], inc=dv)
            b.op("vector", lambda e, c: e.tensor_copy(M["prevb"], se),
                 ins=[Bse], outs=[M["B_prevb"]], inc=dv)
            self.yhalf_chunk(run, cci, ch, M["prevb"], M["B_prevb"])

    def yhalf_chunk(self, run, cci, ch, prevv, Bprev):
        M, b = self.M, self.b
        ld, dv, ac = self.s_ld, self.s_dv, self.s_ac
        ps, psB = self.ps, self.psB
        yg3 = M["ych"].rearrange("p (f t) -> p f t", t=CHUNK)
        Byg = M["B_ych"]
        Cw, BCw = M["Cw"], M["B_Cw"]
        for pair in range(32):
            yb = psB[2 + pair % 2]
            yv = ps[2 + pair % 2]
            for half in range(2):
                h = 2 * pair + half
                lb = psB[half]
                lv = ps[half]
                self.rowbcast(half, M["edAc"][ch][0:HEADS, :],
                              M["B_edAc"][ch], h, CHUNK, HEADS, bf=True,
                              mv_bank=6 + half)
                b.op("vector", lambda e, c, ch=ch, lv=lv: e.tensor_mul(
                    Cw, M["Cb"][ch], lv[:, 0:CHUNK]),
                    ins=[M["B_Cb"][ch], lb], outs=[BCw], inc=dv)
                self.mm(yv[half * 64:(half + 1) * 64, 0:CHUNK],
                        prevv[:, h * PD:(h + 1) * PD], Cw, yb,
                        [Bprev, BCw], start=True, stop=True,
                        first=(half == 0))
            zgv = M["zg2"][:, (pair % 2) * CHUNK:(pair % 2 + 1) * CHUNK]
            b.op("sync", lambda e, c, pair=pair, ch=ch, zgv=zgv: e.dma_start(
                out=zgv,
                in_=self.d_z[ch * HID + pair * 128:
                             ch * HID + (pair + 1) * 128, :]),
                ins=[self.Bd_z], outs=[M["B_zg2"]], inc=ld, amt=16)
            b.op("scalar", lambda e, c, zgv=zgv: e.activation(
                zgv, zgv, AF.Silu),
                ins=[M["B_zg2"]], outs=[M["B_zg2"]], inc=ac)
            stg = M["ypstg"][:, (pair % 2) * CHUNK:(pair % 2 + 1) * CHUNK]
            b.op("sync", lambda e, c, pair=pair, ch=ch, stg=stg: e.dma_start(
                out=stg,
                in_=self.d_yp[ch * HID + pair * 128:
                              ch * HID + (pair + 1) * 128, :]),
                ins=[self.Bd_yp], outs=[M["B_ypstg"]], inc=ld, amt=16)
            b.op("vector", lambda e, c, pair=pair, zgv=zgv, yv=yv:
                 e.tensor_mul(
                     yg3[:, pair, :], zgv, yv[:, 0:CHUNK]),
                 ins=[M["B_zg2"], yb], outs=[Byg], inc=dv)
            b.op("vector", lambda e, c, pair=pair, stg=stg: e.tensor_add(
                yg3[:, pair, :], yg3[:, pair, :], stg),
                ins=[Byg, M["B_ypstg"]], outs=[Byg], inc=dv)
        sq, Bsq = M["ysq"], M["B_ysq"]
        for ft in range(32):
            b.op("scalar", lambda e, c, ft=ft: e.activation(
                sq[:, 0:CHUNK], yg3[:, ft, :], AF.Square),
                ins=[Byg], outs=[Bsq], inc=ac)
            self.mm(ps[4][0:1, 0:CHUNK], self.onesb[:, 0:1], sq[:, 0:CHUNK],
                    psB[4], [Bsq, self.B_c], start=(ft == 0), stop=(ft == 31),
                    first=(ft == 0))
        rr = self.ln_rows[0:1, 0:CHUNK]
        Brr = self.B_lnrows
        b.op("vector", lambda e, c: e.tensor_scalar_mul(
            rr, ps[4][0:1, 0:CHUNK], 1.0 / HID),
            ins=[psB[4]], outs=[Brr], inc=dv)
        b.op("scalar", lambda e, c: e.activation(
            rr, rr, AF.Sqrt, bias=self.cc[0:1, 1:2]),
            ins=[Brr, self.B_c], outs=[Brr], inc=ac)
        b.op("vector", lambda e, c: e.reciprocal(rr, rr),
             ins=[Brr], outs=[Brr], inc=dv)
        self.mm(ps[5][:, 0:CHUNK], self.onesf[0:1, :], rr, psB[5],
                [Brr, self.B_c], start=True, stop=True, first=True)
        for ft in range(32):
            b.op("vector", lambda e, c, ft=ft: e.scalar_tensor_tensor(
                yg3[:, ft, :], yg3[:, ft, :],
                self.p_normw[:, ft:ft + 1], ps[5][:, 0:CHUNK],
                ALU.mult, ALU.mult),
                ins=[Byg, self.B_par, psB[5]], outs=[Byg], inc=dv)
        xr3 = self.xres.rearrange("p (f t) -> p f t", t=TPC)
        tc0 = run * BLK + cci * CHUNK
        for m in range(16):
            wb = M["wobuf"][m % 2]
            Bwb = M["B_wobuf"][m % 2]
            b.op("gpsimd", lambda e, c, m=m, wb=wb: e.dma_start(
                out=wb.rearrange("p (k q) -> p k q", k=32),
                in_=self.i_woutT[:, m * 128:(m + 1) * 128].rearrange(
                    "(k p) q -> p k q", p=128)),
                outs=[Bwb], inc=self.s_wg, amt=16)
            wb3 = wb.rearrange("p (k q) -> p k q", k=32)
            pb = psB[(m % 2)]
            pv = ps[(m % 2)]
            for k in range(32):
                self.mm(pv[:, 0:CHUNK], wb3[:, k, :], yg3[:, k, :],
                        pb, [Bwb, Byg], start=(k == 0), stop=(k == 31),
                        first=(k == 0))
            b.op("vector", lambda e, c, m=m, pv=pv: e.scalar_tensor_tensor(
                xr3[:, m, tc0:tc0 + CHUNK], pv[:, 0:CHUNK], 1.0,
                xr3[:, m, tc0:tc0 + CHUNK], ALU.mult, ALU.add),
                ins=[pb, self.B_x], outs=[self.B_x], inc=dv)


    def emit_dequant(self):
        nc, b = self.nc, self.b
        af, ab = self.af, self.ab
        dv, gp, ld, wg = self.s_dv, self.s_gp, self.s_ld, self.s_wg
        ps, psB = self.ps, self.psB
        af.mark()
        ab.mark()
        D = {}
        seed = self.phase_seed()

        def abuf(n, name, ar=None):
            D[name] = (ar or ab).alloc(n, name)
            bb = Buf(name)
            bb.freed = list(seed)
            D["B_" + name] = bb
            return D[name]

        wt = abuf(1408, "wt")
        wt32 = wt.bitcast(I32)
        nn = abuf(4096, "nn").bitcast(I32)
        s31 = abuf(4096, "s31").bitcast(I32)
        uu = abuf(4096, "uu").bitcast(I32)
        tt = abuf(4096, "tt").bitcast(I32)
        val = abuf(5632, "val")
        tstg = abuf(5632, "tstg")
        self.scf16 = nc.alloc_sbuf_tensor("scf16", [128, 64], F16)
        D["B_sc"] = Buf("sc")
        MININT = -2147483648

        def deq_chunk(Wc, wsrc, scsrc, vdst, Bwt, Bsc, Bval):
            for k in range(8):
                b.op("vector", lambda e, c, k=k: e.tensor_scalar(
                    nn[:, k * Wc:(k + 1) * Wc], wsrc, 4 * k, 7,
                    ALU.logical_shift_right, ALU.bitwise_and),
                    ins=[Bwt], outs=[D["B_nn"]], inc=dv)
                b.op("vector", lambda e, c, k=k: e.tensor_scalar(
                    s31[:, k * Wc:(k + 1) * Wc], wsrc, 28 - 4 * k, MININT,
                    ALU.logical_shift_left, ALU.bitwise_and),
                    ins=[Bwt], outs=[D["B_s31"]], inc=dv)
            n8 = 8 * Wc
            b.op("vector", lambda e, c: e.tensor_scalar(
                uu[:, 0:n8], nn[:, 0:n8], 2, 250, ALU.mult, ALU.add),
                ins=[D["B_nn"]], outs=[D["B_uu"]], inc=dv)
            b.op("vector", lambda e, c: e.tensor_scalar(
                tt[:, 0:n8], nn[:, 0:n8], 252, None, ALU.add),
                ins=[D["B_nn"]], outs=[D["B_tt"]], inc=dv)
            b.op("vector", lambda e, c: e.tensor_tensor(
                uu[:, 0:n8], uu[:, 0:n8], tt[:, 0:n8], ALU.min),
                ins=[D["B_uu"], D["B_tt"]], outs=[D["B_uu"]], inc=dv)
            b.op("vector", lambda e, c: e.tensor_scalar(
                tt[:, 0:n8], nn[:, 0:n8], 505, None, ALU.mult),
                ins=[D["B_nn"]], outs=[D["B_tt"]], inc=dv)
            b.op("vector", lambda e, c: e.tensor_tensor(
                uu[:, 0:n8], uu[:, 0:n8], tt[:, 0:n8], ALU.min),
                ins=[D["B_uu"], D["B_tt"]], outs=[D["B_uu"]], inc=dv)
            b.op("vector", lambda e, c: e.tensor_single_scalar(
                uu[:, 0:n8], uu[:, 0:n8], 22, ALU.logical_shift_left),
                ins=[D["B_uu"]], outs=[D["B_uu"]], inc=dv)
            b.op("vector", lambda e, c: e.tensor_tensor(
                uu[:, 0:n8], uu[:, 0:n8], s31[:, 0:n8], ALU.bitwise_or),
                ins=[D["B_uu"], D["B_s31"]], outs=[D["B_uu"]], inc=dv)
            ngrp = (Wc + 15) // 16
            scb = scsrc.unsqueeze(1).unsqueeze(3).broadcast_to(
                [128, 8, ngrp, 16])[:, :, :, :].rearrange(
                "p a g j -> p (a g j)") if False else None
            scv = scsrc.unsqueeze(1).unsqueeze(3)
            scv = scv.broadcast_to([128, 8, ngrp, 16])
            uv = uu[:, 0:n8].bitcast(F32).rearrange(
                "p (k g j) -> p k g j", k=8, j=16)
            vv = vdst.rearrange("p (g j k) -> p k g j", k=8, j=16)
            b.op("vector", lambda e, c, uv=uv, scv=scv, vv=vv: e.tensor_tensor(
                vv, uv, scv, ALU.mult),
                ins=[D["B_uu"], Bsc], outs=[Bval], inc=dv)

        for it in range(11):
            b.op("sync", lambda e, c, it=it: e.dma_start(
                out=wt32[:, 0:256],
                in_=self.i_gup[it * 128:(it + 1) * 128, :]),
                ins=[], outs=[D["B_wt"]], inc=ld, amt=16)
            b.op("sync", lambda e, c, it=it: e.dma_start(
                out=self.scf16.ap()[:, 0:16],
                in_=self.i_gus[it * 128:(it + 1) * 128, :]),
                ins=[], outs=[D["B_sc"]], inc=ld, amt=16)
            deq_chunk(256, wt32[:, 0:256], self.scf16.ap()[:, 0:16],
                      val[:, 0:2048], D["B_wt"], D["B_sc"], D["B_val"])
            for k in range(16):
                self.tposeb(tstg[:, k * 128:(k + 1) * 128], D["B_tstg"],
                            val[:, k * 128:(k + 1) * 128], [D["B_val"]],
                            bank=4 + k % 2, r=128, c=128)
            b.op("sync", lambda e, c, it=it: e.dma_start(
                out=self.d_guin.rearrange("(k p) q -> p k q", p=128)
                [:, :, it * 128:(it + 1) * 128],
                in_=tstg[:, 0:2048].rearrange("p (k q) -> p k q", q=128)),
                ins=[D["B_tstg"]], outs=[Buf("guin")], inc=ld, amt=16)
        n_guin = self.s_ld.n
        for ht in range(2):
            b.op("sync", lambda e, c, ht=ht: e.dma_start(
                out=wt32[:, 0:704],
                in_=self.i_dnp[ht * 128:(ht + 1) * 128, :]),
                ins=[], outs=[D["B_wt"]], inc=ld, amt=16)
            b.op("sync", lambda e, c, ht=ht: e.dma_start(
                out=self.scf16.ap()[:, 0:44],
                in_=self.i_dns[ht * 128:(ht + 1) * 128, :]),
                ins=[], outs=[D["B_sc"]], inc=ld, amt=16)
            for cc in range(3):
                Wc = (256, 256, 192)[cc]
                deq_chunk(Wc, wt32[:, cc * 256:cc * 256 + Wc],
                          self.scf16.ap()[:, cc * 16:cc * 16 + (Wc + 15) // 16],
                          val[:, cc * 2048:cc * 2048 + 8 * Wc],
                          D["B_wt"], D["B_sc"], D["B_val"])
            for k in range(44):
                self.tposeb(tstg[:, k * 128:(k + 1) * 128], D["B_tstg"],
                            val[:, k * 128:(k + 1) * 128], [D["B_val"]],
                            bank=4 + k % 2, r=128, c=128)
            b.op("sync", lambda e, c, ht=ht: e.dma_start(
                out=self.d_dnin.rearrange("(k p) q -> p k q", p=128)
                [:, :, ht * 128:(ht + 1) * 128],
                in_=tstg[:, 0:5632].rearrange("p (k q) -> p k q", q=128)),
                ins=[D["B_tstg"]], outs=[Buf("dnin")], inc=ld, amt=16)
        n_dnin = self.s_ld.n
        self.Bd_guout = Buf("d_guout")
        self.Bd_dnout = Buf("d_dnout")
        b.raw("gpsimd", lambda e, c: e.collective_compute(
            "AllGather", ALU.bypass, replica_groups=[list(range(NCORE))],
            ins=[self.d_guin.opt()], outs=[self.d_guout.opt()]),
            waits=[(self.s_ld, n_guin)], inc=self.s_aggu)
        self.Bd_guout.ready = [(self.s_aggu, 1)]
        b.raw("gpsimd", lambda e, c: e.collective_compute(
            "AllGather", ALU.bypass, replica_groups=[list(range(NCORE))],
            ins=[self.d_dnin.opt()], outs=[self.d_dnout.opt()]),
            waits=[(self.s_ld, n_dnin)], inc=self.s_agdn)
        self.Bd_dnout.ready = [(self.s_agdn, 1)]
        af.release()
        ab.release()

    def emit_ffn(self, use):
        nc, b = self.nc, self.b
        af, ab = self.af, self.ab
        dv, ac, ld, wg = self.s_dv, self.s_ac, self.s_ld, self.s_wg
        ps, psB = self.ps, self.psB
        af.mark()
        ab.mark()
        F = {}
        seed = self.phase_seed()

        def abuf(n, name):
            F[name] = ab.alloc(n, name)
            bb = Buf(name)
            bb.freed = list(seed)
            F["B_" + name] = bb
            return F[name]

        xlnT = abuf(16 * BLK, "xlnTf")
        abuf(1024, "lnstgf")
        hbuf = abuf(44 * BLK, "hbuf")
        def mkbuf(name):
            bb = Buf(name)
            bb.freed = list(seed)
            return bb

        F["wgu"] = [ab.alloc(16 * 128, f"wgu{i}") for i in range(4)]
        F["B_wgu"] = [mkbuf(f"wgu{i}") for i in range(4)]
        F["wdn"] = [ab.alloc(22 * 128, f"wdn{i}") for i in range(2)]
        F["B_wdn"] = [mkbuf(f"wdn{i}") for i in range(2)]
        hstg = abuf(BLK, "hstg")
        ysb = F["xlnTf"]
        B_ysb = F["B_xlnTf"]
        ysq = abuf(BLK, "ysqf")
        xr3 = self.xres.rearrange("p (f t) -> p f t", t=TPC)
        xlnT3 = xlnT.rearrange("p (f t) -> p f t", t=BLK)
        hb3 = hbuf.rearrange("p (f t) -> p f t", t=BLK)
        ysb3 = ysb.rearrange("p (f t) -> p f t", t=BLK)

        for run in range(2):
            tc0 = run * BLK
            self.emit_ln(
                [(xlnT3[:, ft, :], F["B_xlnTf"]) for ft in range(16)],
                [(xr3[:, ft, tc0:tc0 + BLK], self.B_x) for ft in range(16)],
                BLK, None, F["lnstgf"], F["B_lnstgf"])
            for gi in range(44):
                bufs = []
                for half, grow in ((0, gi * 128), (1, FI + gi * 128)):
                    wbi = (gi % 2) * 2 + half
                    wb = F["wgu"][wbi]
                    Bwb = F["B_wgu"][wbi]
                    r = grow // GU_SH
                    lc = grow - r * GU_SH
                    b.op("gpsimd", lambda e, c, wb=wb, r=r, lc=lc: e.dma_start(
                        out=wb.rearrange("p (k q) -> p k q", k=16),
                        in_=self.d_guout[r * DM:(r + 1) * DM, lc:lc + 128]
                        .rearrange("(k p) q -> p k q", p=128)),
                        ins=[self.Bd_guout], outs=[Bwb], inc=wg, amt=16)
                    bufs.append((wb, Bwb))
                pg, pu = psB[gi % 2], psB[2 + gi % 2]
                vg, vu = ps[gi % 2], ps[2 + gi % 2]
                for half, (wb, Bwb) in enumerate(bufs):
                    pv = (vg, vu)[half]
                    pb = (pg, pu)[half]
                    wb3 = wb.rearrange("p (k q) -> p k q", k=16)
                    for k in range(16):
                        self.mm(pv[:, 0:BLK], wb3[:, k, :], xlnT3[:, k, :],
                                pb, [Bwb, F["B_xlnTf"]], start=(k == 0),
                                stop=(k == 15), first=(k == 0))
                b.op("scalar", lambda e, c, vg=vg: e.activation(
                    hstg, vg[:, 0:BLK], AF.Silu),
                    ins=[pg], outs=[F["B_hstg"]], inc=ac)
                b.op("vector", lambda e, c, gi=gi, vu=vu: e.tensor_mul(
                    hb3[:, gi, :], hstg, vu[:, 0:BLK]),
                    ins=[F["B_hstg"], pu], outs=[F["B_hbuf"]], inc=dv)
            for m2 in range(16):
                r = m2 // 2
                lc = (m2 % 2) * 128
                for kh in range(2):
                    wb = F["wdn"][kh]
                    Bwb = F["B_wdn"][kh]
                    b.op("gpsimd", lambda e, c, wb=wb, r=r, lc=lc, kh=kh:
                         e.dma_start(
                             out=wb.rearrange("p (k q) -> p k q", q=128),
                             in_=self.d_dnout[r * FI + kh * 22 * 128:
                                              r * FI + (kh + 1) * 22 * 128,
                                              lc:lc + 128].rearrange(
                                 "(k p) q -> p k q", p=128)),
                         ins=[self.Bd_dnout], outs=[Bwb], inc=wg, amt=16)
                pd = psB[6 + m2 % 2]
                vd = ps[6 + m2 % 2]
                for k2 in range(44):
                    wb = F["wdn"][k2 // 22]
                    Bwb = F["B_wdn"][k2 // 22]
                    wb3 = wb.rearrange("p (k q) -> p k q", q=128)
                    self.mm(vd[:, 0:BLK], wb3[:, k2 % 22, :], hb3[:, k2, :],
                            pd, [Bwb, F["B_hbuf"]], start=(k2 == 0),
                            stop=(k2 == 43), first=(k2 == 0))
                b.op("vector", lambda e, c, m2=m2, vd=vd: e.tensor_copy(
                    ysb3[:, m2, :], vd[:, 0:BLK]),
                    ins=[pd], outs=[B_ysb], inc=dv)
            for ft in range(16):
                b.op("scalar", lambda e, c, ft=ft: e.activation(
                    ysq, ysb3[:, ft, :], AF.Square),
                    ins=[B_ysb], outs=[F["B_ysqf"]], inc=ac)
                self.mm(ps[4][0:1, 0:BLK], self.onesb[:, 0:1], ysq,
                        psB[4], [F["B_ysqf"], self.B_c], start=(ft == 0),
                        stop=(ft == 15), first=(ft == 0))
            rr = self.ln_rows[0:1, 0:BLK]
            Brr = self.B_lnrows
            b.op("vector", lambda e, c: e.tensor_scalar_mul(
                rr, ps[4][0:1, 0:BLK], 1.0 / DM),
                ins=[psB[4]], outs=[Brr], inc=dv)
            b.op("scalar", lambda e, c: e.activation(
                rr, rr, AF.Sqrt, bias=self.cc[0:1, 2:3]),
                ins=[Brr, self.B_c], outs=[Brr], inc=ac)
            b.op("vector", lambda e, c: e.reciprocal(rr, rr),
                 ins=[Brr], outs=[Brr], inc=dv)
            self.mm(ps[5][:, 0:BLK], self.onesf[0:1, :], rr, psB[5],
                    [Brr, self.B_c], start=True, stop=True, first=True)
            for ft in range(16):
                b.op("vector", lambda e, c, ft=ft: e.tensor_mul(
                    ysq, ysb3[:, ft, :], ps[5][:, 0:BLK]),
                    ins=[B_ysb, psB[5]], outs=[F["B_ysqf"]], inc=dv)
                b.op("vector", lambda e, c, ft=ft, tc0=tc0: e.tensor_add(
                    xr3[:, ft, tc0:tc0 + BLK], xr3[:, ft, tc0:tc0 + BLK],
                    ysq),
                    ins=[F["B_ysqf"], self.B_x], outs=[self.B_x], inc=dv)
        af.release()
        ab.release()


    def emit_attn(self):
        nc, b = self.nc, self.b
        af, ab = self.af, self.ab
        dv, ac, ld, wg, gp = (self.s_dv, self.s_ac, self.s_ld, self.s_wg,
                              self.s_gp)
        ps, psB, pp = self.ps, self.psB, self.pp
        af.mark()
        ab.mark()
        A = {}
        seed = self.phase_seed()

        def abuf(n, name, ar=None):
            A[name] = (ar or ab).alloc(n, name)
            bb = Buf(name)
            bb.freed = list(seed)
            A["B_" + name] = bb
            return A[name]

        xln = abuf(16 * TPC, "xln")
        abuf(1024, "lnstga")
        qa = abuf(1024, "qa")
        qsb = abuf(1024, "qsb")
        kc = abuf(1024, "kc")

        wqa = abuf(16 * 64, "wqa")
        wqb = abuf(16 * 128, "wqb")
        A["wkva"] = [ab.alloc(16 * 128, f"wkva{i}") for i in range(2)]
        A["B_wkva"] = []
        for i in range(2):
            bb = Buf(f"wkva{i}")
            bb.freed = list(seed)
            A["B_wkva"].append(bb)
        wkvb = abuf(16 * 128, "wkvb")
        cos64 = abuf(TPC, "cos64")
        sin64 = abuf(TPC, "sin64")
        kst = abuf(TPC, "kst")
        vst = abuf(TPC, "vst")
        abuf(TPC, "vfm")
        abuf(BLK, "pacc")
        abuf(TPC, "paccS", af)
        qh = abuf(TPC, "qh")
        A["pb"] = [ab.alloc(BLK, f"pb{i}") for i in range(4)]
        A["B_pb"] = []
        for i in range(4):
            bb = Buf(f"pb{i}")
            bb.freed = list(seed)
            A["B_pb"].append(bb)
        rlb = abuf(TPC, "rlb")
        osb = abuf(TPC, "osb")
        kcsq = rlb
        A["kcsq"] = kcsq
        A["B_kcsq"] = A["B_rlb"]
        zrowf = abuf(64, "zrowf", af)
        zrow = zrowf.bitcast(BF16)
        A["zrow"] = zrow
        A["B_zrow"] = A["B_zrowf"]
        theta = abuf(TPC, "theta", af)
        th2 = abuf(TPC, "th2", af)
        qi32 = abuf(TPC, "qi32", af)
        invf = abuf(2, "invf", af)
        sgn = abuf(1, "sgn", af)
        poss = self.ln_rows[0:1, 1024:2048]
        A["poss"] = poss
        A["B_poss"] = self.B_lnrows
        bnd = abuf(8, "bnd", af)
        negC = abuf(1, "negC", af)
        xr3 = self.xres.rearrange("p (f t) -> p f t", t=TPC)
        xln3 = xln.rearrange("p (f t) -> p f t", t=TPC)

        b.op("gpsimd", lambda e, c: e.memset(zrow, 0.0),
             outs=[A["B_zrow"]], inc=gp)

        b.op("gpsimd", lambda e, c: e.iota(
            invf[0:HEADS, 0:1], pattern=[[1, 1]], base=0, channel_multiplier=1,
            allow_small_or_imprecise_dtypes=True),
            outs=[A["B_invf"]], inc=gp)
        b.op("vector", lambda e, c: e.tensor_scalar(
            sgn[0:HEADS, :], invf[0:HEADS, 0:1], 31.5, None, ALU.is_gt),
            ins=[A["B_invf"]], outs=[A["B_sgn"]], inc=dv)
        b.op("vector", lambda e, c: e.scalar_tensor_tensor(
            invf[0:HEADS, 1:2], sgn[0:HEADS, :], -32.0, invf[0:HEADS, 0:1],
            ALU.mult, ALU.add),
            ins=[A["B_sgn"], A["B_invf"]], outs=[A["B_invf"]], inc=dv)
        import math as _m
        b.op("scalar", lambda e, c: e.activation(
            invf[0:HEADS, 0:1], invf[0:HEADS, 1:2], AF.Exp,
            scale=-_m.log(10000.0) / 32.0),
            ins=[A["B_invf"]], outs=[A["B_invf"]], inc=ac)
        b.op("gpsimd", lambda e, c: e.iota(
            poss[0:1, 0:TPC], pattern=[[0, 2], [1, BLK]], base=0,
            channel_multiplier=0, allow_small_or_imprecise_dtypes=True),
            outs=[A["B_poss"]], inc=gp)
        b.op("vector", lambda e, c: e.tensor_scalar_mul(
            bnd[0:1, 0:1], self.p_jcol[0:1, :], 512.0),
            ins=[self.B_par], outs=[A["B_bnd"]], inc=dv)
        b.op("vector", lambda e, c: e.tensor_scalar(
            bnd[0:1, 1:2], bnd[0:1, 0:1], -1.0, 7680.0, ALU.mult, ALU.add),
            ins=[A["B_bnd"]], outs=[A["B_bnd"]], inc=dv)
        b.op("vector", lambda e, c: e.tensor_scalar(
            poss[0:1, 0:BLK], poss[0:1, 0:BLK], bnd[0:1, 0:1], None, ALU.add),
            ins=[A["B_poss"], A["B_bnd"]], outs=[A["B_poss"]], inc=dv)
        b.op("vector", lambda e, c: e.tensor_scalar(
            poss[0:1, BLK:TPC], poss[0:1, BLK:TPC], bnd[0:1, 1:2], None,
            ALU.add),
            ins=[A["B_poss"], A["B_bnd"]], outs=[A["B_poss"]], inc=dv)
        self.tposef(poss[0:1, 0:0 + HEADS] if False else
                    self.ln_rows[0:1, 0:HEADS], self.B_lnrows,
                    invf[0:HEADS, 0:1], [A["B_invf"]], bank=5, r=HEADS, c=1)
        for half in range(2):
            self.mm(ps[half][0:HEADS, 0:BLK], self.ln_rows[0:1, 0:HEADS],
                    poss[0:1, half * BLK:(half + 1) * BLK], psB[half],
                    [self.B_lnrows, A["B_poss"]], start=True, stop=True,
                    first=True)
            b.op("vector", lambda e, c, half=half: e.tensor_copy(
                theta[0:HEADS, half * BLK:(half + 1) * BLK],
                ps[half][0:HEADS, 0:BLK]),
                ins=[psB[half]], outs=[A["B_theta"]], inc=dv)

        def trig(dst, dbuf, offs):
            TWO_PI = 2.0 * _m.pi
            b.op("vector", lambda e, c: e.tensor_scalar(
                th2[0:HEADS, :], theta[0:HEADS, :], 1.0 / TWO_PI,
                0.5 + offs / TWO_PI, ALU.mult, ALU.add),
                ins=[A["B_theta"]], outs=[A["B_th2"]], inc=dv)
            b.op("vector", lambda e, c: e.tensor_scalar(
                qi32[0:HEADS, :].bitcast(I32) if False else
                qi32[0:HEADS, :], th2[0:HEADS, :], 0.0, None, ALU.add),
                ins=[A["B_th2"]], outs=[A["B_qi32"]], inc=dv)
            b.op("vector", lambda e, c: e.tensor_copy(
                th2[0:HEADS, :].bitcast(I32), qi32[0:HEADS, :]),
                ins=[A["B_qi32"]], outs=[A["B_th2"]], inc=dv)
            b.op("vector", lambda e, c: e.scalar_tensor_tensor(
                th2[0:HEADS, :], th2[0:HEADS, :].bitcast(I32), -TWO_PI,
                theta[0:HEADS, :], ALU.mult, ALU.add),
                ins=[A["B_th2"], A["B_theta"]], outs=[A["B_th2"]], inc=dv)
            if offs != 0.0:
                b.op("vector", lambda e, c: e.tensor_scalar_add(
                    th2[0:HEADS, :], th2[0:HEADS, :], offs),
                    ins=[A["B_th2"]], outs=[A["B_th2"]], inc=dv)
            b.op("vector", lambda e, c: e.tensor_scalar(
                th2[0:HEADS, :], th2[0:HEADS, :], _m.pi, -_m.pi,
                ALU.min, ALU.max),
                ins=[A["B_th2"]], outs=[A["B_th2"]], inc=dv)
            b.op("scalar", lambda e, c, dst=dst: e.activation(
                dst[0:HEADS, :], th2[0:HEADS, :], AF.Sin),
                ins=[A["B_th2"]], outs=[dbuf], inc=ac)

        trig(sin64, A["B_sin64"], 0.0)
        trig(cos64, A["B_cos64"], _m.pi / 2.0)

        self.emit_ln(
            [(xln3[:, ft, 0:BLK], A["B_xln"]) for ft in range(16)],
            [(xr3[:, ft, 0:BLK], self.B_x) for ft in range(16)],
            BLK, (self.grows[0:1, 0:DM], self.B_rows), A["lnstga"],
            A["B_lnstga"])
        self.emit_ln(
            [(xln3[:, ft, BLK:TPC], A["B_xln"]) for ft in range(16)],
            [(xr3[:, ft, BLK:TPC], self.B_x) for ft in range(16)],
            BLK, (self.grows[0:1, 0:DM], self.B_rows), A["lnstga"],
            A["B_lnstga"])
        b.op("gpsimd", lambda e, c: e.dma_start(
            out=wqa.rearrange("p (k q) -> p k q", k=16),
            in_=self.i_wqa.rearrange("(k p) q -> p k q", p=128)),
            outs=[A["B_wqa"]], inc=wg, amt=16)
        wqa3 = wqa.rearrange("p (k q) -> p k q", k=16)
        for half in range(2):
            pb_ = psB[half]
            pv = ps[half]
            for k in range(16):
                self.mm(pv[0:QR, 0:BLK], wqa3[:, k, :],
                        xln3[:, k, half * BLK:(half + 1) * BLK], pb_,
                        [A["B_wqa"], A["B_xln"]], start=(k == 0),
                        stop=(k == 15), first=(k == 0))
            b.op("vector", lambda e, c, half=half, pv=pv: e.tensor_copy(
                qa[0:QR, half * BLK:(half + 1) * BLK], pv[0:QR, 0:BLK]),
                ins=[pb_], outs=[A["B_qa"]], inc=dv)
        b.op("gpsimd", lambda e, c: e.dma_start(
            out=wqb[0:QR, :].rearrange("p (k q) -> p k q", k=16),
            in_=self.i_wqb.rearrange("a (k q) -> a k q", q=128)),
            outs=[A["B_wqb"]], inc=wg, amt=16)
        wqb3 = wqb[0:QR, :].rearrange("p (k q) -> p k q", k=16)
        maxq = bnd[0:1, 2:3]

        sgnc = bnd[0:HEADS, 4:5]
        b.op("vector", lambda e, c: e.tensor_scalar(
            sgnc, sgn[0:HEADS, :], 2.0, -1.0, ALU.mult, ALU.add),
            ins=[A["B_sgn"]], outs=[A["B_bnd"]], inc=dv)
        b.op("vector", lambda e, c: e.tensor_scalar(
            sin64[0:HEADS, :], sin64[0:HEADS, :], sgnc, None, ALU.mult),
            ins=[A["B_sin64"], A["B_bnd"]], outs=[A["B_sin64"]], inc=dv)

        rotb = th2
        Brotb = A["B_th2"]

        def rope_evac(dst, dbuf, srcv, srcb):
            for half in range(2):
                sl = slice(half * BLK, (half + 1) * BLK)
                sv, sb_ = srcv[half], srcb[half]
                shmask = (list(range(8, 16)) + list(range(0, 8))
                          + list(range(16, 32)))
                b.op("vector", lambda e, c, sv=sv, shmask=shmask:
                     e.stream_shuffle(
                         rotb[0:HEADS, 0:BLK], sv[0:HEADS, 0:BLK], shmask),
                     ins=[sb_], outs=[Brotb], inc=dv)
                b.op("vector", lambda e, c, sl=sl: e.tensor_mul(
                    rotb[0:HEADS, 0:BLK], rotb[0:HEADS, 0:BLK],
                    sin64[0:HEADS, sl]),
                    ins=[Brotb, A["B_sin64"]], outs=[Brotb], inc=dv)
                b.op("vector", lambda e, c, sv=sv, sl=sl, dst=dst:
                     e.tensor_mul(dst[0:HEADS, sl], sv[0:HEADS, 0:BLK],
                                  cos64[0:HEADS, sl]),
                     ins=[sb_, A["B_cos64"]], outs=[dbuf], inc=dv)
                b.op("vector", lambda e, c, sl=sl, dst=dst: e.tensor_add(
                    dst[0:HEADS, sl], dst[0:HEADS, sl], rotb[0:HEADS, 0:BLK]),
                    ins=[Brotb, dbuf], outs=[dbuf], inc=dv)
                b.op("vector", lambda e, c, sv=sv, sl=sl, dst=dst:
                     e.tensor_copy(dst[HEADS:128, sl], sv[HEADS:128, 0:BLK]),
                     ins=[sb_], outs=[dbuf], inc=dv)

        for h in range(NH):
            pbs = [psB[2], psB[3]]
            pvs = [ps[2], ps[3]]
            for half in range(2):
                self.mm(pvs[half][:, 0:BLK], wqb3[:, h, :],
                        qa[0:QR, half * BLK:(half + 1) * BLK], pbs[half],
                        [A["B_wqb"], A["B_qa"]], start=True, stop=True,
                        first=True)
            rope_evac(qsb, A["B_qsb"], pvs, pbs)
            b.op("scalar", lambda e, c: e.activation(
                kcsq, qsb, AF.Square), ins=[A["B_qsb"]],
                outs=[A["B_kcsq"]], inc=ac)
            for half in range(2):
                self.mm(ps[4 + half][0:1, 0:BLK], self.onesb[:, 0:1],
                        kcsq[:, half * BLK:(half + 1) * BLK], psB[4 + half],
                        [A["B_kcsq"], self.B_c], start=True, stop=True,
                        first=True)
                b.op("vector", lambda e, c, half=half: e.reduce_max(
                    bnd[0:1, 3:4], ps[4 + half][0:1, 0:BLK], AXX),
                    ins=[psB[4 + half]], outs=[A["B_bnd"]], inc=dv)
                if h == 0 and half == 0:
                    b.op("vector", lambda e, c: e.tensor_copy(
                        maxq, bnd[0:1, 3:4]),
                        ins=[A["B_bnd"]], outs=[A["B_bnd"]], inc=dv)
                else:
                    b.op("vector", lambda e, c: e.tensor_tensor(
                        maxq, maxq, bnd[0:1, 3:4], ALU.max),
                        ins=[A["B_bnd"]], outs=[A["B_bnd"]], inc=dv)
            b.op("sync", lambda e, c, h=h: e.dma_start(
                out=self.d_q[h * DH:(h + 1) * DH, :], in_=qsb),
                ins=[A["B_qsb"]], outs=[self.Bd_q], inc=ld, amt=16)

        self.emit_ln(
            [(xln3[:, ft, 0:BLK], A["B_xln"]) for ft in range(16)],
            [(xr3[:, ft, 0:BLK], self.B_x) for ft in range(16)],
            BLK, (self.grows[0:1, DM:2 * DM], self.B_rows), A["lnstga"],
            A["B_lnstga"])
        self.emit_ln(
            [(xln3[:, ft, BLK:TPC], A["B_xln"]) for ft in range(16)],
            [(xr3[:, ft, BLK:TPC], self.B_x) for ft in range(16)],
            BLK, (self.grows[0:1, DM:2 * DM], self.B_rows), A["lnstga"],
            A["B_lnstga"])
        vsb = qsb
        B_vsb = A["B_qsb"]
        vflat = [self.d_kvin[r][EMB:2 * EMB, :].flatten()
                 .rearrange("(t d) -> t d", d=EMB) for r in range(2)]
        Bd_kvin = [Buf("kvin0"), Buf("kvin1")]
        b.op("gpsimd", lambda e, c: e.dma_start(
            out=wkvb.rearrange("p (k q) -> p k q", k=16),
            in_=self.i_wkvb.rearrange("a (k q) -> a k q", q=128)),
            outs=[A["B_wkvb"]], inc=wg, amt=16)
        for h2 in range(2):
            b.op("scalar", lambda e, c, h2=h2: e.activation(
                kcsq[:, 0:TPC], wkvb[0:KR, h2 * TPC:(h2 + 1) * TPC],
                AF.Square),
                ins=[A["B_wkvb"]], outs=[A["B_kcsq"]], inc=ac)
            for q4 in range(2):
                self.mm(ps[5][0:1, 0:BLK], self.onesb[0:KR, 0:1],
                        kcsq[0:KR, q4 * 512:(q4 + 1) * 512], psB[5],
                        [A["B_kcsq"], self.B_c], start=(h2 == 0 and q4 == 0),
                        stop=(h2 == 1 and q4 == 1),
                        first=(h2 == 0 and q4 == 0))
        b.op("vector", lambda e, c: e.reduce_sum(
            bnd[0:1, 5:6], ps[5][0:1, 0:BLK], AXX),
            ins=[psB[5]], outs=[A["B_bnd"]], inc=dv)
        for m in range(17):
            wb = A["wkva"][m % 2]
            Bwb = A["B_wkva"][m % 2]
            b.op("gpsimd", lambda e, c, m=m, wb=wb: e.dma_start(
                out=wb.rearrange("p (k q) -> p k q", k=16),
                in_=self.i_wkva[:, m * 128:(m + 1) * 128].rearrange(
                    "(k p) q -> p k q", p=128)),
                outs=[Bwb], inc=wg, amt=16)
            vstg = A["pb"][0]
            B_vstg = A["B_pb"][0]
            wb3 = wb.rearrange("p (k q) -> p k q", k=16)
            for half in range(2):
                pb_ = psB[half]
                pv = ps[half]
                for k in range(16):
                    self.mm(pv[:, 0:BLK], wb3[:, k, :],
                            xln3[:, k, half * BLK:(half + 1) * BLK], pb_,
                            [Bwb, A["B_xln"]], start=(k == 0),
                            stop=(k == 15), first=(k == 0))
                if m == 0:
                    b.op("vector", lambda e, c, half=half, pv=pv:
                         e.tensor_copy(
                             kc[:, half * BLK:(half + 1) * BLK],
                             pv[:, 0:BLK]),
                         ins=[pb_], outs=[A["B_kc"]], inc=dv)
                else:
                    hh = m - 1
                    b.op("vector", lambda e, c, pv=pv:
                         e.tensor_copy(vsb[:, 0:BLK], pv[:, 0:BLK]),
                         ins=[pb_], outs=[B_vsb], inc=dv)
                    b.op("sync", lambda e, c, half=half, hh=hh:
                         e.dma_start(
                             out=self.d_kvin[half][EMB + hh * DH:
                                                   EMB + (hh + 1) * DH, :],
                             in_=vsb[:, 0:BLK]),
                         ins=[B_vsb], outs=[Bd_kvin[half]], inc=ld,
                         amt=16)
        b.op("scalar", lambda e, c: e.activation(
            kcsq, kc, AF.Square), ins=[A["B_kc"]], outs=[A["B_kcsq"]], inc=ac)
        for half in range(2):
            self.mm(ps[4][0:1, 0:BLK], self.onesb[:, 0:1],
                    kcsq[:, half * BLK:(half + 1) * BLK], psB[4],
                    [A["B_kcsq"], self.B_c], start=True, stop=True, first=True)
            b.op("vector", lambda e, c: e.reduce_max(
                bnd[0:1, 3:4], ps[4][0:1, 0:BLK], AXX),
                ins=[psB[4]], outs=[A["B_bnd"]], inc=dv)
            b.op("vector", lambda e, c: e.tensor_tensor(
                bnd[0:1, 3:4], bnd[0:1, 3:4], bnd[0:1, 5:6], ALU.mult),
                ins=[A["B_bnd"]], outs=[A["B_bnd"]], inc=dv)
            b.op("vector", lambda e, c: e.tensor_scalar_mul(
                bnd[0:1, 3:4], bnd[0:1, 3:4], 1.02),
                ins=[A["B_bnd"]], outs=[A["B_bnd"]], inc=dv)
            b.op("vector", lambda e, c: e.memset(
                self.ln_rows[0:1, 0:BLK], 0.0),
                ins=[], outs=[self.B_lnrows], inc=dv)
            b.op("vector", lambda e, c: e.tensor_copy(
                self.ln_rows[0:1, 0:1], bnd[0:1, 3:4]),
                ins=[A["B_bnd"]], outs=[self.B_lnrows], inc=dv)
            self.odma("gpsimd", lambda e, c, half=half: e.dma_start(
                out=self.d_kvin[half][4096:4097, :],
                in_=self.ln_rows[0:1, 0:BLK]),
                ins=[self.B_lnrows], outs=[Bd_kvin[half]])
        wkvb3 = wkvb.rearrange("p (k q) -> p k q", k=16)
        for h in range(NH):
            pbs = [psB[2], psB[3]]
            pvs = [ps[2], ps[3]]
            for half in range(2):
                self.mm(pvs[half][:, 0:BLK], wkvb3[:, h, :],
                        kc[:, half * BLK:(half + 1) * BLK], pbs[half],
                        [A["B_wkvb"], A["B_kc"]], start=True, stop=True,
                        first=True)
            rope_evac(qsb, A["B_qsb"], pvs, pbs)
            for half in range(2):
                b.op("sync", lambda e, c, h=h, half=half: e.dma_start(
                    out=self.d_kvin[half][h * DH:(h + 1) * DH, :],
                    in_=qsb[:, half * BLK:(half + 1) * BLK]),
                    ins=[A["B_qsb"]], outs=[Bd_kvin[half]], inc=ld, amt=16)
        self.Bd_kvout = [Buf("kvout0"), Buf("kvout1")]
        for r in range(2):
            self.b.op("gpsimd", lambda e, c, r=r: e.collective_compute(
                "AllGather", ALU.bypass, replica_groups=[list(range(NCORE))],
                ins=[self.d_kvin[r].opt()], outs=[self.d_kvout[r].opt()]),
                ins=[Bd_kvin[r]], inc=self.s_agkv[r])
            self.Bd_kvout[r].ready = [(self.s_agkv[r], 1)]
        stg8 = self.ln_rows[0:8, 0:2]
        for r in range(2):
            self.odma("gpsimd", lambda e, c, r=r: e.dma_start(
                out=stg8[:, r:r + 1],
                in_=self.d_kvout[r].rearrange("(s q) t -> s q t", q=4097)
                [:, 4096, 0:1],
                allow_slow_non_contiguous=True),
                ins=[self.Bd_kvout[r]], outs=[self.B_lnrows])
        self.tposef(self.ln_rows[0:1, 8:16], self.B_lnrows, stg8[:, 0:1],
                    [self.B_lnrows], bank=5, r=8, c=1)
        self.tposef(self.ln_rows[0:1, 16:24], self.B_lnrows, stg8[:, 1:2],
                    [self.B_lnrows], bank=5, r=8, c=1)
        b.op("vector", lambda e, c: e.reduce_max(
            bnd[0:1, 6:7], self.ln_rows[0:1, 8:24], AXX),
            ins=[self.B_lnrows], outs=[A["B_bnd"]], inc=dv)
        b.op("vector", lambda e, c: e.tensor_tensor(
            bnd[0:1, 6:7], bnd[0:1, 6:7], maxq, ALU.mult),
            ins=[A["B_bnd"]], outs=[A["B_bnd"]], inc=dv)
        b.op("scalar", lambda e, c: e.activation(
            bnd[0:1, 6:7], bnd[0:1, 6:7], AF.Sqrt),
            ins=[A["B_bnd"]], outs=[A["B_bnd"]], inc=ac)
        b.op("vector", lambda e, c: e.tensor_scalar_mul(
            bnd[0:1, 6:7], bnd[0:1, 6:7], -1.0 / float(DH) ** 0.5),
            ins=[A["B_bnd"]], outs=[A["B_bnd"]], inc=dv)
        b.op("vector", lambda e, c: e.tensor_copy(
            self.ln_rows[0:1, 24:25], bnd[0:1, 6:7]),
            ins=[A["B_bnd"]], outs=[self.B_lnrows], inc=dv)
        self.mm(ps[5][:, 0:1], self.onesf[0:1, :], self.ln_rows[0:1, 24:25],
                psB[5], [self.B_lnrows, self.B_c], start=True, stop=True,
                first=True)
        b.op("vector", lambda e, c: e.tensor_copy(negC[:, 0:1], ps[5][:, 0:1]),
             ins=[psB[5]], outs=[A["B_negC"]], inc=dv)

        self.emit_flash(A, Bd_kvin)
        self.emit_wo(A)
        af.release()
        ab.release()

    def emit_flash(self, A, Bd_kvin):
        import bass_rust
        b = self.b
        dv, ac, ld, wg = self.s_dv, self.s_ac, self.s_ld, self.s_wg
        ps, psB, pp = self.ps, self.psB, self.pp
        kst, vst, qh = A["kst"], A["vst"], A["qh"]
        pacc = A["pacc"]
        SC = 1.0 / float(DH) ** 0.5
        ppS = [pp[0][:, 0:512], pp[0][:, 512:1024],
               pp[1][:, 0:512], pp[1][:, 512:1024]]
        pbS = [psB[0], psB[1], psB[2], psB[3]]
        ppO = pp[2]
        pbO = Buf("ppO")
        pbO.freed = (psB[4].ready + psB[4].freed + psB[5].ready
                     + psB[5].freed)
        ppL = pp[3]
        pbL = Buf("ppL")
        pbL.freed = (psB[6].ready + psB[6].freed + psB[7].ready
                     + psB[7].freed)
        zb = A["zrow"]
        B_kst = [Buf("kst0"), Buf("kst1")]
        B_vst = [Buf("vst0"), Buf("vst1")]
        B_vfm = [Buf("vfm0"), Buf("vfm1")]
        for bb in B_kst + B_vst + B_vfm:
            bb.freed = list(A["B_kst"].freed)
        B_pacc = Buf("pacc")
        B_pacc.freed = list(A["B_pacc"].freed)
        Bd_kvall = Buf("kvall")
        Bd_kvall.ready = [(self.s_agkv[0], 1), (self.s_agkv[1], 1)]
        B_astg = Buf("astg")

        for s in range(1, 8):
            eng, qsem = "scalar", self.s_rs[s % 2]

            def restage(e, c, s=s):
                cidr = e.to_reg(e.partition_id())
                t = e.alloc_register(f"rs_t{s}")
                rk = e.alloc_register(f"rs_k{s}")
                e.reg_alu(t, cidr, 17 - 2 * s, ALU.add)
                e.reg_alu(rk, cidr, s, ALU.is_lt)
                e.reg_alu(t, t, rk, ALU.mult)
                e.reg_alu(rk, t, s - 1, ALU.add)
                e.reg_alu(rk, rk, 4097, ALU.mult)
                e.free_register(t)
                rv = bass_rust.RuntimeValue(rk, min_val=0,
                                            max_val=15 * 4097)
                return e.dma_start(
                    out=self.d_astg[(s - 1) * 4096:s * 4096, :],
                    in_=self.d_kvall[bass.ds(rv, 4096), :])
            b.op(eng, restage, ins=[Bd_kvall], outs=[B_astg], inc=qsem,
                 amt=16, relax=True)
        B_astg.ready = [(self.s_rs[0], self.s_rs[0].n),
                        (self.s_rs[1], self.s_rs[1].n)]

        def qsrv(e, c, s):
            if "fl_qs" not in c._c:
                cidr = e.to_reg(e.partition_id())
                t = e.alloc_register("fl_t")
                rvs = {}
                for s2 in range(1, 8):
                    rq = e.alloc_register(f"fl_qs{s2}")
                    e.reg_alu(t, cidr, s2, ALU.is_lt)
                    e.reg_alu(rq, t, 512, ALU.mult)
                    rvs[s2] = bass_rust.RuntimeValue(
                        rq, min_val=0, max_val=512)
                e.free_register(t)
                c._c["fl_qs"] = rvs
            return c._c["fl_qs"][s]

        def qcol(e, c, sl, t, row1=False):
            pfx = t[0:1, :] if row1 else t[:, :]
            if sl == 0:
                return pfx[:, 0:512]
            if sl >= 8:
                return pfx[:, 512:1024]
            return pfx[:, bass.ds(qsrv(e, c, sl), 512)]

        for h in range(NH):
            b.op("sync", lambda e, c, h=h: e.dma_start(
                out=qh, in_=self.d_q[h * DH:(h + 1) * DH, :]),
                ins=[self.Bd_q], outs=[A["B_qh"]], inc=self.s_ld, amt=16)
            for side in range(2):
                ov = ppO[:, side * 512:(side + 1) * 512]
                wl = list(pbO.ready) + (pbO.freed if side == 0 else []) \
                    + A["B_qh"].ready + A["B_zrow"].ready
                n = b.raw("tensor", lambda e, c, ov=ov: e.matmul(
                    ov, zb[0:1, 0:128], qh[0:1, 0:512], start=True,
                    stop=False), waits=wl, inc=self.s_pe)
                pbO.ready = [(self.s_pe, n)]
                if side == 0:
                    pbO.freed = []
            b.op("vector", lambda e, c: e.memset(A["paccS"], 0.0),
                 ins=[], outs=[A["B_paccS"]], inc=dv)

            def stage_dma(sl, h):
                par = sl % 2
                ko = kst[:, par * 512:(par + 1) * 512]
                vf = A["vfm"][:, par * 512:(par + 1) * 512]
                if sl == 0 or sl == 8:
                    r = 0 if sl == 0 else 1
                    ksrc = self.d_kvin[r][h * DH:(h + 1) * DH, :]
                    vsrc = self.d_kvin[r][EMB + h * DH:
                                          EMB + (h + 1) * DH, :]
                    kdep = vdep = Bd_kvin[r]
                elif sl <= 7:
                    base = (sl - 1) * 4096
                    ksrc = self.d_astg[base + h * DH:
                                       base + (h + 1) * DH, :]
                    vsrc = self.d_astg[base + EMB + h * DH:
                                       base + EMB + (h + 1) * DH, :]
                    kdep = vdep = B_astg
                else:
                    seg = sl - 9
                    base = seg * 4097
                    ksrc = self.d_kvall[base + h * DH:
                                        base + (h + 1) * DH, :]
                    vsrc = self.d_kvall[base + EMB + h * DH:
                                        base + EMB + (h + 1) * DH, :]
                    kdep = vdep = Bd_kvall
                b.op("gpsimd", lambda e, c, ko=ko, ksrc=ksrc: e.dma_start(
                    out=ko, in_=ksrc),
                    ins=[kdep], outs=[B_kst[par]], inc=self.s_kq[par],
                    amt=16, relax=True)
                b.op("gpsimd", lambda e, c, vf=vf, vsrc=vsrc: e.dma_start(
                    out=vf, in_=vsrc),
                    ins=[vdep], outs=[B_vfm[par]], inc=self.s_vq[par],
                    amt=16, relax=True)
                n0 = [self.s_vt[0].n, self.s_vt[1].n]
                for w in range(4):
                    teng = "sync" if w < 2 else "scalar"
                    ti = 0 if w < 2 else 1
                    b.op(teng, lambda e, c, vf=vf, par=par, w=w:
                         e.dma_start_transpose(
                             out=vst[:, par * 512 + w * 128:
                                     par * 512 + (w + 1) * 128],
                             in_=vf[:, w * 128:(w + 1) * 128]),
                         ins=[B_vfm[par]], outs=[B_vst[par]],
                         inc=self.s_vt[ti], amt=16, relax=n0[ti])
                B_vst[par].ready = [(self.s_vt[0], self.s_vt[0].n),
                                    (self.s_vt[1], self.s_vt[1].n)]

            stage_dma(0, h)
            stage_dma(1, h)
            for sl in range(17):
                par = sl % 2
                diag = sl in (0, 8)
                for w in range(4):
                    pS = ppS[w]
                    pbSw = pbS[w]
                    pbuf = A["pb"][w]
                    pbufB = A["B_pb"][w]
                    wl = (list(pbSw.ready) + list(pbSw.freed)
                          + B_kst[par].ready + A["B_qh"].ready)
                    n = b.raw("tensor", lambda e, c, pS=pS, w=w, par=par,
                              sl=sl: e.matmul(
                                  pS,
                                  kst[:, par * 512 + w * 128:
                                      par * 512 + (w + 1) * 128],
                                  qcol(e, c, sl, qh), start=True,
                                  stop=True),
                              waits=wl, inc=self.s_pe)
                    pbSw.ready = [(self.s_pe, n)]
                    pbSw.freed = []
                    B_kst[par].freed.append((self.s_pe, n))
                    b.op("scalar", lambda e, c, pS=pS, pbuf=pbuf:
                         e.activation(pbuf, pS, AF.Exp, scale=SC,
                                      bias=A["negC"][:, 0:1]),
                         ins=[pbSw, A["B_negC"]], outs=[pbufB],
                         inc=self.s_ac)
                    if diag:
                        b.op("vector", lambda e, c, pbuf=pbuf, w=w:
                             e.tensor_mul(
                                 pbuf, pbuf,
                                 self.fmask[:, w * BLK:(w + 1) * BLK]),
                             ins=[pbufB, self.B_c], outs=[pbufB],
                             inc=self.s_dv)
                    if w == 0:
                        b.op("vector", lambda e, c, pbuf=pbuf:
                             e.tensor_copy(pacc, pbuf),
                             ins=[pbufB], outs=[B_pacc], inc=dv)
                    else:
                        b.op("vector", lambda e, c, pbuf=pbuf:
                             e.tensor_add(pacc, pacc, pbuf),
                             ins=[pbufB, B_pacc], outs=[B_pacc], inc=dv)
                for w in range(4):
                    pbuf = A["pb"][w]
                    pbufB = A["B_pb"][w]
                    wl = list(pbO.ready) + pbufB.ready + B_vst[par].ready
                    n = b.raw("tensor", lambda e, c, pbuf=pbuf, w=w,
                              par=par, sl=sl: e.matmul(
                                  qcol(e, c, sl, ppO),
                                  vst[:, par * 512 + w * 128:
                                      par * 512 + (w + 1) * 128],
                                  pbuf, start=False, stop=False),
                              waits=wl, inc=self.s_pe)
                    pbO.ready = [(self.s_pe, n)]
                    pbufB.freed.append((self.s_pe, n))
                    B_vst[par].freed.append((self.s_pe, n))
                b.op("vector", lambda e, c, sl=sl: e.tensor_add(
                    qcol(e, c, sl, A["paccS"]),
                    qcol(e, c, sl, A["paccS"]), pacc),
                    ins=[B_pacc, A["B_paccS"]], outs=[A["B_paccS"]],
                    inc=dv)
                if sl + 2 <= 16:
                    stage_dma(sl + 2, h)
            for side in range(2):
                ov = ppO[:, side * 512:(side + 1) * 512]
                n = b.raw("tensor", lambda e, c, ov=ov: e.matmul(
                    ov, zb[0:1, 0:128], qh[0:1, 0:512], start=False,
                    stop=True), waits=list(pbO.ready), inc=self.s_pe)
                pbO.ready = [(self.s_pe, n)]
                wl = (list(pbL.ready) + list(pbL.freed)
                      + A["B_paccS"].ready + self.B_c.ready)
                n = b.raw("tensor", lambda e, c, side=side: e.matmul(
                    ppL[0:1, side * 512:(side + 1) * 512],
                    self.onesf[:, 0:1],
                    A["paccS"][:, side * 512:(side + 1) * 512],
                    start=True, stop=True), waits=wl, inc=self.s_pe)
                pbL.ready = [(self.s_pe, n)]
                pbL.freed = [] if side == 0 else pbL.freed
                A["B_paccS"].freed.append((self.s_pe, n))
            rl = self.ln_rows[0:1, 0:TPC]
            b.op("vector", lambda e, c: e.reciprocal(rl, ppL[0:1, :]),
                 ins=[pbL], outs=[self.B_lnrows], inc=dv)
            for half in range(2):
                self.mm(ps[2 + half][:, 0:BLK], self.onesf[0:1, :],
                        rl[0:1, half * BLK:(half + 1) * BLK], psB[2 + half],
                        [self.B_lnrows, self.B_c], start=True, stop=True,
                        first=True)
                b.op("vector", lambda e, c, half=half: e.tensor_copy(
                    A["rlb"][:, half * BLK:(half + 1) * BLK],
                    ps[2 + half][:, 0:BLK]),
                    ins=[psB[2 + half]], outs=[A["B_rlb"]], inc=dv)
            b.op("vector", lambda e, c: e.tensor_mul(
                A["osb"], ppO[:, :], A["rlb"]),
                ins=[pbO, A["B_rlb"]], outs=[A["B_osb"]], inc=dv)
            b.op("sync", lambda e, c, h=h: e.dma_start(
                out=self.d_q[h * DH:(h + 1) * DH, :], in_=A["osb"]),
                ins=[A["B_osb"]], outs=[self.Bd_q], inc=self.s_ld, amt=16)
        for i in (4, 5):
            psB[i].ready = list(pbO.ready)
            psB[i].freed = list(pbO.freed)
        for i in (6, 7):
            psB[i].ready = list(pbL.ready)
            psB[i].freed = list(pbL.freed)

    def emit_wo(self, A):
        b = self.b
        dv, wg, ld = self.s_dv, self.s_wg, self.s_ld
        ps, psB = self.ps, self.psB
        xr3 = self.xres.rearrange("p (f t) -> p f t", t=TPC)
        wob = [A["wkva"][i] for i in range(2)]
        Bwob = [A["B_wkva"][i] for i in range(2)]
        obuf = A["xln"][:, 0:8192]
        obuf2 = A["xln"][:, 8192:16384]
        B_obuf = Buf("obuf")
        B_obuf.freed = A["B_xln"].freed + A["B_xln"].ready
        B_obuf2 = Buf("obuf2")
        B_obuf2.freed = A["B_xln"].freed + A["B_xln"].ready
        for k in range(16):
            b.op("sync", lambda e, c, k=k: e.dma_start(
                out=obuf[:, k * 512:(k + 1) * 512],
                in_=self.d_q[k * DH:(k + 1) * DH, 0:512]),
                ins=[self.Bd_q], outs=[B_obuf], inc=ld, amt=16)
            b.op("sync", lambda e, c, k=k: e.dma_start(
                out=obuf2[:, k * 512:(k + 1) * 512],
                in_=self.d_q[k * DH:(k + 1) * DH, 512:1024]),
                ins=[self.Bd_q], outs=[B_obuf2], inc=ld, amt=16)
        for m in range(16):
            wb, Bwb = wob[m % 2], Bwob[m % 2]
            b.op("gpsimd", lambda e, c, m=m, wb=wb: e.dma_start(
                out=wb.rearrange("p (k q) -> p k q", k=16),
                in_=self.i_wo[:, m * 128:(m + 1) * 128].rearrange(
                    "(k p) q -> p k q", p=128)),
                outs=[Bwb], inc=wg, amt=16)
            wb3 = wb.rearrange("p (k q) -> p k q", k=16)
            for half in range(2):
                pb_ = psB[4 + half]
                pv = ps[4 + half]
                src = obuf if half == 0 else obuf2
                sbuf = B_obuf if half == 0 else B_obuf2
                for k in range(16):
                    self.mm(pv[:, 0:BLK], wb3[:, k, :],
                            src[:, k * 512:(k + 1) * 512], pb_,
                            [Bwb, sbuf], start=(k == 0), stop=(k == 15),
                            first=(k == 0))
                tc0 = half * BLK
                b.op("vector", lambda e, c, m=m, tc0=tc0, pv=pv:
                     e.scalar_tensor_tensor(
                         xr3[:, m, tc0:tc0 + BLK], pv[:, 0:BLK], 1.0,
                         xr3[:, m, tc0:tc0 + BLK], ALU.mult, ALU.add),
                     ins=[pb_, self.B_x], outs=[self.B_x], inc=dv)

    def emit_store(self):
        b = self.b
        b.op("sync", lambda e, c: e.dma_start(
            out=self.o_out.rearrange("(f p) t -> p f t", p=128),
            in_=self.xres.rearrange("p (f t) -> p f t", t=TPC)),
            ins=[self.B_x], outs=[Buf("out")], inc=self.s_ld, amt=16)

    def finish(self):
        b = self.b
        sems = [self.s_ld, self.s_wg, self.s_pe, self.s_dv, self.s_ac,
                self.s_gp, self.s_od, self.s_odg, self.s_aq] \
            + self.s_kq + self.s_vq + self.s_vt + self.s_rs + self.s_agst \
            + [self.s_aggu, self.s_agdn] + self.s_agkv + self.s_aglr
        waits = [(s, s.n) for s in sems if s.n > 0]
        b.raw("gpsimd", lambda e, c: None, waits=waits)

    def build(self):
        self.declare()
        self.mem()
        self.sems()
        self.prelude()
        if self.dbg != "pre":
            self.mamba()
        if self.dbg in ("f2", "a3", ""):
            self.emit_dequant()
            self.emit_ffn(0)
        if self.dbg in ("a3", ""):
            self.emit_attn()
        if self.dbg == "":
            self.emit_ffn(1)
            self.emit_store()
        if self.dbg in ("pre", "ip0", "ln0", "f0", "ff", "m1", "f2", "a3"):
            o = self.dout("xres", DM, TPC)
            self.b.op("sync", lambda e, c: e.dma_start(
                out=o.rearrange("(f p) t -> p f t", p=128),
                in_=self.xres.rearrange("p (f t) -> p f t", t=TPC)),
                ins=[self.B_x], outs=[Buf("dbgo")], inc=self.s_ld, amt=16)
            self.finish()
            self.b.finalize()
            return
        self.finish()
        self.b.finalize()


def _prep_core_inputs(inputs, j):
    f32 = np.float32
    x = np.asarray(inputs["L0_attn_a_x"], f32)[0]
    xT = np.ascontiguousarray(x.T)
    bA, bB = j, 15 - j
    xc = np.zeros((DM, TPC + 6), f32)
    for slot, bb in ((0, bA), (1, bB)):
        c0 = slot * 515
        t0 = bb * BLK
        if t0 >= 3:
            xc[:, c0:c0 + 3] = xT[:, t0 - 3:t0]
        xc[:, c0 + 3:c0 + 3 + BLK] = xT[:, t0:t0 + BLK]

    D = np.asarray(inputs["L0_attn_a_D"], f32)
    dp = np.zeros((128, 32), f32)
    for t in range(32):
        dp[0:64, t] = D[2 * t]
        dp[64:128, t] = D[2 * t + 1]

    wkva = np.asarray(inputs["L0_attn_b_w_kva"], f32)
    used = [wkva[:, 0:KR]]
    for h in range(NH):
        c0 = KR + h * 2 * DH
        used.append(wkva[:, c0:c0 + DH])
    wkva_used = np.ascontiguousarray(np.concatenate(used, axis=1))

    gup = np.asarray(inputs["L0_ffn_gate_up_packed"], np.int32)
    gus = np.asarray(inputs["L0_ffn_gate_up_scales"], np.float16)
    dnp = np.asarray(inputs["L0_ffn_down_packed"], np.int32)
    dns = np.asarray(inputs["L0_ffn_down_scales"], np.float16)

    d = {
        "x": xc,
        "w_inT": np.ascontiguousarray(
            np.asarray(inputs["L0_attn_a_in_proj_w"], f32).T),
        "conv_w": np.asarray(inputs["L0_attn_a_conv_w"], f32),
        "dt_bias": np.asarray(inputs["L0_attn_a_dt_bias"], f32).reshape(-1, 1),
        "A_log": np.asarray(inputs["L0_attn_a_A_log"], f32).reshape(-1, 1),
        "D_pairs": dp,
        "norm_w": np.ascontiguousarray(
            np.asarray(inputs["L0_attn_a_norm_w"], f32).reshape(32, 128).T),
        "w_outT": np.ascontiguousarray(
            np.asarray(inputs["L0_attn_a_out_proj_w"], f32).T),
        "g_q_row": np.asarray(inputs["L0_attn_b_g_q"], f32).reshape(1, -1),
        "g_kv_row": np.asarray(inputs["L0_attn_b_g_kv"], f32).reshape(1, -1),
        "w_qa": np.asarray(inputs["L0_attn_b_w_qa"], f32),
        "w_qb": np.asarray(inputs["L0_attn_b_w_qb"], f32),
        "w_kva_used": wkva_used,
        "w_kvb": np.asarray(inputs["L0_attn_b_w_kvb"], f32),
        "w_o": np.asarray(inputs["L0_attn_b_w_o"], f32),
        "gup_packed": np.ascontiguousarray(gup[j * GU_SH:(j + 1) * GU_SH, :]),
        "gup_scalesT": np.ascontiguousarray(
            gus[:, j * GU_SH:(j + 1) * GU_SH].T),
        "down_packed": np.ascontiguousarray(dnp[j * DN_SH:(j + 1) * DN_SH, :]),
        "down_scalesT": np.ascontiguousarray(
            dns[:, j * DN_SH:(j + 1) * DN_SH].T),
        "jcol": np.full((128, 1), float(j), f32),
    }
    return d


_PROG_CACHE = {}


def _get_prog(dbg=""):
    if dbg not in _PROG_CACHE:
        p = Prog(dbg)
        p.build()
        _PROG_CACHE[dbg] = p
    return _PROG_CACHE[dbg]


def run_cores(inputs, dbg=""):
    p = _get_prog(dbg)
    in_maps = [_prep_core_inputs(inputs, j) for j in range(NCORE)]
    res = run_bass_kernel_spmd(p.nc, in_maps, core_ids=list(range(NCORE)))
    return res.results


def kernel(**inputs):
    res = run_cores(inputs)
    full = np.zeros((SEQ_TOTAL, DM), np.float32)
    for j in range(NCORE):
        o = np.asarray(res[j]["out"], np.float32)
        full[j * BLK:(j + 1) * BLK] = o[:, 0:BLK].T
        full[(15 - j) * BLK:(16 - j) * BLK] = o[:, BLK:TPC].T
    return full.reshape(1, SEQ_TOTAL, DM)


SEQ_TOTAL = 8192


# version 08
# speedup vs baseline: 1.1772x; 1.1125x over previous
import numpy as np
import ml_dtypes
import concourse.bass as bass
import concourse.mybir as mybir
from concourse.bass_utils import run_bass_kernel_spmd

F32 = mybir.dt.float32
BF16 = mybir.dt.bfloat16
I32 = mybir.dt.int32
F16 = mybir.dt.float16
AF = mybir.ActivationFunctionType
ALU = mybir.AluOpType
AXX = mybir.AxisListType.X

NCORE = 8
DM = 2048
HID = 4096
STATE = 128
HEADS = 64
PD = 64
CHUNK = 256
CONVD = HID + 2 * STATE
DIN = 2 * HID + 2 * STATE + HEADS
BLK = 512
TPC = 1024
EMB = 2048
NH = 16
DH = 128
QR = 64
KR = 128
RD = 64
FI = 5632
GS = 128
GU_SH = 2 * FI // NCORE
DN_SH = DM // NCORE

LN_EPS = 1e-5
MAMBA_EPS = 1e-5
FFN_EPS = 1e-6

ENGINES = ("sync", "gpsimd", "vector", "scalar", "tensor")


class Sem:
    __slots__ = ("name", "h", "n")

    def __init__(self, name, h):
        self.name, self.h, self.n = name, h, 0


class Buf:
    __slots__ = ("name", "ready", "freed")

    def __init__(self, name=""):
        self.name, self.ready, self.freed = name, [], []


class EngCtx:
    def __init__(self, eng):
        self.eng = eng
        self._c = {}

    @property
    def cid(self):
        if "cid" not in self._c:
            self._c["cid"] = self.eng.snap(self.eng.partition_id(),
                                           min_val=0, max_val=NCORE - 1)
        return self._c["cid"]


class B:
    def __init__(self, nc):
        self.nc = nc
        self.streams = {e: [] for e in ENGINES}
        self.waited = {e: {} for e in ENGINES}

    def sem(self, name):
        return Sem(name, self.nc.semaphore(name).__enter__())

    def raw(self, eng, fn, waits=(), inc=None, amt=1, dyn=None, relax=False):
        if relax is not False and inc is not None:
            waits = list(waits) + [(inc, inc.n if relax is True else relax)]
        wmax = {}
        for (s, c) in waits:
            if c > 0 and c > wmax.get(s, 0):
                wmax[s] = c
        wl = list(wmax.items())
        if inc is not None:
            inc.n += amt
        wd = self.waited[eng]
        if eng == "tensor":
            wl = [(s, c) for (s, c) in wl if s.name != "pe"]

        lagwait = None
        if inc is not None and amt == 16 and not relax:
            lagwait = inc.n

        def closure(e, ctx, wl=wl, fn=fn, inc=inc, amt=amt, dyn=dyn,
                    lagwait=lagwait):
            for s, c in wl:
                if wd.get(s.name, -1) < c:
                    e.wait_ge(s.h, c)
                    wd[s.name] = c
            if dyn is not None:
                dyn(e, ctx)
            try:
                instr = fn(e, ctx)
            except Exception as ex:
                import traceback
                cv = {k: v for k, v in
                      (fn.__defaults__ and
                       zip(fn.__code__.co_varnames[2:], fn.__defaults__)
                       or [])}
                raise RuntimeError(
                    f"op failed: vars={cv} err={ex}") from ex
            if inc is not None and instr is not None:
                instr.then_inc(inc.h, amt)
            if lagwait is not None and lagwait > 0 and \
                    wd.get(inc.name, -1) < lagwait:
                e.wait_ge(inc.h, lagwait)
                wd[inc.name] = lagwait

        self.streams[eng].append(closure)
        return inc.n if inc is not None else None

    def op(self, eng, fn, ins=(), outs=(), inc=None, amt=1, waits=(), dyn=None,
           relax=False):
        wl = list(waits)
        for bu in ins:
            wl += bu.ready
        for bu in outs:
            wl += bu.ready + bu.freed
        n = self.raw(eng, fn, waits=wl, inc=inc, amt=amt, dyn=dyn, relax=relax)
        dep = (inc, n)
        for bu in outs:
            bu.ready = [dep]
            bu.freed = []
        for bu in ins:
            bu.freed.append(dep)
        return n

    def finalize(self):
        streams = self.streams
        with self.nc.Block() as block:
            def run(e, name):
                ctx = EngCtx(e)
                for f in streams[name]:
                    f(e, ctx)

            @block.sync
            def _(e):
                run(e, "sync")

            @block.gpsimd
            def _(e):
                run(e, "gpsimd")

            @block.vector
            def _(e):
                run(e, "vector")

            @block.scalar
            def _(e):
                run(e, "scalar")

            @block.tensor
            def _(e):
                run(e, "tensor")


class Arena:
    def __init__(self, nc, name, cols, dtype):
        self.t = nc.alloc_sbuf_tensor(name, [128, cols], dtype)
        self.cols = cols
        self.cur = 0
        self.marks = []

    def alloc(self, n, name=""):
        assert self.cur + n <= self.cols, \
            f"arena overflow: {name} wants {n} at {self.cur}/{self.cols}"
        ap = self.t.ap()[:, self.cur:self.cur + n]
        self.cur += n
        return ap

    def mark(self):
        self.marks.append(self.cur)

    def release(self):
        self.cur = self.marks.pop()


class Prog:
    def __init__(self, dbg=""):
        self.nc = bass.Bass("TRN2", target_bir_lowering=False)
        self.b = B(self.nc)
        self.dbg = dbg
        self.dbg_out = {}

    def dout(self, name, rows, cols, dt=F32):
        ap = self.nc.dram_tensor("dbg_" + name, [rows, cols], dt,
                                 kind="ExternalOutput").ap()
        self.dbg_out[name] = ap
        return ap

    def declare(self):
        nc = self.nc

        def din(name, shape, dt=F32):
            return nc.dram_tensor(name, shape, dt, kind="ExternalInput").ap()

        self.i_x = din("x", [DM, TPC + 6])
        self.i_winT = din("w_inT", [DM, DIN])
        self.i_conv = din("conv_w", [CONVD, 4])
        self.i_dtb = din("dt_bias", [HEADS, 1])
        self.i_alog = din("A_log", [HEADS, 1])
        self.i_dpair = din("D_pairs", [128, 32])
        self.i_normw = din("norm_w", [128, HID // 128])
        self.i_woutT = din("w_outT", [HID, DM])
        self.i_gqr = din("g_q_row", [1, DM])
        self.i_gkvr = din("g_kv_row", [1, DM])
        self.i_wqa = din("w_qa", [DM, QR])
        self.i_wqb = din("w_qb", [QR, EMB])
        self.i_wkva = din("w_kva_used", [DM, KR + NH * DH])
        self.i_wkvb = din("w_kvb", [KR, EMB])
        self.i_wo = din("w_o", [EMB, EMB])
        self.i_gup = din("gup_packed", [GU_SH, DM // 8], I32)
        self.i_gus = din("gup_scalesT", [GU_SH, DM // GS], F16)
        self.i_dnp = din("down_packed", [DN_SH, FI // 8], I32)
        self.i_dns = din("down_scalesT", [DN_SH, FI // GS], F16)
        self.i_jcol = din("jcol", [128, 1])

        self.o_out = nc.dram_tensor("out", [DM, TPC], F32,
                                    kind="ExternalOutput").ap()

        def scr(name, rows, cols, dt=BF16):
            return nc.dram_tensor(name, [rows, cols], dt, kind="Internal").ap()

        self.d_z = scr("z_spill", 4 * HID, CHUNK)
        self.d_yp = scr("yp_spill", 4 * HID, CHUNK)
        self.d_stin = [scr(f"st_in{r}", 128, HEADS * PD, F32)
                       for r in range(2)]
        self.d_stout = [scr(f"st_out{r}", NCORE * 128, HEADS * PD, F32)
                        for r in range(2)]
        self.d_lrin = [scr(f"lr_in{r}", 1, HEADS, F32) for r in range(2)]
        self.d_lrout = [scr(f"lr_out{r}", NCORE, HEADS, F32)
                        for r in range(2)]
        self.Bd_lrin = [Buf("d_lrin0"), Buf("d_lrin1")]
        self.Bd_lrout = [Buf("d_lrout0"), Buf("d_lrout1")]
        self.d_guin = scr("gu_in", DM, GU_SH)
        self.d_guout = scr("gu_out", NCORE * DM, GU_SH)
        self.d_dnin = scr("dn_in", FI, DN_SH)
        self.d_dnout = scr("dn_out", NCORE * FI, DN_SH)
        self.d_kvin = [scr(f"kv_in{r}", 2 * EMB + 1, BLK) for r in range(2)]
        self.d_kvall = scr("kv_all", 2 * NCORE * (2 * EMB + 1), BLK)
        RUNROWS = NCORE * (2 * EMB + 1)
        self.d_kvout = [self.d_kvall[0:RUNROWS, :],
                        self.d_kvall[RUNROWS:2 * RUNROWS, :]]
        self.d_q = scr("q_spill", NH * DH, TPC)
        self.d_astg = scr("attn_stg", 7 * 4096, BLK)
        self.d_xbc1 = scr("xbc1_spill", 34 * 128, CHUNK)
        self.Bd_xbc1 = Buf("d_xbc1")
        self.Bd_z = Buf("d_z")
        self.Bd_yp = Buf("d_yp")
        self.Bd_stin = [Buf("d_stin0"), Buf("d_stin1")]
        self.Bd_stout = [Buf("d_stout0"), Buf("d_stout1")]
        self.Bd_q = Buf("d_q")

    def mem(self):
        nc = self.nc
        self.af = Arena(nc, "af", 23800, F32)
        self.ab = Arena(nc, "ab", 58752, BF16)

        self.xres = self.af.alloc(16 * TPC, "xres")
        self.onesf = self.af.alloc(128, "onesf")
        self.cc = self.af.alloc(8, "cc")
        self.p_dtb = self.af.alloc(1, "dtb")
        self.p_negA = self.af.alloc(1, "negA")
        self.p_dpair = self.af.alloc(32, "dpair")
        self.p_normw = self.af.alloc(32, "normw")
        self.p_convw = self.af.alloc(34 * 4, "convw")
        self.p_jcol = self.af.alloc(1, "jcol")
        self.identf = self.af.alloc(128, "identf")
        self.ln_rows = self.af.alloc(4 * 512, "ln_rows")
        self.B_lnrows = Buf("lnrows")

        self.grows = self.ab.alloc(2 * DM, "grows")
        self.onesb = self.ab.alloc(128, "onesb")
        self.lnr16 = self.ab.alloc(1024, "lnr16")
        self.B_lnr16 = Buf("lnr16")
        self.identb = self.ab.alloc(128, "identb")
        self.trilm = self.ab.alloc(2 * CHUNK, "trilm")
        self.fmask = self.ab.alloc(4 * BLK, "fmask")

        self.ppsb = [nc.alloc_psum_tensor(f"pp{i}", [128, 1024], F32)
                     for i in range(4)]
        self.pp = [t.ap() for t in self.ppsb]
        self.ps = [self.pp[i // 2][:, (i % 2) * 512:(i % 2) * 512 + 512]
                   for i in range(8)]
        self.psbf = [p.bitcast(BF16) for p in self.ps]
        self.psB = [Buf(f"ps{i}") for i in range(8)]

        self.B_x = Buf("xres")
        self.B_par = Buf("par")
        self.B_c = Buf("const")
        self.B_rows = Buf("rows2048")

    def sems(self):
        b = self.b
        self.s_ld = b.sem("ld")
        self.s_wg = b.sem("wg")
        self.s_pe = b.sem("pe")
        self.s_dv = b.sem("dv")
        self.s_ac = b.sem("ac")
        self.s_gp = b.sem("gp")
        self.s_agst = [b.sem("agst0"), b.sem("agst1")]
        self.s_aggu = b.sem("aggu")
        self.s_agdn = b.sem("agdn")
        self.s_agkv = [b.sem("agkv0"), b.sem("agkv1")]
        self.s_aglr = [b.sem("aglr0"), b.sem("aglr1")]
        self.s_od = b.sem("od")
        self.s_odg = b.sem("odg")
        self.s_aq = b.sem("aq")
        self.s_kq = [b.sem("kq0"), b.sem("kq1")]
        self.s_vq = [b.sem("vq0"), b.sem("vq1")]
        self.s_vt = [b.sem("vt0"), b.sem("vt1")]
        self.s_rs = [b.sem("rs0"), b.sem("rs1")]

    def mm(self, out_ap, lhsT_ap, rhs_ap, pbuf, ins, start, stop, first):
        b = self.b
        wl = []
        for bu in ins:
            wl += bu.ready
        wl += pbuf.ready
        if first:
            wl += pbuf.freed
        n = b.raw("tensor", lambda e, c: e.matmul(
            out_ap, lhsT_ap, rhs_ap, start=start, stop=stop),
            waits=wl, inc=self.s_pe)
        dep = (self.s_pe, n)
        pbuf.ready = [dep]
        if first:
            pbuf.freed = []
        for bu in ins:
            bu.freed.append(dep)
        return n

    def tposeb(self, dst_ap, dst_buf, src_ap, src_bufs, bank, r, c):
        pb = self.psB[bank]
        pv = self.psbf[bank][0:c, 0:r]
        b = self.b
        wl = pb.ready + pb.freed + self.B_c.ready
        for bu in src_bufs:
            wl = wl + bu.ready
        n = b.raw("tensor", lambda e, cx: e.transpose(
            pv, src_ap, self.identb[0:r, 0:r]), waits=wl, inc=self.s_pe)
        pb.ready = [(self.s_pe, n)]
        pb.freed = []
        for bu in src_bufs:
            bu.freed.append((self.s_pe, n))
        b.op("vector", lambda e, cx: e.tensor_copy(dst_ap, pv),
             ins=[pb], outs=[dst_buf], inc=self.s_dv)

    def tposef(self, dst_ap, dst_buf, src_ap, src_bufs, bank, r, c):
        pb = self.psB[bank]
        pv = self.ps[bank][0:c, 0:r]
        b = self.b
        wl = pb.ready + pb.freed + self.B_c.ready
        for bu in src_bufs:
            wl = wl + bu.ready
        n = b.raw("tensor", lambda e, cx: e.transpose(
            pv, src_ap, self.identf[0:r, 0:r]), waits=wl, inc=self.s_pe)
        pb.ready = [(self.s_pe, n)]
        pb.freed = []
        for bu in src_bufs:
            bu.freed.append((self.s_pe, n))
        b.op("vector", lambda e, cx: e.tensor_copy(dst_ap, pv),
             ins=[pb], outs=[dst_buf], inc=self.s_dv)

    def emit_ln(self, dst_tiles, src_tiles, n, gain_row, stg, B_stg,
                banks=(4, 5, 6, 7), n_feat=DM):
        b = self.b
        nt = len(src_tiles)
        s0, s1, s2, s3 = [self.psB[i] for i in banks]
        p0, p1, p2, p3 = [self.ps[i] for i in banks]
        xb = stg[:, 0:n]
        sq = stg[:, n:2 * n]
        for ft in range(nt):
            src, bsrc = src_tiles[ft]
            b.op("vector", lambda e, c, src=src, xb=xb: e.tensor_copy(xb, src),
                 ins=[bsrc], outs=[B_stg], inc=self.s_dv)
            b.op("scalar", lambda e, c, src=src, sq=sq: e.activation(
                sq, src, AF.Square), ins=[bsrc], outs=[B_stg], inc=self.s_ac)
            self.mm(p0[0:1, 0:n], self.onesb[:, 0:1], xb, s0,
                    [B_stg, self.B_c], start=(ft == 0), stop=(ft == nt - 1),
                    first=(ft == 0))
            self.mm(p1[0:1, 0:n], self.onesb[:, 0:1], sq, s1,
                    [B_stg, self.B_c], start=(ft == 0), stop=(ft == nt - 1),
                    first=(ft == 0))
        inv = 1.0 / n_feat
        rows = self.ln_rows
        Br = self.B_lnrows
        m_r = rows[0:1, 0:n]
        v_r = rows[0:1, 512:512 + n]
        a_r = rows[0:1, 1024:1024 + n]
        b_r = rows[0:1, 1536:1536 + n]
        b.op("vector", lambda e, c: e.tensor_scalar_mul(m_r, p0[0:1, 0:n], inv),
             ins=[s0], outs=[Br], inc=self.s_dv)
        b.op("vector", lambda e, c: e.tensor_scalar_mul(v_r, p1[0:1, 0:n], inv),
             ins=[s1], outs=[Br], inc=self.s_dv)
        b.op("vector", lambda e, c: e.scalar_tensor_tensor(
            a_r, m_r, -1.0, m_r, ALU.mult, ALU.mult),
            ins=[Br], outs=[Br], inc=self.s_dv)
        b.op("vector", lambda e, c: e.tensor_add(v_r, v_r, a_r),
             ins=[Br], outs=[Br], inc=self.s_dv)
        b.op("scalar", lambda e, c: e.activation(
            v_r, v_r, AF.Sqrt, bias=self.cc[0:1, 0:1]),
            ins=[Br, self.B_c], outs=[Br], inc=self.s_ac)
        b.op("vector", lambda e, c: e.reciprocal(a_r, v_r),
             ins=[Br], outs=[Br], inc=self.s_dv)
        b.op("vector", lambda e, c: e.scalar_tensor_tensor(
            b_r, m_r, -1.0, a_r, ALU.mult, ALU.mult),
            ins=[Br], outs=[Br], inc=self.s_dv)
        hoist = gain_row is None
        if hoist:
            self.mm(p2[:, 0:n], self.onesf[0:1, :], a_r, s2, [Br, self.B_c],
                    start=True, stop=True, first=True)
            self.mm(p3[:, 0:n], self.onesf[0:1, :], b_r, s3, [Br, self.B_c],
                    start=True, stop=True, first=True)
        else:
            a16 = self.lnr16[0:1, 0:n]
            b16 = self.lnr16[0:1, 512:512 + n]
            b.op("vector", lambda e, c: e.tensor_copy(a16, a_r),
                 ins=[Br], outs=[self.B_lnr16], inc=self.s_dv)
            b.op("vector", lambda e, c: e.tensor_copy(b16, b_r),
                 ins=[Br], outs=[self.B_lnr16], inc=self.s_dv)
        for ft in range(nt):
            if not hoist:
                gr, gb = gain_row
                lrow = gr[0:1, ft * 128:(ft + 1) * 128]
                self.mm(p2[:, 0:n], lrow, a16, s2, [self.B_lnr16, gb],
                        start=True, stop=True, first=True)
                self.mm(p3[:, 0:n], lrow, b16, s3, [self.B_lnr16, gb],
                        start=True, stop=True, first=True)
            src, bsrc = src_tiles[ft]
            dst, bdst = dst_tiles[ft]
            b.op("vector", lambda e, c, src=src, xb=xb: e.tensor_mul(
                xb, src, p2[:, 0:n]),
                ins=[bsrc, s2], outs=[B_stg], inc=self.s_dv)
            b.op("vector", lambda e, c, dst=dst, xb=xb: e.tensor_add(
                dst, xb, p3[:, 0:n]),
                ins=[B_stg, s3], outs=[bdst], inc=self.s_dv)


    def phase_seed(self):
        sems = [self.s_ld, self.s_wg, self.s_pe, self.s_dv, self.s_ac,
                self.s_gp, self.s_od, self.s_odg]
        return [(s, s.n) for s in sems if s.n > 0]

    def odma(self, eng, fn, ins=(), outs=()):
        s = self.s_odg if eng == "gpsimd" else self.s_od
        return self.b.op(eng, fn, ins=ins, outs=outs, inc=s, amt=16,
                         waits=[(s, s.n)])

    def rowbcast(self, bcast_bank, src_full, src_buf, h, n, nrows, bf, mv_bank,
                 evac_eng="scalar"):
        ps, psB = self.ps, self.psB
        b = self.b
        ident = self.identb if bf else self.identf
        ones = self.onesb if bf else self.onesf
        rstg = self.M["rowstg16"] if bf else self.M["rowstg"]
        Brstg = self.M["B_rowstg16"] if bf else self.M["B_rowstg"]
        self.mm(ps[mv_bank][0:1, 0:n], ident[0:nrows, h:h + 1], src_full,
                psB[mv_bank], [src_buf, self.B_c], start=True, stop=True,
                first=True)
        if evac_eng == "scalar":
            b.op("scalar", lambda e, c: e.copy(rstg[0:1, 0:n],
                                               ps[mv_bank][0:1, 0:n]),
                 ins=[psB[mv_bank]], outs=[Brstg], inc=self.s_ac)
        else:
            b.op("vector", lambda e, c: e.tensor_copy(rstg[0:1, 0:n],
                                                      ps[mv_bank][0:1, 0:n]),
                 ins=[psB[mv_bank]], outs=[Brstg], inc=self.s_dv)
        self.mm(ps[bcast_bank][:, 0:n], ones[0:1, :], rstg[0:1, 0:n],
                psB[bcast_bank], [Brstg, self.B_c], start=True, stop=True,
                first=True)

    def prelude(self):
        b = self.b
        ld, gp = self.s_ld, self.s_gp

        xr3 = self.xres.rearrange("p (f t) -> p f t", t=TPC)
        xs3 = self.i_x.rearrange("(f p) t -> p f t", p=128)
        b.op("sync", lambda e, c: e.dma_start(
            out=xr3[:, :, 0:BLK], in_=xs3[:, :, 3:3 + BLK]),
            outs=[self.B_x], inc=ld, amt=16)
        b.op("sync", lambda e, c: e.dma_start(
            out=xr3[:, :, BLK:TPC], in_=xs3[:, :, 518:518 + BLK]),
            outs=[self.B_x], inc=ld, amt=16)
        self.xhalo = [self.af.alloc(16 * 3, f"xhalo{r}") for r in range(2)]
        self.B_xh = [Buf("xhalo0"), Buf("xhalo1")]
        for r, src0 in ((0, 0), (1, 515)):
            b.op("sync", lambda e, c, r=r, src0=src0: e.dma_start(
                out=self.xhalo[r].rearrange("p (f t) -> p f t", t=3),
                in_=xs3[:, :, src0:src0 + 3]),
                outs=[self.B_xh[r]], inc=ld, amt=16)

        def pl(dst, src, bufs=None):
            b.op("sync", lambda e, c, d=dst, s=src: e.dma_start(out=d, in_=s),
                 outs=[bufs or self.B_par], inc=ld, amt=16)

        self.odma("sync", lambda e, c: e.dma_start(
            out=self.p_dtb[0:HEADS, :], in_=self.i_dtb), outs=[self.B_par])
        self.odma("sync", lambda e, c: e.dma_start(
            out=self.p_negA[0:HEADS, :], in_=self.i_alog), outs=[self.B_par])
        pl(self.p_dpair, self.i_dpair)
        pl(self.p_jcol, self.i_jcol)
        def plg(dst, src2):
            self.odma("gpsimd", lambda e, c, d=dst, s=src2: e.dma_start(
                out=d, in_=s), outs=[self.B_rows])
        plg(self.grows[0:1, 0:DM], self.i_gqr)
        plg(self.grows[0:1, DM:2 * DM], self.i_gkvr)
        pl(self.p_normw, self.i_normw)
        b.op("sync", lambda e, c: e.dma_start(
            out=self.p_convw.rearrange("p (a b) -> p a b", b=4),
            in_=self.i_conv.rearrange("(a p) b -> p a b", p=128)),
            outs=[self.B_par], inc=ld, amt=16)

        b.op("gpsimd", lambda e, c: e.memset(self.onesf, 1.0),
             outs=[self.B_c], inc=gp)
        b.op("gpsimd", lambda e, c: e.memset(self.onesb, 1.0),
             outs=[self.B_c], inc=gp)
        for t in (self.identb, self.identf):
            b.op("gpsimd", lambda e, c, t=t: e.memset(t, 1.0),
                 outs=[self.B_c], inc=gp)
            b.op("gpsimd", lambda e, c, t=t: e.affine_select(
                out=t, in_=t, compare_op=ALU.is_equal, fill=0.0,
                base=0, pattern=[[-1, 128]], channel_multiplier=1),
                outs=[self.B_c], inc=gp)
        for ti in range(2):
            t = self.trilm[:, ti * CHUNK:(ti + 1) * CHUNK]
            b.op("gpsimd", lambda e, c, t=t: e.memset(t, 1.0),
                 outs=[self.B_c], inc=gp)
            b.op("gpsimd", lambda e, c, t=t, ti=ti: e.affine_select(
                out=t, in_=t, compare_op=ALU.is_ge, fill=0.0,
                base=-ti * 128, pattern=[[1, CHUNK]], channel_multiplier=-1),
                outs=[self.B_c], inc=gp)
        for w in range(4):
            t = self.fmask[:, w * BLK:(w + 1) * BLK]
            b.op("gpsimd", lambda e, c, t=t: e.memset(t, 1.0),
                 outs=[self.B_c], inc=gp)
            b.op("gpsimd", lambda e, c, t=t, w=w: e.affine_select(
                out=t, in_=t, compare_op=ALU.is_ge, fill=0.0,
                base=-128 * w, pattern=[[1, BLK]], channel_multiplier=-1),
                outs=[self.B_c], inc=gp)
        for i, v in enumerate((LN_EPS, MAMBA_EPS, FFN_EPS, 1.0)):
            b.op("gpsimd", lambda e, c, i=i, v=v: e.memset(
                self.cc[:, i:i + 1], v), outs=[self.B_c], inc=gp)

        b.op("scalar", lambda e, c: e.activation(
            self.p_negA[0:HEADS, :], self.p_negA[0:HEADS, :], AF.Exp),
            ins=[self.B_par], outs=[self.B_par], inc=self.s_ac)
        b.op("vector", lambda e, c: e.tensor_scalar_mul(
            self.p_negA[0:HEADS, :], self.p_negA[0:HEADS, :], -1.0),
            ins=[self.B_par], outs=[self.B_par], inc=self.s_dv)

    def mamba(self):
        af, ab, b = self.af, self.ab, self.b
        af.mark()
        ab.mark()
        M = {}
        self.M = M

        def abuf(ar, n, name):
            M[name] = ar.alloc(n, name)
            M["B_" + name] = Buf(name)
            return M[name]

        self.front_col0 = ab.cur
        abuf(ab, 16 * 515, "xlnT")
        M["wbuf"] = [ab.alloc(16 * 128, f"wb{i}") for i in range(2)]
        M["B_wbuf"] = [Buf("wb0"), Buf("wb1")]
        abuf(ab, 34 * 259, "xbc")
        abuf(ab, 1024, "lnstg")
        abuf(ab, 512, "zstg")
        abuf(ab, 512, "xstg1")
        abuf(ab, 512, "zg")
        abuf(ab, 2 * 4096, "xhT")
        abuf(ab, 2 * 64, "dtT")
        abuf(ab, 2 * 64, "decT")
        abuf(ab, 2 * 128, "BT")
        abuf(ab, 2 * 256, "Mt")
        abuf(ab, 64, "cdT")
        abuf(ab, 256, "dec")
        abuf(ab, 256, "dt16")
        abuf(ab, 102, "hs")
        self.front_col1 = ab.cur
        self.front_bufs = [M["B_xlnT"], M["B_wbuf"][0], M["B_wbuf"][1],
                           M["B_xbc"], M["B_lnstg"], M["B_zstg"],
                           M["B_xstg1"], M["B_zg"], M["B_xhT"], M["B_dtT"],
                           M["B_decT"], M["B_BT"], M["B_Mt"], M["B_cdT"],
                           M["B_dec"], M["B_dt16"], M["B_hs"]]
        M["st0"] = [ab.alloc(4096, f"st0_{r}") for r in range(2)]
        M["B_st0"] = [Buf("st0_0"), Buf("st0_1")]
        M["edAc"] = [ab.alloc(256, f"edAc{i}") for i in range(4)]
        M["B_edAc"] = [Buf(f"edAc{i}") for i in range(4)]
        M["cdB"] = [ab.alloc(64, f"cdB{i}") for i in range(4)]
        M["B_cdB"] = [Buf(f"cdB{i}") for i in range(4)]
        M["Cb"] = [ab.alloc(256, f"Cb{i}") for i in range(4)]
        M["B_Cb"] = [Buf(f"Cb{i}") for i in range(4)]
        M["CBm"] = [ab.alloc(2 * 256, f"CBm{i}") for i in range(4)]
        M["B_CBm"] = [Buf(f"CBm{i}") for i in range(4)]
        abuf(ab, 2 * 256, "ypstg")
        abuf(af, 512, "dtraw")
        abuf(af, 512, "convacc")
        abuf(af, 256, "Gtmp")
        abuf(af, 256, "dtf")
        abuf(af, 256, "dAf")
        abuf(af, 256, "dAc")
        abuf(af, 2 * 64, "dAcT")
        abuf(af, 1, "logtot")
        abuf(af, 64, "logrow")
        abuf(af, 3 * 16, "Lr")
        abuf(af, 16, "bvec")
        abuf(af, 16, "bv2")
        abuf(af, 16, "ohtmp")
        abuf(af, 1, "scm1")
        abuf(af, 512, "rowstg")
        abuf(af, 16, "iota16")
        b.op("gpsimd", lambda e, c: e.iota(
            M["iota16"][0:HEADS, :], pattern=[[1, 16]], base=0,
            channel_multiplier=0, allow_small_or_imprecise_dtypes=True),
            outs=[M["B_iota16"]], inc=self.s_gp)

        nfronts = 1 if self.dbg in ("ip0", "ln0", "f0") else 2
        for run in range(nfronts):
            self.mamba_front(run)
        if self.dbg in ("ip0", "ln0", "f0", "ff"):
            af.release()
            ab.release()
            return

        seed = []
        for bu in self.front_bufs:
            seed += bu.ready + bu.freed

        cur = self.front_col0

        def ov(n, name):
            nonlocal cur
            assert cur + n <= self.front_col1, f"overlay overflow {name}"
            M[name] = ab.t.ap()[:, cur:cur + n]
            bb = Buf(name)
            bb.freed = list(seed)
            M["B_" + name] = bb
            cur += n
            return M[name]

        ov(8192, "ych")
        M["strm"] = M["ych"].bitcast(F32)[:, 0:4096]
        M["B_strm"] = M["B_ych"]
        ov(8192, "senter")
        M["senter_f"] = M["senter"].bitcast(F32)
        ov(4096, "prevb")
        ov(512, "ysq")
        ov(256, "Cw")
        ov(512, "zg2")
        ov(16 * 64, "coefB")
        ov(512, "rowstg16")
        ov(16, "coef16")
        ov(64, "coefT")
        M["wobuf"] = [ov(32 * 128, "wo0"), ov(32 * 128, "wo1")]
        M["B_wobuf"] = [M["B_wo0"], M["B_wo1"]]

        for run in range(2):
            self.mamba_yhalf(run)

        af.release()
        ab.release()

    def mamba_front(self, run):
        M, b = self.M, self.b
        ld, wg, dv, ac = self.s_ld, self.s_wg, self.s_dv, self.s_ac
        ps, psB = self.ps, self.psB
        xr3 = self.xres.rearrange("p (f t) -> p f t", t=TPC)
        xlnT3 = M["xlnT"].rearrange("p (f t) -> p f t", t=515)
        B_xln = M["B_xlnT"]

        xh3 = self.xhalo[run].rearrange("p (f t) -> p f t", t=3)
        self.emit_ln(
            [(xlnT3[:, ft, 0:3], B_xln) for ft in range(16)],
            [(xh3[:, ft, :], self.B_xh[run]) for ft in range(16)],
            3, None, M["lnstg"], M["B_lnstg"])
        self.emit_ln(
            [(xlnT3[:, ft, 3:515], B_xln) for ft in range(16)],
            [(xr3[:, ft, run * BLK:(run + 1) * BLK], self.B_x)
             for ft in range(16)],
            BLK, None, M["lnstg"], M["B_lnstg"])

        if self.dbg == "ln0":
            return
        import os
        xbc3 = M["xbc"].rearrange("p (f t) -> p f t", t=259)
        for m in range(int(os.environ.get("MLIM", "67"))):
            wb = M["wbuf"][m % 2]
            Bwb = M["B_wbuf"][m % 2]
            mc = 64 if m == 66 else 128
            b.op("gpsimd", lambda e, c, m=m, wb=wb, mc=mc: e.dma_start(
                out=wb[:, 0:16 * mc].rearrange("p (k q) -> p k q", k=16),
                in_=self.i_winT[:, m * 128:m * 128 + mc].rearrange(
                    "(k p) q -> p k q", p=128)),
                outs=[Bwb], inc=wg, amt=16)
            wb3 = wb[:, 0:16 * mc].rearrange("p (k q) -> p k q", q=mc)
            pmain, phalo = psB[(m % 2) * 2], psB[(m % 2) * 2 + 1]
            vmain, vhalo = ps[(m % 2) * 2], ps[(m % 2) * 2 + 1]
            for k in range(16):
                self.mm(vmain[0:mc, 0:512], wb3[:, k, :], xlnT3[:, k, 3:515],
                        pmain, [Bwb, B_xln], start=(k == 0), stop=(k == 15),
                        first=(k == 0))
            if 32 <= m < 66:
                for k in range(16):
                    self.mm(vhalo[0:mc, 0:3], wb3[:, k, :], xlnT3[:, k, 0:3],
                            phalo, [Bwb, B_xln], start=(k == 0),
                            stop=(k == 15), first=(k == 0))
            if m < 32:
                b.op("vector", lambda e, c, vmain=vmain: e.tensor_copy(
                    M["zstg"], vmain[:, 0:512]),
                    ins=[pmain], outs=[M["B_zstg"]], inc=dv)
                if os.environ.get("NOSPILL"):
                    continue
                b.op("sync", lambda e, c, m=m, run=run: e.dma_start(
                    out=self.d_z.rearrange(
                        "(ci f p) t -> p ci f t", p=128, f=32)
                    [:, 2 * run:2 * run + 2, m, :],
                    in_=M["zstg"].rearrange("p (ci t) -> p ci t", ci=2)),
                    ins=[M["B_zstg"]], outs=[self.Bd_z], inc=ld, amt=16)
            elif m < 66:
                ft = m - 32
                b.op("vector", lambda e, c, ft=ft, vmain=vmain: e.tensor_copy(
                    xbc3[:, ft, 3:259], vmain[:, 0:256]),
                    ins=[pmain], outs=[M["B_xbc"]], inc=dv)
                b.op("vector", lambda e, c, ft=ft, vhalo=vhalo: e.tensor_copy(
                    xbc3[:, ft, 0:3], vhalo[:, 0:3]),
                    ins=[phalo], outs=[M["B_xbc"]], inc=dv)
                b.op("vector", lambda e, c, vmain=vmain: e.tensor_copy(
                    M["xstg1"][:, 0:256], vmain[:, 256:512]),
                    ins=[pmain], outs=[M["B_xstg1"]], inc=dv)
                b.op("sync", lambda e, c, ft=ft: e.dma_start(
                    out=self.d_xbc1[ft * 128:(ft + 1) * 128, :],
                    in_=M["xstg1"][:, 0:256]),
                    ins=[M["B_xstg1"]], outs=[self.Bd_xbc1], inc=ld, amt=16)
            else:
                b.op("vector", lambda e, c, vmain=vmain: e.tensor_copy(
                    M["dtraw"][0:HEADS, :], vmain[0:HEADS, 0:512]),
                    ins=[pmain], outs=[M["B_dtraw"]], inc=dv)

        if self.dbg == "ip0" or self.dbg == "ln0":
            return
        for cci in range(2):
            self.ssd_front(run, cci)

    def emit_conv(self):
        M, b = self.M, self.b
        dv, ac = self.s_dv, self.s_ac
        xbc3 = M["xbc"].rearrange("p (f t) -> p f t", t=259)
        cw3 = self.p_convw.rearrange("p (f a) -> p f a", a=4)
        acc = M["convacc"][:, 0:256]
        Ba = M["B_convacc"]
        for ft in range(34):
            for tap in range(4):
                if tap == 0:
                    b.op("vector", lambda e, c, ft=ft: e.tensor_scalar(
                        acc, xbc3[:, ft, 0:256], cw3[:, ft, 0:1],
                        None, ALU.mult),
                        ins=[M["B_xbc"], self.B_par], outs=[Ba], inc=dv)
                else:
                    b.op("vector", lambda e, c, ft=ft, tap=tap:
                         e.scalar_tensor_tensor(
                             acc, xbc3[:, ft, tap:tap + 256],
                             cw3[:, ft, tap:tap + 1], acc,
                             ALU.mult, ALU.add),
                         ins=[M["B_xbc"], self.B_par, Ba], outs=[Ba], inc=dv)
            b.op("scalar", lambda e, c, ft=ft: e.activation(
                xbc3[:, ft, 3:259], acc, AF.Silu),
                ins=[Ba], outs=[M["B_xbc"]], inc=ac)

    def ssd_front(self, run, cci):
        M, b = self.M, self.b
        ld, dv, ac = self.s_ld, self.s_dv, self.s_ac
        ps, psB = self.ps, self.psB
        xbc3 = M["xbc"].rearrange("p (f t) -> p f t", t=259)
        col0 = 3
        ch = run * 2 + cci

        if cci == 0:
            b.op("vector", lambda e, c: e.tensor_copy(
                M["hs"].rearrange("p (f t) -> p f t", t=3),
                xbc3[:, :, 256:259]),
                ins=[M["B_xbc"]], outs=[M["B_hs"]], inc=dv)
            self.emit_conv()
        else:
            b.op("sync", lambda e, c: e.dma_start(
                out=xbc3[:, :, 3:259],
                in_=self.d_xbc1.rearrange("(f p) t -> p f t", p=128)),
                ins=[self.Bd_xbc1], outs=[M["B_xbc"]], inc=ld, amt=16)
            b.op("vector", lambda e, c: e.tensor_copy(
                xbc3[:, :, 0:3], M["hs"].rearrange("p (f t) -> p f t", t=3)),
                ins=[M["B_hs"]], outs=[M["B_xbc"]], inc=dv)
            self.emit_conv()

        dtr = M["dtraw"][0:HEADS, cci * CHUNK:(cci + 1) * CHUNK]
        dtf = M["dtf"][0:HEADS, :]
        dA = M["dAf"][0:HEADS, :]
        dAc = M["dAc"][0:HEADS, :]
        G = M["Gtmp"]
        Bdtr, Bdtf, BdA, BdAc = (M["B_dtraw"], M["B_dtf"], M["B_dAf"],
                                 M["B_dAc"])
        b.op("scalar", lambda e, c, dtr=dtr: e.activation(
            G[0:HEADS, :], dtr, AF.Abs, bias=self.p_dtb[0:HEADS, :]),
            ins=[Bdtr, self.B_par], outs=[M["B_Gtmp"]], inc=ac)
        b.op("scalar", lambda e, c: e.activation(
            G[0:HEADS, :], G[0:HEADS, :], AF.Exp, scale=-1.0),
            ins=[M["B_Gtmp"]], outs=[M["B_Gtmp"]], inc=ac)
        b.op("scalar", lambda e, c: e.activation(
            G[0:HEADS, :], G[0:HEADS, :], AF.Ln, bias=self.cc[0:HEADS, 3:4]),
            ins=[M["B_Gtmp"], self.B_c], outs=[M["B_Gtmp"]], inc=ac)
        b.op("scalar", lambda e, c, dtr=dtr: e.activation(
            dtf, dtr, AF.Relu, bias=self.p_dtb[0:HEADS, :]),
            ins=[Bdtr, self.B_par], outs=[Bdtf], inc=ac)
        b.op("vector", lambda e, c: e.tensor_add(dtf, dtf, G[0:HEADS, :]),
             ins=[Bdtf, M["B_Gtmp"]], outs=[Bdtf], inc=dv)
        b.op("vector", lambda e, c: e.tensor_scalar(
            dA, dtf, self.p_negA[0:HEADS, :], None, ALU.mult),
            ins=[Bdtf, self.B_par], outs=[BdA], inc=dv)
        b.op("vector", lambda e, c: e.tensor_tensor_scan(
            dAc, dA, dA, 0.0, ALU.add, ALU.bypass),
            ins=[BdA], outs=[BdAc], inc=dv)
        b.op("scalar", lambda e, c: e.activation(
            M["dec"][0:HEADS, :], dAc, AF.Exp, scale=-1.0,
            bias=dAc[:, CHUNK - 1:CHUNK]),
            ins=[BdAc], outs=[M["B_dec"]], inc=ac)
        b.op("scalar", lambda e, c, ch=ch: e.activation(
            M["edAc"][ch][0:HEADS, :], dAc, AF.Exp),
            ins=[BdAc], outs=[M["B_edAc"][ch]], inc=ac)
        b.op("vector", lambda e, c: e.tensor_copy(M["dt16"][0:HEADS, :], dtf),
             ins=[Bdtf], outs=[M["B_dt16"]], inc=dv)
        if cci == 0:
            b.op("vector", lambda e, c: e.tensor_copy(
                M["logtot"][0:HEADS, :], dAc[:, CHUNK - 1:CHUNK]),
                ins=[BdAc], outs=[M["B_logtot"]], inc=dv)
        else:
            b.op("vector", lambda e, c: e.tensor_add(
                M["logtot"][0:HEADS, :], M["logtot"][0:HEADS, :],
                dAc[:, CHUNK - 1:CHUNK]),
                ins=[M["B_logtot"], BdAc], outs=[M["B_logtot"]], inc=dv)

        dtT3 = M["dtT"].rearrange("p (s h) -> p s h", s=2)
        decT3 = M["decT"].rearrange("p (s h) -> p s h", s=2)
        dAcT3 = M["dAcT"].rearrange("p (s h) -> p s h", s=2)
        BT3 = M["BT"].rearrange("p (s q) -> p s q", s=2)
        for ti in range(2):
            self.tposeb(dtT3[:, ti, :], M["B_dtT"],
                        M["dt16"][0:HEADS, ti * 128:(ti + 1) * 128],
                        [M["B_dt16"]], bank=4, r=HEADS, c=128)
            self.tposeb(decT3[:, ti, :], M["B_decT"],
                        M["dec"][0:HEADS, ti * 128:(ti + 1) * 128],
                        [M["B_dec"]], bank=5, r=HEADS, c=128)
            self.tposef(dAcT3[:, ti, :], M["B_dAcT"],
                        dAc[:, ti * 128:(ti + 1) * 128], [BdAc],
                        bank=6, r=HEADS, c=128)
            self.tposeb(BT3[:, ti, :], M["B_BT"],
                        xbc3[:, 32, col0 + ti * 128:col0 + (ti + 1) * 128],
                        [M["B_xbc"]], bank=4, r=128, c=128)
        self.tposeb(M["cdT"][0:1, 0:HEADS], M["B_cdT"],
                    M["edAc"][ch][0:HEADS, CHUNK - 1:CHUNK],
                    [M["B_edAc"][ch]], bank=5, r=HEADS, c=1)
        self.mm(ps[5][:, 0:HEADS], self.onesb[0:1, :], M["cdT"][0:1, 0:HEADS],
                psB[5], [M["B_cdT"], self.B_c], start=True, stop=True,
                first=True)
        b.op("vector", lambda e, c, ch=ch: e.tensor_copy(
            M["cdB"][ch], ps[5][:, 0:HEADS]),
            ins=[psB[5]], outs=[M["B_cdB"][ch]], inc=dv)
        b.op("vector", lambda e, c, ch=ch: e.tensor_copy(
            M["Cb"][ch], xbc3[:, 33, col0:col0 + CHUNK]),
            ins=[M["B_xbc"]], outs=[M["B_Cb"][ch]], inc=dv)

        xhT3 = M["xhT"].rearrange("p (s f) -> p s f", s=2)
        for ft in range(32):
            for w in range(2):
                self.tposeb(xhT3[:, w, ft * 128:(ft + 1) * 128], M["B_xhT"],
                            xbc3[:, ft, col0 + w * 128:col0 + (w + 1) * 128],
                            [M["B_xbc"]], bank=4 + (ft + w) % 2, r=128, c=128)

        for ti in range(2):
            self.mm(ps[6 + ti][:, 0:CHUNK],
                    xbc3[:, 32, col0 + ti * 128:col0 + (ti + 1) * 128],
                    M["Cb"][ch], psB[6 + ti], [M["B_xbc"], M["B_Cb"][ch]],
                    start=True, stop=True, first=True)
            b.op("vector", lambda e, c, ti=ti, ch=ch: e.tensor_mul(
                M["CBm"][ch][:, ti * CHUNK:(ti + 1) * CHUNK],
                self.trilm[:, ti * CHUNK:(ti + 1) * CHUNK],
                ps[6 + ti][:, 0:CHUNK]),
                ins=[psB[6 + ti], self.B_c], outs=[M["B_CBm"][ch]], inc=dv)

        for ti in range(2):
            b.op("vector", lambda e, c, ti=ti: e.tensor_mul(
                xhT3[:, ti, :].rearrange("p (h q) -> p h q", q=PD),
                xhT3[:, ti, :].rearrange("p (h q) -> p h q", q=PD),
                dtT3[:, ti, :].unsqueeze(2).broadcast_to([128, HEADS, PD])),
                ins=[M["B_xhT"], M["B_dtT"]], outs=[M["B_xhT"]], inc=dv)

        for pair in range(32):
            yb = psB[2 + pair % 2]
            yv = ps[2 + pair % 2]
            for half in range(2):
                h = 2 * pair + half
                lb = psB[h % 2]
                lv = ps[h % 2]
                self.rowbcast(h % 2, M["dAc"][0:HEADS, :], M["B_dAc"], h,
                              CHUNK, HEADS, bf=False, mv_bank=6 + h % 2)
                for ti in range(2):
                    b.op("vector", lambda e, c, h=h, ti=ti, lv=lv:
                         e.tensor_scalar(
                             G, lv[:, 0:CHUNK], dAcT3[:, ti, h:h + 1],
                             0.0, ALU.subtract, ALU.min),
                         ins=[lb, M["B_dAcT"]], outs=[M["B_Gtmp"]], inc=dv)
                    b.op("scalar", lambda e, c, ti=ti: e.activation(
                        M["Mt"][:, ti * CHUNK:(ti + 1) * CHUNK], G, AF.Exp),
                        ins=[M["B_Gtmp"]], outs=[M["B_Mt"]], inc=ac)
                    b.op("vector", lambda e, c, ti=ti, ch=ch: e.tensor_mul(
                        M["Mt"][:, ti * CHUNK:(ti + 1) * CHUNK],
                        M["Mt"][:, ti * CHUNK:(ti + 1) * CHUNK],
                        M["CBm"][ch][:, ti * CHUNK:(ti + 1) * CHUNK]),
                        ins=[M["B_Mt"], M["B_CBm"][ch]], outs=[M["B_Mt"]],
                        inc=dv)
                for ti in range(2):
                    self.mm(yv[half * 64:(half + 1) * 64, 0:CHUNK],
                            xhT3[:, ti, h * PD:(h + 1) * PD],
                            M["Mt"][:, ti * CHUNK:(ti + 1) * CHUNK],
                            yb, [M["B_xhT"], M["B_Mt"]],
                            start=(ti == 0), stop=(ti == 1),
                            first=(half == 0 and ti == 0))
            zgv = M["zg"][:, (pair % 2) * CHUNK:(pair % 2 + 1) * CHUNK]
            b.op("sync", lambda e, c, pair=pair, ch=ch, zgv=zgv: e.dma_start(
                out=zgv,
                in_=self.d_z[ch * HID + pair * 128:
                             ch * HID + (pair + 1) * 128, :]),
                ins=[self.Bd_z], outs=[M["B_zg"]], inc=ld, amt=16)
            b.op("scalar", lambda e, c, zgv=zgv: e.activation(
                zgv, zgv, AF.Silu),
                ins=[M["B_zg"]], outs=[M["B_zg"]], inc=ac)
            stg = M["ypstg"][:, (pair % 2) * CHUNK:(pair % 2 + 1) * CHUNK]
            Bstg = M["B_ypstg"]
            b.op("vector", lambda e, c, pair=pair, yv=yv, stg=stg:
                 e.scalar_tensor_tensor(
                     stg, xbc3[:, pair, col0:col0 + CHUNK],
                     self.p_dpair[:, pair:pair + 1], yv[:, 0:CHUNK],
                     ALU.mult, ALU.add),
                 ins=[M["B_xbc"], self.B_par, yb], outs=[Bstg], inc=dv)
            b.op("vector", lambda e, c, stg=stg, zgv=zgv: e.tensor_mul(
                stg, stg, zgv),
                ins=[Bstg, M["B_zg"]], outs=[Bstg], inc=dv)
            b.op("sync", lambda e, c, pair=pair, ch=ch, stg=stg: e.dma_start(
                out=self.d_yp[ch * HID + pair * 128:
                              ch * HID + (pair + 1) * 128, :],
                in_=stg),
                ins=[Bstg], outs=[self.Bd_yp], inc=ld, amt=16)

        for ti in range(2):
            b.op("vector", lambda e, c, ti=ti: e.tensor_mul(
                xhT3[:, ti, :].rearrange("p (h q) -> p h q", q=PD),
                xhT3[:, ti, :].rearrange("p (h q) -> p h q", q=PD),
                decT3[:, ti, :].unsqueeze(2).broadcast_to([128, HEADS, PD])),
                ins=[M["B_xhT"], M["B_decT"]], outs=[M["B_xhT"]], inc=dv)
        st, stb = M["st0"][run], M["B_st0"][run]
        for pair in range(32):
            sb = psB[6 + pair % 2]
            sv = ps[6 + pair % 2]
            for half in range(2):
                h = 2 * pair + half
                for ti in range(2):
                    self.mm(sv[:, half * 64:(half + 1) * 64],
                            BT3[:, ti, :], xhT3[:, ti, h * PD:(h + 1) * PD],
                            sb, [M["B_BT"], M["B_xhT"]],
                            start=(ti == 0), stop=(ti == 1),
                            first=(half == 0 and ti == 0))
            if cci == 0:
                b.op("vector", lambda e, c, pair=pair, st=st, sv=sv:
                     e.tensor_copy(
                         st[:, pair * 128:(pair + 1) * 128], sv[:, 0:128]),
                     ins=[sb], outs=[stb], inc=dv)
            else:
                tf = M["rowstg"][:, 128:256]
                cdv = M["cdB"][ch][:, 2 * pair:2 * pair + 2].unsqueeze(
                    2).broadcast_to([128, 2, PD])
                b.op("vector", lambda e, c, pair=pair, cdv=cdv, tf=tf, ch=ch:
                     e.tensor_mul(
                         tf.rearrange("p (a q) -> p a q", q=PD),
                         M["st0"][run].rearrange("p (h q) -> p h q", q=PD)
                         [:, 2 * pair:2 * pair + 2, :], cdv),
                     ins=[M["B_st0"][run], M["B_cdB"][ch]],
                     outs=[M["B_rowstg"]], inc=dv)
                b.op("vector", lambda e, c, tf=tf, sv=sv: e.tensor_add(
                    tf, tf, sv[:, 0:128]),
                    ins=[M["B_rowstg"], sb], outs=[M["B_rowstg"]], inc=dv)
                b.op("sync", lambda e, c, pair=pair, run=run, tf=tf:
                     e.dma_start(
                         out=self.d_stin[run][:, pair * 128:(pair + 1) * 128],
                         in_=tf),
                     ins=[M["B_rowstg"]], outs=[self.Bd_stin[run]],
                     inc=ld, amt=16)

        if cci == 1:
            self.tposef(M["logrow"][0:1, 0:HEADS], M["B_logrow"],
                        M["logtot"][0:HEADS, :], [M["B_logtot"]],
                        bank=5, r=HEADS, c=1)
            self.odma("sync", lambda e, c, run=run: e.dma_start(
                out=self.d_lrin[run], in_=M["logrow"][0:1, 0:HEADS]),
                ins=[M["B_logrow"]], outs=[self.Bd_lrin[run]])
            b.op("gpsimd", lambda e, c, run=run: e.collective_compute(
                "AllGather", ALU.bypass,
                replica_groups=[list(range(NCORE))],
                ins=[self.d_stin[run].opt()], outs=[self.d_stout[run].opt()]),
                ins=[self.Bd_stin[run]], inc=self.s_agst[run])
            self.Bd_stout[run].ready = [(self.s_agst[run], 1)]
            b.op("gpsimd", lambda e, c, run=run: e.collective_compute(
                "AllGather", ALU.bypass,
                replica_groups=[list(range(NCORE))],
                ins=[self.d_lrin[run].opt()], outs=[self.d_lrout[run].opt()]),
                ins=[self.Bd_lrin[run]], inc=self.s_aglr[run])
            self.Bd_lrout[run].ready = [(self.s_aglr[run], 1)]

    def mamba_prefix_coef(self, run):
        M, b = self.M, self.b
        ld, dv, ac = self.s_ld, self.s_dv, self.s_ac
        ps, psB = self.ps, self.psB
        Lr = M["Lr"]
        BLr = M["B_Lr"]
        stg16 = self.ln_rows[0:16, 0:64]
        Bstg16 = self.B_lnrows
        for half in range(2):
            self.odma("sync", lambda e, c, half=half: e.dma_start(
                out=stg16[half * 8:half * 8 + 8, :],
                in_=self.d_lrout[half]),
                ins=[self.Bd_lrout[half]], outs=[Bstg16])
        L = Lr[0:HEADS, 0:16]
        S = Lr[0:HEADS, 16:32]
        T1 = Lr[0:HEADS, 32:48]
        self.tposef(L, BLr, stg16, [Bstg16], bank=5, r=16, c=HEADS)
        b.op("vector", lambda e, c: e.tensor_tensor_scan(
            S[:, 0:8], L[:, 0:8], L[:, 0:8], 0.0, ALU.add, ALU.bypass),
            ins=[BLr], outs=[BLr], inc=dv)
        b.op("vector", lambda e, c: e.tensor_tensor_scan(
            T1[:, 0:8], L[:, 8:16], L[:, 8:16], 0.0, ALU.add, ALU.bypass),
            ins=[BLr], outs=[BLr], inc=dv)
        b.op("vector", lambda e, c: e.tensor_add(
            T1[:, 8:9], S[:, 7:8], T1[:, 7:8]),
            ins=[BLr], outs=[BLr], inc=dv)
        b.op("vector", lambda e, c: e.tensor_sub(
            S[:, 8:16], L[:, 8:16], T1[:, 0:8]),
            ins=[BLr], outs=[BLr], inc=dv)
        b.op("vector", lambda e, c: e.tensor_scalar(
            S[:, 8:16], S[:, 8:16], T1[:, 8:9], None, ALU.add),
            ins=[BLr], outs=[BLr], inc=dv)
        if run == 0:
            bv, Bbv = M["bvec"][0:HEADS, :], M["B_bvec"]
            b2, Bb2 = M["bv2"][0:HEADS, :], M["B_bv2"]
            it, Bit = M["iota16"][0:HEADS, :], M["B_iota16"]
            b.op("vector", lambda e, c: e.tensor_scalar(
                bv, it, 7.0, None, ALU.is_le),
                ins=[Bit], outs=[Bbv], inc=dv)
            b.op("vector", lambda e, c: e.tensor_scalar(
                b2, it, -2.0, 23.0, ALU.mult, ALU.add),
                ins=[Bit], outs=[Bb2], inc=dv)
            b.op("vector", lambda e, c: e.tensor_scalar(
                bv, bv, -1.0, 1.0, ALU.mult, ALU.add),
                ins=[Bbv], outs=[Bbv], inc=dv)
            b.op("vector", lambda e, c: e.tensor_mul(b2, b2, bv),
                 ins=[Bb2, Bbv], outs=[Bb2], inc=dv)
            b.op("vector", lambda e, c: e.tensor_add(bv, b2, it),
                 ins=[Bb2, Bit], outs=[Bbv], inc=dv)
        tcol = M["bv2"][0:HEADS, 0:1]
        if run == 0:
            b.op("vector", lambda e, c: e.tensor_copy(
                tcol, self.p_jcol[0:HEADS, :]),
                ins=[self.B_par], outs=[M["B_bv2"]], inc=dv)
        else:
            b.op("vector", lambda e, c: e.tensor_scalar(
                tcol, self.p_jcol[0:HEADS, :], -1.0, 15.0, ALU.mult, ALU.add),
                ins=[self.B_par], outs=[M["B_bv2"]], inc=dv)
        mk = Lr[0:HEADS, 32:48]
        b.op("vector", lambda e, c: e.tensor_scalar(
            mk, M["bvec"][0:HEADS, :], tcol, None, ALU.is_lt),
            ins=[M["B_bvec"], M["B_bv2"]], outs=[BLr], inc=dv)
        oh = M["ohtmp"][0:HEADS, :]
        b.op("vector", lambda e, c: e.tensor_scalar(
            oh, M["bvec"][0:HEADS, :], tcol, 1.0, ALU.subtract, ALU.add),
            ins=[M["B_bvec"], M["B_bv2"]], outs=[M["B_ohtmp"]], inc=dv)
        b.op("vector", lambda e, c: e.tensor_scalar(
            oh, oh, 0.0, None, ALU.is_equal),
            ins=[M["B_ohtmp"]], outs=[M["B_ohtmp"]], inc=dv)
        b.op("vector", lambda e, c: e.tensor_mul(oh, oh, S),
             ins=[M["B_ohtmp"], BLr], outs=[M["B_ohtmp"]], inc=dv)
        b.op("vector", lambda e, c: e.reduce_sum(
            M["scm1"][0:HEADS, :], oh, AXX),
            ins=[M["B_ohtmp"]], outs=[M["B_scm1"]], inc=dv)
        cf = M["ohtmp"][0:HEADS, :]
        b.op("vector", lambda e, c: e.tensor_scalar(
            cf, S, M["scm1"][0:HEADS, :], 0.0, ALU.subtract, ALU.max),
            ins=[BLr, M["B_scm1"]], outs=[M["B_ohtmp"]], inc=dv)
        b.op("scalar", lambda e, c: e.activation(cf, cf, AF.Exp, scale=-1.0),
             ins=[M["B_ohtmp"]], outs=[M["B_ohtmp"]], inc=ac)
        b.op("vector", lambda e, c: e.tensor_mul(cf, cf, mk),
             ins=[M["B_ohtmp"], BLr], outs=[M["B_ohtmp"]], inc=dv)
        b.op("vector", lambda e, c: e.tensor_copy(
            M["coef16"][0:HEADS, :], cf),
            ins=[M["B_ohtmp"]], outs=[M["B_coef16"]], inc=dv)
        self.tposeb(M["coefT"][0:16, 0:HEADS], M["B_coefT"],
                    M["coef16"][0:HEADS, :], [M["B_coef16"]],
                    bank=5, r=HEADS, c=16)
        for o in range(16):
            bank = 6 + o % 2
            self.rowbcast(bank, M["coefT"][0:16, 0:HEADS], M["B_coefT"], o,
                          HEADS, 16, bf=True, mv_bank=4)
            b.op("vector", lambda e, c, o=o, bank=bank: e.tensor_copy(
                M["coefB"][:, o * HEADS:(o + 1) * HEADS],
                self.ps[bank][:, 0:HEADS]),
                ins=[psB[bank]], outs=[M["B_coefB"]], inc=dv)

    def mamba_yhalf(self, run):
        M, b = self.M, self.b
        ld, dv = self.s_ld, self.s_dv
        self.mamba_prefix_coef(run)
        se, Bse = M["senter_f"], M["B_senter"]
        sm, Bsm = M["strm"], M["B_strm"]
        for o in range(16):
            src = self.d_stout[0] if o < 8 else self.d_stout[1]
            Bsrc = self.Bd_stout[0] if o < 8 else self.Bd_stout[1]
            slot = o if o < 8 else o - 8
            b.op("sync", lambda e, c, src=src, slot=slot: e.dma_start(
                out=sm, in_=src[slot * 128:slot * 128 + 128, :]),
                ins=[Bsrc], outs=[Bsm], inc=ld, amt=16)
            cb3 = M["coefB"][:, o * HEADS:(o + 1) * HEADS].unsqueeze(
                2).broadcast_to([128, HEADS, PD])
            if o == 0:
                b.op("vector", lambda e, c, cb3=cb3: e.tensor_mul(
                    se.rearrange("p (h q) -> p h q", q=PD),
                    sm.rearrange("p (h q) -> p h q", q=PD), cb3),
                    ins=[Bsm, M["B_coefB"]], outs=[Bse], inc=dv)
            else:
                b.op("vector", lambda e, c, cb3=cb3: e.tensor_mul(
                    sm.rearrange("p (h q) -> p h q", q=PD),
                    sm.rearrange("p (h q) -> p h q", q=PD), cb3),
                    ins=[Bsm, M["B_coefB"]], outs=[Bsm], inc=dv)
                b.op("vector", lambda e, c: e.tensor_add(se, se, sm),
                     ins=[Bse, Bsm], outs=[Bse], inc=dv)
        for cci in range(2):
            ch = run * 2 + cci
            if cci == 1:
                cdB3 = M["cdB"][run * 2].unsqueeze(2).broadcast_to(
                    [128, HEADS, PD])
                b.op("vector", lambda e, c: e.tensor_mul(
                    se.rearrange("p (h q) -> p h q", q=PD),
                    se.rearrange("p (h q) -> p h q", q=PD), cdB3),
                    ins=[Bse, M["B_cdB"][run * 2]], outs=[Bse], inc=dv)
                b.op("vector", lambda e, c: e.tensor_add(
                    se, se, M["st0"][run]),
                    ins=[Bse, M["B_st0"][run]], outs=[Bse], inc=dv)
            b.op("vector", lambda e, c: e.tensor_copy(M["prevb"], se),
                 ins=[Bse], outs=[M["B_prevb"]], inc=dv)
            self.yhalf_chunk(run, cci, ch, M["prevb"], M["B_prevb"])

    def yhalf_chunk(self, run, cci, ch, prevv, Bprev):
        M, b = self.M, self.b
        ld, dv, ac = self.s_ld, self.s_dv, self.s_ac
        ps, psB = self.ps, self.psB
        yg3 = M["ych"].rearrange("p (f t) -> p f t", t=CHUNK)
        Byg = M["B_ych"]
        Cw, BCw = M["Cw"], M["B_Cw"]
        for pair in range(32):
            yb = psB[2 + pair % 2]
            yv = ps[2 + pair % 2]
            for half in range(2):
                h = 2 * pair + half
                lb = psB[half]
                lv = ps[half]
                self.rowbcast(half, M["edAc"][ch][0:HEADS, :],
                              M["B_edAc"][ch], h, CHUNK, HEADS, bf=True,
                              mv_bank=6 + half)
                b.op("vector", lambda e, c, ch=ch, lv=lv: e.tensor_mul(
                    Cw, M["Cb"][ch], lv[:, 0:CHUNK]),
                    ins=[M["B_Cb"][ch], lb], outs=[BCw], inc=dv)
                self.mm(yv[half * 64:(half + 1) * 64, 0:CHUNK],
                        prevv[:, h * PD:(h + 1) * PD], Cw, yb,
                        [Bprev, BCw], start=True, stop=True,
                        first=(half == 0))
            zgv = M["zg2"][:, (pair % 2) * CHUNK:(pair % 2 + 1) * CHUNK]
            b.op("sync", lambda e, c, pair=pair, ch=ch, zgv=zgv: e.dma_start(
                out=zgv,
                in_=self.d_z[ch * HID + pair * 128:
                             ch * HID + (pair + 1) * 128, :]),
                ins=[self.Bd_z], outs=[M["B_zg2"]], inc=ld, amt=16)
            b.op("scalar", lambda e, c, zgv=zgv: e.activation(
                zgv, zgv, AF.Silu),
                ins=[M["B_zg2"]], outs=[M["B_zg2"]], inc=ac)
            stg = M["ypstg"][:, (pair % 2) * CHUNK:(pair % 2 + 1) * CHUNK]
            b.op("sync", lambda e, c, pair=pair, ch=ch, stg=stg: e.dma_start(
                out=stg,
                in_=self.d_yp[ch * HID + pair * 128:
                              ch * HID + (pair + 1) * 128, :]),
                ins=[self.Bd_yp], outs=[M["B_ypstg"]], inc=ld, amt=16)
            b.op("vector", lambda e, c, pair=pair, zgv=zgv, yv=yv:
                 e.tensor_mul(
                     yg3[:, pair, :], zgv, yv[:, 0:CHUNK]),
                 ins=[M["B_zg2"], yb], outs=[Byg], inc=dv)
            b.op("vector", lambda e, c, pair=pair, stg=stg: e.tensor_add(
                yg3[:, pair, :], yg3[:, pair, :], stg),
                ins=[Byg, M["B_ypstg"]], outs=[Byg], inc=dv)
        sq, Bsq = M["ysq"], M["B_ysq"]
        for ft in range(32):
            b.op("scalar", lambda e, c, ft=ft: e.activation(
                sq[:, 0:CHUNK], yg3[:, ft, :], AF.Square),
                ins=[Byg], outs=[Bsq], inc=ac)
            self.mm(ps[4][0:1, 0:CHUNK], self.onesb[:, 0:1], sq[:, 0:CHUNK],
                    psB[4], [Bsq, self.B_c], start=(ft == 0), stop=(ft == 31),
                    first=(ft == 0))
        rr = self.ln_rows[0:1, 0:CHUNK]
        Brr = self.B_lnrows
        b.op("vector", lambda e, c: e.tensor_scalar_mul(
            rr, ps[4][0:1, 0:CHUNK], 1.0 / HID),
            ins=[psB[4]], outs=[Brr], inc=dv)
        b.op("scalar", lambda e, c: e.activation(
            rr, rr, AF.Sqrt, bias=self.cc[0:1, 1:2]),
            ins=[Brr, self.B_c], outs=[Brr], inc=ac)
        b.op("vector", lambda e, c: e.reciprocal(rr, rr),
             ins=[Brr], outs=[Brr], inc=dv)
        self.mm(ps[5][:, 0:CHUNK], self.onesf[0:1, :], rr, psB[5],
                [Brr, self.B_c], start=True, stop=True, first=True)
        for ft in range(32):
            b.op("vector", lambda e, c, ft=ft: e.scalar_tensor_tensor(
                yg3[:, ft, :], yg3[:, ft, :],
                self.p_normw[:, ft:ft + 1], ps[5][:, 0:CHUNK],
                ALU.mult, ALU.mult),
                ins=[Byg, self.B_par, psB[5]], outs=[Byg], inc=dv)
        xr3 = self.xres.rearrange("p (f t) -> p f t", t=TPC)
        tc0 = run * BLK + cci * CHUNK
        for m in range(16):
            wb = M["wobuf"][m % 2]
            Bwb = M["B_wobuf"][m % 2]
            b.op("gpsimd", lambda e, c, m=m, wb=wb: e.dma_start(
                out=wb.rearrange("p (k q) -> p k q", k=32),
                in_=self.i_woutT[:, m * 128:(m + 1) * 128].rearrange(
                    "(k p) q -> p k q", p=128)),
                outs=[Bwb], inc=self.s_wg, amt=16)
            wb3 = wb.rearrange("p (k q) -> p k q", k=32)
            pb = psB[(m % 2)]
            pv = ps[(m % 2)]
            for k in range(32):
                self.mm(pv[:, 0:CHUNK], wb3[:, k, :], yg3[:, k, :],
                        pb, [Bwb, Byg], start=(k == 0), stop=(k == 31),
                        first=(k == 0))
            b.op("vector", lambda e, c, m=m, pv=pv: e.scalar_tensor_tensor(
                xr3[:, m, tc0:tc0 + CHUNK], pv[:, 0:CHUNK], 1.0,
                xr3[:, m, tc0:tc0 + CHUNK], ALU.mult, ALU.add),
                ins=[pb, self.B_x], outs=[self.B_x], inc=dv)


    def emit_dequant(self):
        nc, b = self.nc, self.b
        af, ab = self.af, self.ab
        dv, gp, ld, wg = self.s_dv, self.s_gp, self.s_ld, self.s_wg
        ps, psB = self.ps, self.psB
        af.mark()
        ab.mark()
        D = {}
        seed = self.phase_seed()

        def abuf(n, name, ar=None):
            D[name] = (ar or ab).alloc(n, name)
            bb = Buf(name)
            bb.freed = list(seed)
            D["B_" + name] = bb
            return D[name]

        wt = abuf(1408, "wt")
        wt32 = wt.bitcast(I32)
        nn = abuf(4096, "nn").bitcast(I32)
        s31 = abuf(4096, "s31").bitcast(I32)
        uu = abuf(4096, "uu").bitcast(I32)
        tt = abuf(4096, "tt").bitcast(I32)
        val = abuf(5632, "val")
        tstg = abuf(5632, "tstg")
        self.scf16 = nc.alloc_sbuf_tensor("scf16", [128, 64], F16)
        D["B_sc"] = Buf("sc")
        MININT = -2147483648

        def deq_chunk(Wc, wsrc, scsrc, vdst, Bwt, Bsc, Bval):
            for k in range(8):
                b.op("vector", lambda e, c, k=k: e.tensor_scalar(
                    nn[:, k * Wc:(k + 1) * Wc], wsrc, 4 * k, 7,
                    ALU.logical_shift_right, ALU.bitwise_and),
                    ins=[Bwt], outs=[D["B_nn"]], inc=dv)
                b.op("vector", lambda e, c, k=k: e.tensor_scalar(
                    s31[:, k * Wc:(k + 1) * Wc], wsrc, 28 - 4 * k, MININT,
                    ALU.logical_shift_left, ALU.bitwise_and),
                    ins=[Bwt], outs=[D["B_s31"]], inc=dv)
            n8 = 8 * Wc
            b.op("vector", lambda e, c: e.tensor_scalar(
                uu[:, 0:n8], nn[:, 0:n8], 2, 250, ALU.mult, ALU.add),
                ins=[D["B_nn"]], outs=[D["B_uu"]], inc=dv)
            b.op("vector", lambda e, c: e.tensor_scalar(
                tt[:, 0:n8], nn[:, 0:n8], 252, None, ALU.add),
                ins=[D["B_nn"]], outs=[D["B_tt"]], inc=dv)
            b.op("vector", lambda e, c: e.tensor_tensor(
                uu[:, 0:n8], uu[:, 0:n8], tt[:, 0:n8], ALU.min),
                ins=[D["B_uu"], D["B_tt"]], outs=[D["B_uu"]], inc=dv)
            b.op("vector", lambda e, c: e.tensor_scalar(
                tt[:, 0:n8], nn[:, 0:n8], 505, None, ALU.mult),
                ins=[D["B_nn"]], outs=[D["B_tt"]], inc=dv)
            b.op("vector", lambda e, c: e.tensor_tensor(
                uu[:, 0:n8], uu[:, 0:n8], tt[:, 0:n8], ALU.min),
                ins=[D["B_uu"], D["B_tt"]], outs=[D["B_uu"]], inc=dv)
            b.op("vector", lambda e, c: e.tensor_single_scalar(
                uu[:, 0:n8], uu[:, 0:n8], 22, ALU.logical_shift_left),
                ins=[D["B_uu"]], outs=[D["B_uu"]], inc=dv)
            b.op("vector", lambda e, c: e.tensor_tensor(
                uu[:, 0:n8], uu[:, 0:n8], s31[:, 0:n8], ALU.bitwise_or),
                ins=[D["B_uu"], D["B_s31"]], outs=[D["B_uu"]], inc=dv)
            ngrp = (Wc + 15) // 16
            scb = scsrc.unsqueeze(1).unsqueeze(3).broadcast_to(
                [128, 8, ngrp, 16])[:, :, :, :].rearrange(
                "p a g j -> p (a g j)") if False else None
            scv = scsrc.unsqueeze(1).unsqueeze(3)
            scv = scv.broadcast_to([128, 8, ngrp, 16])
            uv = uu[:, 0:n8].bitcast(F32).rearrange(
                "p (k g j) -> p k g j", k=8, j=16)
            vv = vdst.rearrange("p (g j k) -> p k g j", k=8, j=16)
            b.op("vector", lambda e, c, uv=uv, scv=scv, vv=vv: e.tensor_tensor(
                vv, uv, scv, ALU.mult),
                ins=[D["B_uu"], Bsc], outs=[Bval], inc=dv)

        for it in range(11):
            b.op("sync", lambda e, c, it=it: e.dma_start(
                out=wt32[:, 0:256],
                in_=self.i_gup[it * 128:(it + 1) * 128, :]),
                ins=[], outs=[D["B_wt"]], inc=ld, amt=16)
            b.op("sync", lambda e, c, it=it: e.dma_start(
                out=self.scf16.ap()[:, 0:16],
                in_=self.i_gus[it * 128:(it + 1) * 128, :]),
                ins=[], outs=[D["B_sc"]], inc=ld, amt=16)
            deq_chunk(256, wt32[:, 0:256], self.scf16.ap()[:, 0:16],
                      val[:, 0:2048], D["B_wt"], D["B_sc"], D["B_val"])
            for k in range(16):
                self.tposeb(tstg[:, k * 128:(k + 1) * 128], D["B_tstg"],
                            val[:, k * 128:(k + 1) * 128], [D["B_val"]],
                            bank=4 + k % 2, r=128, c=128)
            b.op("sync", lambda e, c, it=it: e.dma_start(
                out=self.d_guin.rearrange("(k p) q -> p k q", p=128)
                [:, :, it * 128:(it + 1) * 128],
                in_=tstg[:, 0:2048].rearrange("p (k q) -> p k q", q=128)),
                ins=[D["B_tstg"]], outs=[Buf("guin")], inc=ld, amt=16)
        n_guin = self.s_ld.n
        for ht in range(2):
            b.op("sync", lambda e, c, ht=ht: e.dma_start(
                out=wt32[:, 0:704],
                in_=self.i_dnp[ht * 128:(ht + 1) * 128, :]),
                ins=[], outs=[D["B_wt"]], inc=ld, amt=16)
            b.op("sync", lambda e, c, ht=ht: e.dma_start(
                out=self.scf16.ap()[:, 0:44],
                in_=self.i_dns[ht * 128:(ht + 1) * 128, :]),
                ins=[], outs=[D["B_sc"]], inc=ld, amt=16)
            for cc in range(3):
                Wc = (256, 256, 192)[cc]
                deq_chunk(Wc, wt32[:, cc * 256:cc * 256 + Wc],
                          self.scf16.ap()[:, cc * 16:cc * 16 + (Wc + 15) // 16],
                          val[:, cc * 2048:cc * 2048 + 8 * Wc],
                          D["B_wt"], D["B_sc"], D["B_val"])
            for k in range(44):
                self.tposeb(tstg[:, k * 128:(k + 1) * 128], D["B_tstg"],
                            val[:, k * 128:(k + 1) * 128], [D["B_val"]],
                            bank=4 + k % 2, r=128, c=128)
            b.op("sync", lambda e, c, ht=ht: e.dma_start(
                out=self.d_dnin.rearrange("(k p) q -> p k q", p=128)
                [:, :, ht * 128:(ht + 1) * 128],
                in_=tstg[:, 0:5632].rearrange("p (k q) -> p k q", q=128)),
                ins=[D["B_tstg"]], outs=[Buf("dnin")], inc=ld, amt=16)
        n_dnin = self.s_ld.n
        self.Bd_guout = Buf("d_guout")
        self.Bd_dnout = Buf("d_dnout")
        b.raw("gpsimd", lambda e, c: e.collective_compute(
            "AllGather", ALU.bypass, replica_groups=[list(range(NCORE))],
            ins=[self.d_guin.opt()], outs=[self.d_guout.opt()]),
            waits=[(self.s_ld, n_guin)], inc=self.s_aggu)
        self.Bd_guout.ready = [(self.s_aggu, 1)]
        b.raw("gpsimd", lambda e, c: e.collective_compute(
            "AllGather", ALU.bypass, replica_groups=[list(range(NCORE))],
            ins=[self.d_dnin.opt()], outs=[self.d_dnout.opt()]),
            waits=[(self.s_ld, n_dnin)], inc=self.s_agdn)
        self.Bd_dnout.ready = [(self.s_agdn, 1)]
        af.release()
        ab.release()

    def emit_ffn(self, use):
        nc, b = self.nc, self.b
        af, ab = self.af, self.ab
        dv, ac, ld, wg = self.s_dv, self.s_ac, self.s_ld, self.s_wg
        ps, psB = self.ps, self.psB
        af.mark()
        ab.mark()
        F = {}
        seed = self.phase_seed()

        def abuf(n, name):
            F[name] = ab.alloc(n, name)
            bb = Buf(name)
            bb.freed = list(seed)
            F["B_" + name] = bb
            return F[name]

        xlnT = abuf(16 * BLK, "xlnTf")
        abuf(1024, "lnstgf")
        hbuf = abuf(44 * BLK, "hbuf")
        def mkbuf(name):
            bb = Buf(name)
            bb.freed = list(seed)
            return bb

        F["wgu"] = [ab.alloc(16 * 128, f"wgu{i}") for i in range(4)]
        F["B_wgu"] = [mkbuf(f"wgu{i}") for i in range(4)]
        F["wdn"] = [ab.alloc(22 * 128, f"wdn{i}") for i in range(2)]
        F["B_wdn"] = [mkbuf(f"wdn{i}") for i in range(2)]
        hstg = abuf(BLK, "hstg")
        ysb = F["xlnTf"]
        B_ysb = F["B_xlnTf"]
        ysq = abuf(BLK, "ysqf")
        xr3 = self.xres.rearrange("p (f t) -> p f t", t=TPC)
        xlnT3 = xlnT.rearrange("p (f t) -> p f t", t=BLK)
        hb3 = hbuf.rearrange("p (f t) -> p f t", t=BLK)
        ysb3 = ysb.rearrange("p (f t) -> p f t", t=BLK)

        for run in range(2):
            tc0 = run * BLK
            self.emit_ln(
                [(xlnT3[:, ft, :], F["B_xlnTf"]) for ft in range(16)],
                [(xr3[:, ft, tc0:tc0 + BLK], self.B_x) for ft in range(16)],
                BLK, None, F["lnstgf"], F["B_lnstgf"])
            for gi in range(44):
                bufs = []
                for half, grow in ((0, gi * 128), (1, FI + gi * 128)):
                    wbi = (gi % 2) * 2 + half
                    wb = F["wgu"][wbi]
                    Bwb = F["B_wgu"][wbi]
                    r = grow // GU_SH
                    lc = grow - r * GU_SH
                    b.op("gpsimd", lambda e, c, wb=wb, r=r, lc=lc: e.dma_start(
                        out=wb.rearrange("p (k q) -> p k q", k=16),
                        in_=self.d_guout[r * DM:(r + 1) * DM, lc:lc + 128]
                        .rearrange("(k p) q -> p k q", p=128)),
                        ins=[self.Bd_guout], outs=[Bwb], inc=wg, amt=16)
                    bufs.append((wb, Bwb))
                pg, pu = psB[gi % 2], psB[2 + gi % 2]
                vg, vu = ps[gi % 2], ps[2 + gi % 2]
                for half, (wb, Bwb) in enumerate(bufs):
                    pv = (vg, vu)[half]
                    pb = (pg, pu)[half]
                    wb3 = wb.rearrange("p (k q) -> p k q", k=16)
                    for k in range(16):
                        self.mm(pv[:, 0:BLK], wb3[:, k, :], xlnT3[:, k, :],
                                pb, [Bwb, F["B_xlnTf"]], start=(k == 0),
                                stop=(k == 15), first=(k == 0))
                b.op("scalar", lambda e, c, vg=vg: e.activation(
                    hstg, vg[:, 0:BLK], AF.Silu),
                    ins=[pg], outs=[F["B_hstg"]], inc=ac)
                b.op("vector", lambda e, c, gi=gi, vu=vu: e.tensor_mul(
                    hb3[:, gi, :], hstg, vu[:, 0:BLK]),
                    ins=[F["B_hstg"], pu], outs=[F["B_hbuf"]], inc=dv)
            for m2 in range(16):
                r = m2 // 2
                lc = (m2 % 2) * 128
                for kh in range(2):
                    wb = F["wdn"][kh]
                    Bwb = F["B_wdn"][kh]
                    b.op("gpsimd", lambda e, c, wb=wb, r=r, lc=lc, kh=kh:
                         e.dma_start(
                             out=wb.rearrange("p (k q) -> p k q", q=128),
                             in_=self.d_dnout[r * FI + kh * 22 * 128:
                                              r * FI + (kh + 1) * 22 * 128,
                                              lc:lc + 128].rearrange(
                                 "(k p) q -> p k q", p=128)),
                         ins=[self.Bd_dnout], outs=[Bwb], inc=wg, amt=16)
                pd = psB[6 + m2 % 2]
                vd = ps[6 + m2 % 2]
                for k2 in range(44):
                    wb = F["wdn"][k2 // 22]
                    Bwb = F["B_wdn"][k2 // 22]
                    wb3 = wb.rearrange("p (k q) -> p k q", q=128)
                    self.mm(vd[:, 0:BLK], wb3[:, k2 % 22, :], hb3[:, k2, :],
                            pd, [Bwb, F["B_hbuf"]], start=(k2 == 0),
                            stop=(k2 == 43), first=(k2 == 0))
                b.op("vector", lambda e, c, m2=m2, vd=vd: e.tensor_copy(
                    ysb3[:, m2, :], vd[:, 0:BLK]),
                    ins=[pd], outs=[B_ysb], inc=dv)
            for ft in range(16):
                b.op("scalar", lambda e, c, ft=ft: e.activation(
                    ysq, ysb3[:, ft, :], AF.Square),
                    ins=[B_ysb], outs=[F["B_ysqf"]], inc=ac)
                self.mm(ps[4][0:1, 0:BLK], self.onesb[:, 0:1], ysq,
                        psB[4], [F["B_ysqf"], self.B_c], start=(ft == 0),
                        stop=(ft == 15), first=(ft == 0))
            rr = self.ln_rows[0:1, 0:BLK]
            Brr = self.B_lnrows
            b.op("vector", lambda e, c: e.tensor_scalar_mul(
                rr, ps[4][0:1, 0:BLK], 1.0 / DM),
                ins=[psB[4]], outs=[Brr], inc=dv)
            b.op("scalar", lambda e, c: e.activation(
                rr, rr, AF.Sqrt, bias=self.cc[0:1, 2:3]),
                ins=[Brr, self.B_c], outs=[Brr], inc=ac)
            b.op("vector", lambda e, c: e.reciprocal(rr, rr),
                 ins=[Brr], outs=[Brr], inc=dv)
            self.mm(ps[5][:, 0:BLK], self.onesf[0:1, :], rr, psB[5],
                    [Brr, self.B_c], start=True, stop=True, first=True)
            for ft in range(16):
                b.op("vector", lambda e, c, ft=ft: e.tensor_mul(
                    ysq, ysb3[:, ft, :], ps[5][:, 0:BLK]),
                    ins=[B_ysb, psB[5]], outs=[F["B_ysqf"]], inc=dv)
                b.op("vector", lambda e, c, ft=ft, tc0=tc0: e.tensor_add(
                    xr3[:, ft, tc0:tc0 + BLK], xr3[:, ft, tc0:tc0 + BLK],
                    ysq),
                    ins=[F["B_ysqf"], self.B_x], outs=[self.B_x], inc=dv)
        af.release()
        ab.release()


    def emit_attn(self):
        nc, b = self.nc, self.b
        af, ab = self.af, self.ab
        dv, ac, ld, wg, gp = (self.s_dv, self.s_ac, self.s_ld, self.s_wg,
                              self.s_gp)
        ps, psB, pp = self.ps, self.psB, self.pp
        af.mark()
        ab.mark()
        A = {}
        seed = self.phase_seed()

        def abuf(n, name, ar=None):
            A[name] = (ar or ab).alloc(n, name)
            bb = Buf(name)
            bb.freed = list(seed)
            A["B_" + name] = bb
            return A[name]

        xln = abuf(16 * TPC, "xln")
        abuf(1024, "lnstga")
        qa = abuf(1024, "qa")
        qsb = abuf(1024, "qsb")
        kc = abuf(1024, "kc")

        wqa = abuf(16 * 64, "wqa")
        wqb = abuf(16 * 128, "wqb")
        A["wkva"] = [ab.alloc(16 * 128, f"wkva{i}") for i in range(2)]
        A["B_wkva"] = []
        for i in range(2):
            bb = Buf(f"wkva{i}")
            bb.freed = list(seed)
            A["B_wkva"].append(bb)
        wkvb = abuf(16 * 128, "wkvb")
        cos64 = abuf(TPC, "cos64")
        sin64 = abuf(TPC, "sin64")
        kst = abuf(TPC, "kst")
        vst = abuf(TPC, "vst")
        abuf(TPC, "vfm")
        abuf(BLK, "pacc")
        abuf(TPC, "paccS", af)
        qh = abuf(TPC, "qh")
        A["pb"] = [ab.alloc(BLK, f"pb{i}") for i in range(4)]
        A["B_pb"] = []
        for i in range(4):
            bb = Buf(f"pb{i}")
            bb.freed = list(seed)
            A["B_pb"].append(bb)
        rlb = abuf(TPC, "rlb")
        osb = abuf(TPC, "osb")
        kcsq = rlb
        A["kcsq"] = kcsq
        A["B_kcsq"] = A["B_rlb"]
        zrowf = abuf(64, "zrowf", af)
        zrow = zrowf.bitcast(BF16)
        A["zrow"] = zrow
        A["B_zrow"] = A["B_zrowf"]
        theta = abuf(TPC, "theta", af)
        th2 = abuf(TPC, "th2", af)
        qi32 = abuf(TPC, "qi32", af)
        invf = abuf(2, "invf", af)
        sgn = abuf(1, "sgn", af)
        poss = self.ln_rows[0:1, 1024:2048]
        A["poss"] = poss
        A["B_poss"] = self.B_lnrows
        bnd = abuf(8, "bnd", af)
        negC = abuf(1, "negC", af)
        xr3 = self.xres.rearrange("p (f t) -> p f t", t=TPC)
        xln3 = xln.rearrange("p (f t) -> p f t", t=TPC)

        b.op("gpsimd", lambda e, c: e.memset(zrow, 0.0),
             outs=[A["B_zrow"]], inc=gp)

        b.op("gpsimd", lambda e, c: e.iota(
            invf[0:HEADS, 0:1], pattern=[[1, 1]], base=0, channel_multiplier=1,
            allow_small_or_imprecise_dtypes=True),
            outs=[A["B_invf"]], inc=gp)
        b.op("vector", lambda e, c: e.tensor_scalar(
            sgn[0:HEADS, :], invf[0:HEADS, 0:1], 31.5, None, ALU.is_gt),
            ins=[A["B_invf"]], outs=[A["B_sgn"]], inc=dv)
        b.op("vector", lambda e, c: e.scalar_tensor_tensor(
            invf[0:HEADS, 1:2], sgn[0:HEADS, :], -32.0, invf[0:HEADS, 0:1],
            ALU.mult, ALU.add),
            ins=[A["B_sgn"], A["B_invf"]], outs=[A["B_invf"]], inc=dv)
        import math as _m
        b.op("scalar", lambda e, c: e.activation(
            invf[0:HEADS, 0:1], invf[0:HEADS, 1:2], AF.Exp,
            scale=-_m.log(10000.0) / 32.0),
            ins=[A["B_invf"]], outs=[A["B_invf"]], inc=ac)
        b.op("gpsimd", lambda e, c: e.iota(
            poss[0:1, 0:TPC], pattern=[[0, 2], [1, BLK]], base=0,
            channel_multiplier=0, allow_small_or_imprecise_dtypes=True),
            outs=[A["B_poss"]], inc=gp)
        b.op("vector", lambda e, c: e.tensor_scalar_mul(
            bnd[0:1, 0:1], self.p_jcol[0:1, :], 512.0),
            ins=[self.B_par], outs=[A["B_bnd"]], inc=dv)
        b.op("vector", lambda e, c: e.tensor_scalar(
            bnd[0:1, 1:2], bnd[0:1, 0:1], -1.0, 7680.0, ALU.mult, ALU.add),
            ins=[A["B_bnd"]], outs=[A["B_bnd"]], inc=dv)
        b.op("vector", lambda e, c: e.tensor_scalar(
            poss[0:1, 0:BLK], poss[0:1, 0:BLK], bnd[0:1, 0:1], None, ALU.add),
            ins=[A["B_poss"], A["B_bnd"]], outs=[A["B_poss"]], inc=dv)
        b.op("vector", lambda e, c: e.tensor_scalar(
            poss[0:1, BLK:TPC], poss[0:1, BLK:TPC], bnd[0:1, 1:2], None,
            ALU.add),
            ins=[A["B_poss"], A["B_bnd"]], outs=[A["B_poss"]], inc=dv)
        self.tposef(poss[0:1, 0:0 + HEADS] if False else
                    self.ln_rows[0:1, 0:HEADS], self.B_lnrows,
                    invf[0:HEADS, 0:1], [A["B_invf"]], bank=5, r=HEADS, c=1)
        for half in range(2):
            self.mm(ps[half][0:HEADS, 0:BLK], self.ln_rows[0:1, 0:HEADS],
                    poss[0:1, half * BLK:(half + 1) * BLK], psB[half],
                    [self.B_lnrows, A["B_poss"]], start=True, stop=True,
                    first=True)
            b.op("vector", lambda e, c, half=half: e.tensor_copy(
                theta[0:HEADS, half * BLK:(half + 1) * BLK],
                ps[half][0:HEADS, 0:BLK]),
                ins=[psB[half]], outs=[A["B_theta"]], inc=dv)

        def trig(dst, dbuf, offs):
            TWO_PI = 2.0 * _m.pi
            b.op("vector", lambda e, c: e.tensor_scalar(
                th2[0:HEADS, :], theta[0:HEADS, :], 1.0 / TWO_PI,
                0.5 + offs / TWO_PI, ALU.mult, ALU.add),
                ins=[A["B_theta"]], outs=[A["B_th2"]], inc=dv)
            b.op("vector", lambda e, c: e.tensor_scalar(
                qi32[0:HEADS, :].bitcast(I32) if False else
                qi32[0:HEADS, :], th2[0:HEADS, :], 0.0, None, ALU.add),
                ins=[A["B_th2"]], outs=[A["B_qi32"]], inc=dv)
            b.op("vector", lambda e, c: e.tensor_copy(
                th2[0:HEADS, :].bitcast(I32), qi32[0:HEADS, :]),
                ins=[A["B_qi32"]], outs=[A["B_th2"]], inc=dv)
            b.op("vector", lambda e, c: e.scalar_tensor_tensor(
                th2[0:HEADS, :], th2[0:HEADS, :].bitcast(I32), -TWO_PI,
                theta[0:HEADS, :], ALU.mult, ALU.add),
                ins=[A["B_th2"], A["B_theta"]], outs=[A["B_th2"]], inc=dv)
            if offs != 0.0:
                b.op("vector", lambda e, c: e.tensor_scalar_add(
                    th2[0:HEADS, :], th2[0:HEADS, :], offs),
                    ins=[A["B_th2"]], outs=[A["B_th2"]], inc=dv)
            b.op("vector", lambda e, c: e.tensor_scalar(
                th2[0:HEADS, :], th2[0:HEADS, :], _m.pi, -_m.pi,
                ALU.min, ALU.max),
                ins=[A["B_th2"]], outs=[A["B_th2"]], inc=dv)
            b.op("scalar", lambda e, c, dst=dst: e.activation(
                dst[0:HEADS, :], th2[0:HEADS, :], AF.Sin),
                ins=[A["B_th2"]], outs=[dbuf], inc=ac)

        trig(sin64, A["B_sin64"], 0.0)
        trig(cos64, A["B_cos64"], _m.pi / 2.0)

        self.emit_ln(
            [(xln3[:, ft, 0:BLK], A["B_xln"]) for ft in range(16)],
            [(xr3[:, ft, 0:BLK], self.B_x) for ft in range(16)],
            BLK, (self.grows[0:1, 0:DM], self.B_rows), A["lnstga"],
            A["B_lnstga"])
        self.emit_ln(
            [(xln3[:, ft, BLK:TPC], A["B_xln"]) for ft in range(16)],
            [(xr3[:, ft, BLK:TPC], self.B_x) for ft in range(16)],
            BLK, (self.grows[0:1, 0:DM], self.B_rows), A["lnstga"],
            A["B_lnstga"])
        b.op("gpsimd", lambda e, c: e.dma_start(
            out=wqa.rearrange("p (k q) -> p k q", k=16),
            in_=self.i_wqa.rearrange("(k p) q -> p k q", p=128)),
            outs=[A["B_wqa"]], inc=wg, amt=16)
        wqa3 = wqa.rearrange("p (k q) -> p k q", k=16)
        for half in range(2):
            pb_ = psB[half]
            pv = ps[half]
            for k in range(16):
                self.mm(pv[0:QR, 0:BLK], wqa3[:, k, :],
                        xln3[:, k, half * BLK:(half + 1) * BLK], pb_,
                        [A["B_wqa"], A["B_xln"]], start=(k == 0),
                        stop=(k == 15), first=(k == 0))
            b.op("vector", lambda e, c, half=half, pv=pv: e.tensor_copy(
                qa[0:QR, half * BLK:(half + 1) * BLK], pv[0:QR, 0:BLK]),
                ins=[pb_], outs=[A["B_qa"]], inc=dv)
        b.op("gpsimd", lambda e, c: e.dma_start(
            out=wqb[0:QR, :].rearrange("p (k q) -> p k q", k=16),
            in_=self.i_wqb.rearrange("a (k q) -> a k q", q=128)),
            outs=[A["B_wqb"]], inc=wg, amt=16)
        wqb3 = wqb[0:QR, :].rearrange("p (k q) -> p k q", k=16)
        maxq = bnd[0:1, 2:3]

        sgnc = bnd[0:HEADS, 4:5]
        b.op("vector", lambda e, c: e.tensor_scalar(
            sgnc, sgn[0:HEADS, :], 2.0, -1.0, ALU.mult, ALU.add),
            ins=[A["B_sgn"]], outs=[A["B_bnd"]], inc=dv)
        b.op("vector", lambda e, c: e.tensor_scalar(
            sin64[0:HEADS, :], sin64[0:HEADS, :], sgnc, None, ALU.mult),
            ins=[A["B_sin64"], A["B_bnd"]], outs=[A["B_sin64"]], inc=dv)

        rotb = th2
        Brotb = A["B_th2"]

        def rope_evac(dst, dbuf, srcv, srcb):
            for half in range(2):
                sl = slice(half * BLK, (half + 1) * BLK)
                sv, sb_ = srcv[half], srcb[half]
                shmask = (list(range(8, 16)) + list(range(0, 8))
                          + list(range(16, 32)))
                b.op("vector", lambda e, c, sv=sv, shmask=shmask:
                     e.stream_shuffle(
                         rotb[0:HEADS, 0:BLK], sv[0:HEADS, 0:BLK], shmask),
                     ins=[sb_], outs=[Brotb], inc=dv)
                b.op("vector", lambda e, c, sl=sl: e.tensor_mul(
                    rotb[0:HEADS, 0:BLK], rotb[0:HEADS, 0:BLK],
                    sin64[0:HEADS, sl]),
                    ins=[Brotb, A["B_sin64"]], outs=[Brotb], inc=dv)
                b.op("vector", lambda e, c, sv=sv, sl=sl, dst=dst:
                     e.tensor_mul(dst[0:HEADS, sl], sv[0:HEADS, 0:BLK],
                                  cos64[0:HEADS, sl]),
                     ins=[sb_, A["B_cos64"]], outs=[dbuf], inc=dv)
                b.op("vector", lambda e, c, sl=sl, dst=dst: e.tensor_add(
                    dst[0:HEADS, sl], dst[0:HEADS, sl], rotb[0:HEADS, 0:BLK]),
                    ins=[Brotb, dbuf], outs=[dbuf], inc=dv)
                b.op("vector", lambda e, c, sv=sv, sl=sl, dst=dst:
                     e.tensor_copy(dst[HEADS:128, sl], sv[HEADS:128, 0:BLK]),
                     ins=[sb_], outs=[dbuf], inc=dv)

        for h in range(NH):
            pbs = [psB[2], psB[3]]
            pvs = [ps[2], ps[3]]
            for half in range(2):
                self.mm(pvs[half][:, 0:BLK], wqb3[:, h, :],
                        qa[0:QR, half * BLK:(half + 1) * BLK], pbs[half],
                        [A["B_wqb"], A["B_qa"]], start=True, stop=True,
                        first=True)
            rope_evac(qsb, A["B_qsb"], pvs, pbs)
            b.op("scalar", lambda e, c: e.activation(
                kcsq, qsb, AF.Square), ins=[A["B_qsb"]],
                outs=[A["B_kcsq"]], inc=ac)
            for half in range(2):
                self.mm(ps[4 + half][0:1, 0:BLK], self.onesb[:, 0:1],
                        kcsq[:, half * BLK:(half + 1) * BLK], psB[4 + half],
                        [A["B_kcsq"], self.B_c], start=True, stop=True,
                        first=True)
                b.op("vector", lambda e, c, half=half: e.reduce_max(
                    bnd[0:1, 3:4], ps[4 + half][0:1, 0:BLK], AXX),
                    ins=[psB[4 + half]], outs=[A["B_bnd"]], inc=dv)
                if h == 0 and half == 0:
                    b.op("vector", lambda e, c: e.tensor_copy(
                        maxq, bnd[0:1, 3:4]),
                        ins=[A["B_bnd"]], outs=[A["B_bnd"]], inc=dv)
                else:
                    b.op("vector", lambda e, c: e.tensor_tensor(
                        maxq, maxq, bnd[0:1, 3:4], ALU.max),
                        ins=[A["B_bnd"]], outs=[A["B_bnd"]], inc=dv)
            b.op("sync", lambda e, c, h=h: e.dma_start(
                out=self.d_q[h * DH:(h + 1) * DH, :], in_=qsb),
                ins=[A["B_qsb"]], outs=[self.Bd_q], inc=ld, amt=16)

        self.emit_ln(
            [(xln3[:, ft, 0:BLK], A["B_xln"]) for ft in range(16)],
            [(xr3[:, ft, 0:BLK], self.B_x) for ft in range(16)],
            BLK, (self.grows[0:1, DM:2 * DM], self.B_rows), A["lnstga"],
            A["B_lnstga"])
        self.emit_ln(
            [(xln3[:, ft, BLK:TPC], A["B_xln"]) for ft in range(16)],
            [(xr3[:, ft, BLK:TPC], self.B_x) for ft in range(16)],
            BLK, (self.grows[0:1, DM:2 * DM], self.B_rows), A["lnstga"],
            A["B_lnstga"])
        vsb = qsb
        B_vsb = A["B_qsb"]
        vflat = [self.d_kvin[r][EMB:2 * EMB, :].flatten()
                 .rearrange("(t d) -> t d", d=EMB) for r in range(2)]
        Bd_kvin = [Buf("kvin0"), Buf("kvin1")]
        b.op("gpsimd", lambda e, c: e.dma_start(
            out=wkvb.rearrange("p (k q) -> p k q", k=16),
            in_=self.i_wkvb.rearrange("a (k q) -> a k q", q=128)),
            outs=[A["B_wkvb"]], inc=wg, amt=16)
        for h2 in range(2):
            b.op("scalar", lambda e, c, h2=h2: e.activation(
                kcsq[:, 0:TPC], wkvb[0:KR, h2 * TPC:(h2 + 1) * TPC],
                AF.Square),
                ins=[A["B_wkvb"]], outs=[A["B_kcsq"]], inc=ac)
            for q4 in range(2):
                self.mm(ps[5][0:1, 0:BLK], self.onesb[0:KR, 0:1],
                        kcsq[0:KR, q4 * 512:(q4 + 1) * 512], psB[5],
                        [A["B_kcsq"], self.B_c], start=(h2 == 0 and q4 == 0),
                        stop=(h2 == 1 and q4 == 1),
                        first=(h2 == 0 and q4 == 0))
        b.op("vector", lambda e, c: e.reduce_sum(
            bnd[0:1, 5:6], ps[5][0:1, 0:BLK], AXX),
            ins=[psB[5]], outs=[A["B_bnd"]], inc=dv)
        for m in range(17):
            wb = A["wkva"][m % 2]
            Bwb = A["B_wkva"][m % 2]
            b.op("gpsimd", lambda e, c, m=m, wb=wb: e.dma_start(
                out=wb.rearrange("p (k q) -> p k q", k=16),
                in_=self.i_wkva[:, m * 128:(m + 1) * 128].rearrange(
                    "(k p) q -> p k q", p=128)),
                outs=[Bwb], inc=wg, amt=16)
            vstg = A["pb"][0]
            B_vstg = A["B_pb"][0]
            wb3 = wb.rearrange("p (k q) -> p k q", k=16)
            for half in range(2):
                pb_ = psB[half]
                pv = ps[half]
                for k in range(16):
                    self.mm(pv[:, 0:BLK], wb3[:, k, :],
                            xln3[:, k, half * BLK:(half + 1) * BLK], pb_,
                            [Bwb, A["B_xln"]], start=(k == 0),
                            stop=(k == 15), first=(k == 0))
                if m == 0:
                    b.op("vector", lambda e, c, half=half, pv=pv:
                         e.tensor_copy(
                             kc[:, half * BLK:(half + 1) * BLK],
                             pv[:, 0:BLK]),
                         ins=[pb_], outs=[A["B_kc"]], inc=dv)
                else:
                    hh = m - 1
                    b.op("vector", lambda e, c, pv=pv:
                         e.tensor_copy(vsb[:, 0:BLK], pv[:, 0:BLK]),
                         ins=[pb_], outs=[B_vsb], inc=dv)
                    b.op("sync", lambda e, c, half=half, hh=hh:
                         e.dma_start(
                             out=self.d_kvin[half][EMB + hh * DH:
                                                   EMB + (hh + 1) * DH, :],
                             in_=vsb[:, 0:BLK]),
                         ins=[B_vsb], outs=[Bd_kvin[half]], inc=ld,
                         amt=16)
        b.op("scalar", lambda e, c: e.activation(
            kcsq, kc, AF.Square), ins=[A["B_kc"]], outs=[A["B_kcsq"]], inc=ac)
        for half in range(2):
            self.mm(ps[4][0:1, 0:BLK], self.onesb[:, 0:1],
                    kcsq[:, half * BLK:(half + 1) * BLK], psB[4],
                    [A["B_kcsq"], self.B_c], start=True, stop=True, first=True)
            b.op("vector", lambda e, c: e.reduce_max(
                bnd[0:1, 3:4], ps[4][0:1, 0:BLK], AXX),
                ins=[psB[4]], outs=[A["B_bnd"]], inc=dv)
            b.op("vector", lambda e, c: e.tensor_tensor(
                bnd[0:1, 3:4], bnd[0:1, 3:4], bnd[0:1, 5:6], ALU.mult),
                ins=[A["B_bnd"]], outs=[A["B_bnd"]], inc=dv)
            b.op("vector", lambda e, c: e.tensor_scalar_mul(
                bnd[0:1, 3:4], bnd[0:1, 3:4], 1.02),
                ins=[A["B_bnd"]], outs=[A["B_bnd"]], inc=dv)
            b.op("vector", lambda e, c: e.memset(
                self.ln_rows[0:1, 0:BLK], 0.0),
                ins=[], outs=[self.B_lnrows], inc=dv)
            b.op("vector", lambda e, c: e.tensor_copy(
                self.ln_rows[0:1, 0:1], bnd[0:1, 3:4]),
                ins=[A["B_bnd"]], outs=[self.B_lnrows], inc=dv)
            self.odma("gpsimd", lambda e, c, half=half: e.dma_start(
                out=self.d_kvin[half][4096:4097, :],
                in_=self.ln_rows[0:1, 0:BLK]),
                ins=[self.B_lnrows], outs=[Bd_kvin[half]])
        wkvb3 = wkvb.rearrange("p (k q) -> p k q", k=16)
        for h in range(NH):
            pbs = [psB[2], psB[3]]
            pvs = [ps[2], ps[3]]
            for half in range(2):
                self.mm(pvs[half][:, 0:BLK], wkvb3[:, h, :],
                        kc[:, half * BLK:(half + 1) * BLK], pbs[half],
                        [A["B_wkvb"], A["B_kc"]], start=True, stop=True,
                        first=True)
            rope_evac(qsb, A["B_qsb"], pvs, pbs)
            for half in range(2):
                b.op("sync", lambda e, c, h=h, half=half: e.dma_start(
                    out=self.d_kvin[half][h * DH:(h + 1) * DH, :],
                    in_=qsb[:, half * BLK:(half + 1) * BLK]),
                    ins=[A["B_qsb"]], outs=[Bd_kvin[half]], inc=ld, amt=16)
        self.Bd_kvout = [Buf("kvout0"), Buf("kvout1")]
        for r in range(2):
            self.b.op("gpsimd", lambda e, c, r=r: e.collective_compute(
                "AllGather", ALU.bypass, replica_groups=[list(range(NCORE))],
                ins=[self.d_kvin[r].opt()], outs=[self.d_kvout[r].opt()]),
                ins=[Bd_kvin[r]], inc=self.s_agkv[r])
            self.Bd_kvout[r].ready = [(self.s_agkv[r], 1)]
        stg8 = self.ln_rows[0:8, 0:2]
        for r in range(2):
            self.odma("gpsimd", lambda e, c, r=r: e.dma_start(
                out=stg8[:, r:r + 1],
                in_=self.d_kvout[r].rearrange("(s q) t -> s q t", q=4097)
                [:, 4096, 0:1],
                allow_slow_non_contiguous=True),
                ins=[self.Bd_kvout[r]], outs=[self.B_lnrows])
        self.tposef(self.ln_rows[0:1, 8:16], self.B_lnrows, stg8[:, 0:1],
                    [self.B_lnrows], bank=5, r=8, c=1)
        self.tposef(self.ln_rows[0:1, 16:24], self.B_lnrows, stg8[:, 1:2],
                    [self.B_lnrows], bank=5, r=8, c=1)
        b.op("vector", lambda e, c: e.reduce_max(
            bnd[0:1, 6:7], self.ln_rows[0:1, 8:24], AXX),
            ins=[self.B_lnrows], outs=[A["B_bnd"]], inc=dv)
        b.op("vector", lambda e, c: e.tensor_tensor(
            bnd[0:1, 6:7], bnd[0:1, 6:7], maxq, ALU.mult),
            ins=[A["B_bnd"]], outs=[A["B_bnd"]], inc=dv)
        b.op("scalar", lambda e, c: e.activation(
            bnd[0:1, 6:7], bnd[0:1, 6:7], AF.Sqrt),
            ins=[A["B_bnd"]], outs=[A["B_bnd"]], inc=ac)
        b.op("vector", lambda e, c: e.tensor_scalar_mul(
            bnd[0:1, 6:7], bnd[0:1, 6:7], -1.0 / float(DH) ** 0.5),
            ins=[A["B_bnd"]], outs=[A["B_bnd"]], inc=dv)
        b.op("vector", lambda e, c: e.tensor_copy(
            self.ln_rows[0:1, 24:25], bnd[0:1, 6:7]),
            ins=[A["B_bnd"]], outs=[self.B_lnrows], inc=dv)
        self.mm(ps[5][:, 0:1], self.onesf[0:1, :], self.ln_rows[0:1, 24:25],
                psB[5], [self.B_lnrows, self.B_c], start=True, stop=True,
                first=True)
        b.op("vector", lambda e, c: e.tensor_copy(negC[:, 0:1], ps[5][:, 0:1]),
             ins=[psB[5]], outs=[A["B_negC"]], inc=dv)

        self.emit_flash(A, Bd_kvin)
        self.emit_wo(A)
        af.release()
        ab.release()

    def emit_flash(self, A, Bd_kvin):
        import bass_rust
        b = self.b
        dv, ac, ld, wg = self.s_dv, self.s_ac, self.s_ld, self.s_wg
        ps, psB, pp = self.ps, self.psB, self.pp
        kst, vst, qh = A["kst"], A["vst"], A["qh"]
        pacc = A["pacc"]
        SC = 1.0 / float(DH) ** 0.5
        ppS = [pp[0][:, 0:512], pp[0][:, 512:1024]]
        pbS = [psB[0], psB[1]]
        ppO = pp[2]
        pbO = Buf("ppO")
        pbO.freed = (psB[4].ready + psB[4].freed + psB[5].ready
                     + psB[5].freed)
        ppL = pp[3]
        pbL = Buf("ppL")
        pbL.freed = (psB[6].ready + psB[6].freed + psB[7].ready
                     + psB[7].freed)
        zb = A["zrow"]
        B_kst = [Buf("kst0"), Buf("kst1")]
        B_vst = [Buf("vst0"), Buf("vst1")]
        B_vfm = [Buf("vfm0"), Buf("vfm1")]
        for bb in B_kst + B_vst + B_vfm:
            bb.freed = list(A["B_kst"].freed)
        B_pacc = Buf("pacc")
        B_pacc.freed = list(A["B_pacc"].freed)
        Bd_kvall = Buf("kvall")
        Bd_kvall.ready = [(self.s_agkv[0], 1), (self.s_agkv[1], 1)]
        B_astg = Buf("astg")

        for s in range(1, 8):
            eng, qsem = "scalar", self.s_rs[s % 2]

            def restage(e, c, s=s):
                cidr = e.to_reg(e.partition_id())
                t = e.alloc_register(f"rs_t{s}")
                rk = e.alloc_register(f"rs_k{s}")
                e.reg_alu(t, cidr, 17 - 2 * s, ALU.add)
                e.reg_alu(rk, cidr, s, ALU.is_lt)
                e.reg_alu(t, t, rk, ALU.mult)
                e.reg_alu(rk, t, s - 1, ALU.add)
                e.reg_alu(rk, rk, 4097, ALU.mult)
                e.free_register(t)
                rv = bass_rust.RuntimeValue(rk, min_val=0,
                                            max_val=15 * 4097)
                return e.dma_start(
                    out=self.d_astg[(s - 1) * 4096:s * 4096, :],
                    in_=self.d_kvall[bass.ds(rv, 4096), :])
            b.op(eng, restage, ins=[Bd_kvall], outs=[B_astg], inc=qsem,
                 amt=16, relax=True)
        B_astg.ready = [(self.s_rs[0], self.s_rs[0].n),
                        (self.s_rs[1], self.s_rs[1].n)]

        def qsrv(e, c, s):
            if "fl_qs" not in c._c:
                cidr = e.to_reg(e.partition_id())
                t = e.alloc_register("fl_t")
                rvs = {}
                for s2 in range(1, 8):
                    rq = e.alloc_register(f"fl_qs{s2}")
                    e.reg_alu(t, cidr, s2, ALU.is_lt)
                    e.reg_alu(rq, t, 512, ALU.mult)
                    rvs[s2] = bass_rust.RuntimeValue(
                        rq, min_val=0, max_val=512)
                e.free_register(t)
                c._c["fl_qs"] = rvs
            return c._c["fl_qs"][s]

        def qcol(e, c, sl, t, row1=False):
            pfx = t[0:1, :] if row1 else t[:, :]
            if sl == 0:
                return pfx[:, 0:512]
            if sl >= 8:
                return pfx[:, 512:1024]
            return pfx[:, bass.ds(qsrv(e, c, sl), 512)]

        for h in range(NH):
            b.op("sync", lambda e, c, h=h: e.dma_start(
                out=qh, in_=self.d_q[h * DH:(h + 1) * DH, :]),
                ins=[self.Bd_q], outs=[A["B_qh"]], inc=self.s_ld, amt=16)
            for side in range(2):
                ov = ppO[:, side * 512:(side + 1) * 512]
                wl = list(pbO.ready) + (pbO.freed if side == 0 else []) \
                    + A["B_qh"].ready + A["B_zrow"].ready
                n = b.raw("tensor", lambda e, c, ov=ov: e.matmul(
                    ov, zb[0:1, 0:128], qh[0:1, 0:512], start=True,
                    stop=False), waits=wl, inc=self.s_pe)
                pbO.ready = [(self.s_pe, n)]
                if side == 0:
                    pbO.freed = []
            b.op("vector", lambda e, c: e.memset(A["paccS"], 0.0),
                 ins=[], outs=[A["B_paccS"]], inc=dv)

            def stage_dma(sl, h, kmode):
                par = sl % 2
                ko = kst[:, par * 512:(par + 1) * 512]
                vf = A["vfm"][:, par * 512:(par + 1) * 512]
                if sl == 0 or sl == 8:
                    r = 0 if sl == 0 else 1
                    ksrc = self.d_kvin[r][h * DH:(h + 1) * DH, :]
                    vsrc = self.d_kvin[r][EMB + h * DH:
                                          EMB + (h + 1) * DH, :]
                    kdep = vdep = Bd_kvin[r]
                elif sl <= 7:
                    base = (sl - 1) * 4096
                    ksrc = self.d_astg[base + h * DH:
                                       base + (h + 1) * DH, :]
                    vsrc = self.d_astg[base + EMB + h * DH:
                                       base + EMB + (h + 1) * DH, :]
                    kdep = vdep = B_astg
                else:
                    seg = sl - 9
                    base = seg * 4097
                    ksrc = self.d_kvall[base + h * DH:
                                        base + (h + 1) * DH, :]
                    vsrc = self.d_kvall[base + EMB + h * DH:
                                        base + EMB + (h + 1) * DH, :]
                    kdep = vdep = Bd_kvall
                if kmode:
                    b.op("gpsimd", lambda e, c, ko=ko, ksrc=ksrc:
                         e.dma_start(out=ko, in_=ksrc),
                         ins=[kdep], outs=[B_kst[par]],
                         inc=self.s_kq[par], amt=16, relax=True)
                else:
                    b.op("gpsimd", lambda e, c, vf=vf, vsrc=vsrc:
                         e.dma_start(out=vf, in_=vsrc),
                         ins=[vdep], outs=[B_vfm[par]],
                         inc=self.s_vq[par], amt=16, relax=True)

            def tpose_slot(sl):
                par = sl % 2
                for w in range(4):
                    self.tposeb(
                        vst[:, par * 512 + w * 128:par * 512 +
                            (w + 1) * 128],
                        B_vst[par],
                        A["vfm"][:, par * 512 + w * 128:par * 512 +
                                 (w + 1) * 128],
                        [B_vfm[par]], bank=2 + w % 2, r=128, c=128)

            for sl2 in range(2):
                stage_dma(sl2, h, True)
                stage_dma(sl2, h, False)
            tpose_slot(0)
            for sl in range(17):
                par = sl % 2
                diag = sl in (0, 8)
                if sl + 2 <= 16:
                    stage_dma(sl + 2, h, False)
                if sl + 1 <= 16:
                    tpose_slot(sl + 1)
                for w in range(4):
                    pS = ppS[w % 2]
                    pbSw = pbS[w % 2]
                    pbuf = A["pb"][w]
                    pbufB = A["B_pb"][w]
                    wl = (list(pbSw.ready) + list(pbSw.freed)
                          + B_kst[par].ready + A["B_qh"].ready)
                    n = b.raw("tensor", lambda e, c, pS=pS, w=w, par=par,
                              sl=sl: e.matmul(
                                  pS,
                                  kst[:, par * 512 + w * 128:
                                      par * 512 + (w + 1) * 128],
                                  qcol(e, c, sl, qh), start=True,
                                  stop=True),
                              waits=wl, inc=self.s_pe)
                    pbSw.ready = [(self.s_pe, n)]
                    pbSw.freed = []
                    B_kst[par].freed.append((self.s_pe, n))
                    b.op("scalar", lambda e, c, pS=pS, pbuf=pbuf:
                         e.activation(pbuf, pS, AF.Exp, scale=SC,
                                      bias=A["negC"][:, 0:1]),
                         ins=[pbSw, A["B_negC"]], outs=[pbufB],
                         inc=self.s_ac)
                    if diag:
                        b.op("vector", lambda e, c, pbuf=pbuf, w=w:
                             e.tensor_mul(
                                 pbuf, pbuf,
                                 self.fmask[:, w * BLK:(w + 1) * BLK]),
                             ins=[pbufB, self.B_c], outs=[pbufB],
                             inc=self.s_dv)
                    if w == 0:
                        b.op("vector", lambda e, c, pbuf=pbuf:
                             e.tensor_copy(pacc, pbuf),
                             ins=[pbufB], outs=[B_pacc], inc=dv)
                    else:
                        b.op("vector", lambda e, c, pbuf=pbuf:
                             e.tensor_add(pacc, pacc, pbuf),
                             ins=[pbufB, B_pacc], outs=[B_pacc], inc=dv)
                for w in range(4):
                    pbuf = A["pb"][w]
                    pbufB = A["B_pb"][w]
                    wl = list(pbO.ready) + pbufB.ready + B_vst[par].ready
                    n = b.raw("tensor", lambda e, c, pbuf=pbuf, w=w,
                              par=par, sl=sl: e.matmul(
                                  qcol(e, c, sl, ppO),
                                  vst[:, par * 512 + w * 128:
                                      par * 512 + (w + 1) * 128],
                                  pbuf, start=False, stop=False),
                              waits=wl, inc=self.s_pe)
                    pbO.ready = [(self.s_pe, n)]
                    pbufB.freed.append((self.s_pe, n))
                    B_vst[par].freed.append((self.s_pe, n))
                b.op("vector", lambda e, c, sl=sl: e.tensor_add(
                    qcol(e, c, sl, A["paccS"]),
                    qcol(e, c, sl, A["paccS"]), pacc),
                    ins=[B_pacc, A["B_paccS"]], outs=[A["B_paccS"]],
                    inc=dv)
                if sl + 2 <= 16:
                    stage_dma(sl + 2, h, True)
            for side in range(2):
                ov = ppO[:, side * 512:(side + 1) * 512]
                n = b.raw("tensor", lambda e, c, ov=ov: e.matmul(
                    ov, zb[0:1, 0:128], qh[0:1, 0:512], start=False,
                    stop=True), waits=list(pbO.ready), inc=self.s_pe)
                pbO.ready = [(self.s_pe, n)]
                wl = (list(pbL.ready) + list(pbL.freed)
                      + A["B_paccS"].ready + self.B_c.ready)
                n = b.raw("tensor", lambda e, c, side=side: e.matmul(
                    ppL[0:1, side * 512:(side + 1) * 512],
                    self.onesf[:, 0:1],
                    A["paccS"][:, side * 512:(side + 1) * 512],
                    start=True, stop=True), waits=wl, inc=self.s_pe)
                pbL.ready = [(self.s_pe, n)]
                pbL.freed = [] if side == 0 else pbL.freed
                A["B_paccS"].freed.append((self.s_pe, n))
            rl = self.ln_rows[0:1, 0:TPC]
            b.op("vector", lambda e, c: e.reciprocal(rl, ppL[0:1, :]),
                 ins=[pbL], outs=[self.B_lnrows], inc=dv)
            for half in range(2):
                self.mm(ps[2 + half][:, 0:BLK], self.onesf[0:1, :],
                        rl[0:1, half * BLK:(half + 1) * BLK], psB[2 + half],
                        [self.B_lnrows, self.B_c], start=True, stop=True,
                        first=True)
                b.op("vector", lambda e, c, half=half: e.tensor_copy(
                    A["rlb"][:, half * BLK:(half + 1) * BLK],
                    ps[2 + half][:, 0:BLK]),
                    ins=[psB[2 + half]], outs=[A["B_rlb"]], inc=dv)
            b.op("vector", lambda e, c: e.tensor_mul(
                A["osb"], ppO[:, :], A["rlb"]),
                ins=[pbO, A["B_rlb"]], outs=[A["B_osb"]], inc=dv)
            b.op("sync", lambda e, c, h=h: e.dma_start(
                out=self.d_q[h * DH:(h + 1) * DH, :], in_=A["osb"]),
                ins=[A["B_osb"]], outs=[self.Bd_q], inc=self.s_ld, amt=16)
        for i in (4, 5):
            psB[i].ready = list(pbO.ready)
            psB[i].freed = list(pbO.freed)
        for i in (6, 7):
            psB[i].ready = list(pbL.ready)
            psB[i].freed = list(pbL.freed)

    def emit_wo(self, A):
        b = self.b
        dv, wg, ld = self.s_dv, self.s_wg, self.s_ld
        ps, psB = self.ps, self.psB
        xr3 = self.xres.rearrange("p (f t) -> p f t", t=TPC)
        wob = [A["wkva"][i] for i in range(2)]
        Bwob = [A["B_wkva"][i] for i in range(2)]
        obuf = A["xln"][:, 0:8192]
        obuf2 = A["xln"][:, 8192:16384]
        B_obuf = Buf("obuf")
        B_obuf.freed = A["B_xln"].freed + A["B_xln"].ready
        B_obuf2 = Buf("obuf2")
        B_obuf2.freed = A["B_xln"].freed + A["B_xln"].ready
        for k in range(16):
            b.op("sync", lambda e, c, k=k: e.dma_start(
                out=obuf[:, k * 512:(k + 1) * 512],
                in_=self.d_q[k * DH:(k + 1) * DH, 0:512]),
                ins=[self.Bd_q], outs=[B_obuf], inc=ld, amt=16)
            b.op("sync", lambda e, c, k=k: e.dma_start(
                out=obuf2[:, k * 512:(k + 1) * 512],
                in_=self.d_q[k * DH:(k + 1) * DH, 512:1024]),
                ins=[self.Bd_q], outs=[B_obuf2], inc=ld, amt=16)
        for m in range(16):
            wb, Bwb = wob[m % 2], Bwob[m % 2]
            b.op("gpsimd", lambda e, c, m=m, wb=wb: e.dma_start(
                out=wb.rearrange("p (k q) -> p k q", k=16),
                in_=self.i_wo[:, m * 128:(m + 1) * 128].rearrange(
                    "(k p) q -> p k q", p=128)),
                outs=[Bwb], inc=wg, amt=16)
            wb3 = wb.rearrange("p (k q) -> p k q", k=16)
            for half in range(2):
                pb_ = psB[4 + half]
                pv = ps[4 + half]
                src = obuf if half == 0 else obuf2
                sbuf = B_obuf if half == 0 else B_obuf2
                for k in range(16):
                    self.mm(pv[:, 0:BLK], wb3[:, k, :],
                            src[:, k * 512:(k + 1) * 512], pb_,
                            [Bwb, sbuf], start=(k == 0), stop=(k == 15),
                            first=(k == 0))
                tc0 = half * BLK
                b.op("vector", lambda e, c, m=m, tc0=tc0, pv=pv:
                     e.scalar_tensor_tensor(
                         xr3[:, m, tc0:tc0 + BLK], pv[:, 0:BLK], 1.0,
                         xr3[:, m, tc0:tc0 + BLK], ALU.mult, ALU.add),
                     ins=[pb_, self.B_x], outs=[self.B_x], inc=dv)

    def emit_store(self):
        b = self.b
        b.op("sync", lambda e, c: e.dma_start(
            out=self.o_out.rearrange("(f p) t -> p f t", p=128),
            in_=self.xres.rearrange("p (f t) -> p f t", t=TPC)),
            ins=[self.B_x], outs=[Buf("out")], inc=self.s_ld, amt=16)

    def finish(self):
        b = self.b
        sems = [self.s_ld, self.s_wg, self.s_pe, self.s_dv, self.s_ac,
                self.s_gp, self.s_od, self.s_odg, self.s_aq] \
            + self.s_kq + self.s_vq + self.s_vt + self.s_rs + self.s_agst \
            + [self.s_aggu, self.s_agdn] + self.s_agkv + self.s_aglr
        waits = [(s, s.n) for s in sems if s.n > 0]
        b.raw("gpsimd", lambda e, c: None, waits=waits)

    def build(self):
        self.declare()
        self.mem()
        self.sems()
        self.prelude()
        if self.dbg != "pre":
            self.mamba()
        if self.dbg in ("f2", "a3", ""):
            self.emit_dequant()
            self.emit_ffn(0)
        if self.dbg in ("a3", ""):
            self.emit_attn()
        if self.dbg == "":
            self.emit_ffn(1)
            self.emit_store()
        if self.dbg in ("pre", "ip0", "ln0", "f0", "ff", "m1", "f2", "a3"):
            o = self.dout("xres", DM, TPC)
            self.b.op("sync", lambda e, c: e.dma_start(
                out=o.rearrange("(f p) t -> p f t", p=128),
                in_=self.xres.rearrange("p (f t) -> p f t", t=TPC)),
                ins=[self.B_x], outs=[Buf("dbgo")], inc=self.s_ld, amt=16)
            self.finish()
            self.b.finalize()
            return
        self.finish()
        self.b.finalize()


def _prep_core_inputs(inputs, j):
    f32 = np.float32
    x = np.asarray(inputs["L0_attn_a_x"], f32)[0]
    xT = np.ascontiguousarray(x.T)
    bA, bB = j, 15 - j
    xc = np.zeros((DM, TPC + 6), f32)
    for slot, bb in ((0, bA), (1, bB)):
        c0 = slot * 515
        t0 = bb * BLK
        if t0 >= 3:
            xc[:, c0:c0 + 3] = xT[:, t0 - 3:t0]
        xc[:, c0 + 3:c0 + 3 + BLK] = xT[:, t0:t0 + BLK]

    D = np.asarray(inputs["L0_attn_a_D"], f32)
    dp = np.zeros((128, 32), f32)
    for t in range(32):
        dp[0:64, t] = D[2 * t]
        dp[64:128, t] = D[2 * t + 1]

    wkva = np.asarray(inputs["L0_attn_b_w_kva"], f32)
    used = [wkva[:, 0:KR]]
    for h in range(NH):
        c0 = KR + h * 2 * DH
        used.append(wkva[:, c0:c0 + DH])
    wkva_used = np.ascontiguousarray(np.concatenate(used, axis=1))

    gup = np.asarray(inputs["L0_ffn_gate_up_packed"], np.int32)
    gus = np.asarray(inputs["L0_ffn_gate_up_scales"], np.float16)
    dnp = np.asarray(inputs["L0_ffn_down_packed"], np.int32)
    dns = np.asarray(inputs["L0_ffn_down_scales"], np.float16)

    d = {
        "x": xc,
        "w_inT": np.ascontiguousarray(
            np.asarray(inputs["L0_attn_a_in_proj_w"], f32).T),
        "conv_w": np.asarray(inputs["L0_attn_a_conv_w"], f32),
        "dt_bias": np.asarray(inputs["L0_attn_a_dt_bias"], f32).reshape(-1, 1),
        "A_log": np.asarray(inputs["L0_attn_a_A_log"], f32).reshape(-1, 1),
        "D_pairs": dp,
        "norm_w": np.ascontiguousarray(
            np.asarray(inputs["L0_attn_a_norm_w"], f32).reshape(32, 128).T),
        "w_outT": np.ascontiguousarray(
            np.asarray(inputs["L0_attn_a_out_proj_w"], f32).T),
        "g_q_row": np.asarray(inputs["L0_attn_b_g_q"], f32).reshape(1, -1),
        "g_kv_row": np.asarray(inputs["L0_attn_b_g_kv"], f32).reshape(1, -1),
        "w_qa": np.asarray(inputs["L0_attn_b_w_qa"], f32),
        "w_qb": np.asarray(inputs["L0_attn_b_w_qb"], f32),
        "w_kva_used": wkva_used,
        "w_kvb": np.asarray(inputs["L0_attn_b_w_kvb"], f32),
        "w_o": np.asarray(inputs["L0_attn_b_w_o"], f32),
        "gup_packed": np.ascontiguousarray(gup[j * GU_SH:(j + 1) * GU_SH, :]),
        "gup_scalesT": np.ascontiguousarray(
            gus[:, j * GU_SH:(j + 1) * GU_SH].T),
        "down_packed": np.ascontiguousarray(dnp[j * DN_SH:(j + 1) * DN_SH, :]),
        "down_scalesT": np.ascontiguousarray(
            dns[:, j * DN_SH:(j + 1) * DN_SH].T),
        "jcol": np.full((128, 1), float(j), f32),
    }
    return d


_PROG_CACHE = {}


def _get_prog(dbg=""):
    if dbg not in _PROG_CACHE:
        p = Prog(dbg)
        p.build()
        _PROG_CACHE[dbg] = p
    return _PROG_CACHE[dbg]


def run_cores(inputs, dbg=""):
    p = _get_prog(dbg)
    in_maps = [_prep_core_inputs(inputs, j) for j in range(NCORE)]
    res = run_bass_kernel_spmd(p.nc, in_maps, core_ids=list(range(NCORE)))
    return res.results


def kernel(**inputs):
    res = run_cores(inputs)
    full = np.zeros((SEQ_TOTAL, DM), np.float32)
    for j in range(NCORE):
        o = np.asarray(res[j]["out"], np.float32)
        full[j * BLK:(j + 1) * BLK] = o[:, 0:BLK].T
        full[(15 - j) * BLK:(16 - j) * BLK] = o[:, BLK:TPC].T
    return full.reshape(1, SEQ_TOTAL, DM)


SEQ_TOTAL = 8192
